# version 04 (final)
# baseline (speedup 1.0000x reference)
import numpy as np
import ml_dtypes
from contextlib import ExitStack

import concourse.bass as bass
import concourse.mybir as mybir
from concourse.bass import IndirectOffsetOnAxis
from concourse.bass_utils import run_bass_kernel_spmd

F32 = mybir.dt.float32
BF16 = mybir.dt.bfloat16
I32 = mybir.dt.int32
AF = mybir.ActivationFunctionType
ALU = mybir.AluOpType
AX = mybir.AxisListType

CFG = dict(S=2048, NBC=16, PHASES=("a", "m1", "b", "m2"), NCORES=8)

D = 1024
NE, NEC, CAP, TOPK, INTER = 64, 32, 256, 8, 768
NHEAD, DH, LM = 8, 64, 256
CONVK, WQK = 33, 256
LGRP = 8
B_STOP = None
NEG = -1.0e30
DT = D // 128


class CX:

    def __init__(self, nc, stack):
        self.nc, self.stack = nc, stack
        self.sems, self.cnt, self.wc = {}, {}, {}

    def sem(self, name):
        if name not in self.sems:
            self.sems[name] = self.stack.enter_context(self.nc.semaphore(name))
            self.cnt[name] = 0
        return self.sems[name]

    def inc(self, name, inst, n=1):
        inst.then_inc(self.sem(name), n)
        self.cnt[name] += n
        return (name, self.cnt[name])

    def tok(self, name):
        self.sem(name)
        return (name, self.cnt[name])

    def wait(self, eng, ek, *toks):
        for tk in toks:
            if tk is None:
                continue
            nm, v = tk
            if v <= 0:
                continue
            if self.wc.get((ek, nm), 0) < v:
                eng.wait_ge(self.sem(nm), v)
                self.wc[(ek, nm)] = v


def emit_ln_tile(cx, block, sb, src, dst, pre_s_tok=None, scr=''):
    ln_mean = sb['ln_mean' + scr]
    ln_sqs = sb['ln_sqs' + scr]
    ln_sq = sb['ln_sq' + scr]
    ln_var = sb['ln_var' + scr]
    ln_rstd = sb['ln_rstd' + scr]
    ln_nb = sb['ln_nb' + scr]
    def _(v):
        cx.inc('v', v.reduce_sum(ln_mean[:], src[:], axis=AX.X))
        cx.inc('v', v.tensor_tensor(ln_sqs[:], src[:], src[:], ALU.mult))
        cx.wait(v, 'v', cx.tok('v'))
        cx.inc('v', v.tensor_scalar(ln_mean[:], ln_mean[:], 1.0 / D, None, ALU.mult))
        cx.inc('v', v.reduce_sum(ln_sq[:], ln_sqs[:], axis=AX.X))
        cx.wait(v, 'v', cx.tok('v'))
        cx.inc('v', v.tensor_tensor(ln_var[:], ln_mean[:], ln_mean[:], ALU.mult))
        cx.wait(v, 'v', cx.tok('v'))
        cx.inc('v', v.scalar_tensor_tensor(ln_var[:], ln_sq[:], 1.0 / D,
                                           ln_var[:], ALU.mult, ALU.subtract))
        cx.wait(v, 'v', cx.tok('v'))
        cx.inc('v', v.tensor_scalar(ln_var[:], ln_var[:], 1e-5, None, ALU.add))
    block.vector(_)
    tv2 = cx.tok('v')

    def _(s):
        cx.wait(s, 's', tv2)
        cx.inc('s', s.activation(out=ln_sq[:], in_=ln_var[:], func=AF.Sqrt))
    block.scalar(_)
    ts2 = cx.tok('s')

    def _(v):
        cx.wait(v, 'v', ts2)
        cx.inc('v', v.reciprocal(ln_rstd[:], ln_sq[:]))
        cx.wait(v, 'v', cx.tok('v'))
        cx.inc('v', v.tensor_tensor(ln_nb[:], ln_rstd[:], ln_mean[:],
                                    ALU.mult))
        cx.wait(v, 'v', cx.tok('v'))
        cx.inc('v', v.tensor_scalar(ln_nb[:], ln_nb[:], -1.0, None, ALU.mult))
    block.vector(_)
    tv3 = cx.tok('v')

    def _(s):
        cx.wait(s, 's', tv3, pre_s_tok)
        cx.inc('s', s.activation(out=dst[:], in_=src[:], func=AF.Identity,
                                 bias=ln_nb[:], scale=ln_rstd[:]))
    block.scalar(_)
    return cx.tok('s')


def emit_attn_a(cx, block, sb, ps, dram, S, NBC):
    nc = cx.nc
    ST = S // 128
    prev = dict(pe_n1_readers=None,
                v_psA_free=None,
                s_psA_free=None,
                s_pv_free=None,
                v_pv_free=None,
                tp_done=None,
                exp_done=None,
                n1_store=None,
                h1_store=None,
                h0_used=None)

    for b in range(NBC):
        base = b * S
        ln_done = None
        for tt in range(ST):
            pb = (b * ST + tt) % 2
            srcb = sb['lnin'] if pb == 0 else sb['lnin_b']
            dstb = sb['lnout'] if pb == 0 else sb['lnoutb']
            kfree = ('dn1', max(0, cx.cnt.get('dn1', 0) - 16))

            def _(y, tt=tt, srcb=srcb, kfree=kfree):
                cx.wait(y, 'y', kfree)
                cx.inc('dln', y.dma_start(
                    out=srcb[:],
                    in_=dram['h0'][base + tt * 128:base + (tt + 1) * 128, :]), 16)
            block.sync(_)
            tok_ld = cx.tok('dln')

            def _(v, kfree=kfree):
                cx.wait(v, 'v', tok_ld, kfree)
            block.vector(_)

            def _(s, kfree=kfree):
                cx.wait(s, 's', tok_ld, kfree)
            block.scalar(_)
            tok_ln = emit_ln_tile(cx, block, sb, srcb, dstb, kfree,
                                  scr=('' if pb == 0 else 'b'))

            def _(y, tt=tt, dstb=dstb):
                cx.wait(y, 'y', tok_ln)
                cx.inc('dn1', y.dma_start(out=dram['n1scr'][tt * 128:(tt + 1) * 128, :],
                                          in_=dstb[:]), 16)
            block.sync(_)
            prev['n1_store'] = cx.tok('dn1')

        def _(y):
            cx.wait(y, 'y', prev['n1_store'], prev['pe_n1_readers'])
            for dc in range(DT):
                cx.inc('dtp', y.dma_start(out=sb['n1fm'][:, dc, 0:S],
                                          in_=dram['n1scr'][0:S, dc * 128:(dc + 1) * 128],
                                          transpose=True), 16)
        block.sync(_)
        tok_n1 = cx.tok('dtp')

        nmv = S // 512 if S >= 512 else 1
        mvw = min(512, S)
        for do in range(DT):
            def _(t, do=do):
                cx.wait(t, 't', tok_n1, prev['s_psA_free'], prev['v_psA_free'])
                last = None
                for di in range(DT):
                    t.ldweights(sb['wqk'][:, di, do * 128:(do + 1) * 128])
                    for tm in range(nmv):
                        last = t.matmul(ps['A'][:, tm * mvw:(tm + 1) * mvw],
                                        sb['wqk'][:, di, do * 128:(do + 1) * 128],
                                        sb['n1fm'][:, di, tm * mvw:(tm + 1) * mvw],
                                        start=(di == 0), stop=(di == DT - 1))
                cx.inc('pe', last)
            block.tensor(_)
            tok_mm = cx.tok('pe')

            def _(s, do=do):
                cx.wait(s, 's', tok_mm)
                cx.inc('s', s.activation(out=sb['qfm'][:, do, 0:S], in_=ps['A'][:, 0:S],
                                         func=AF.Identity, scale=1.0 / 32.0))
            block.scalar(_)
            prev['s_psA_free'] = cx.tok('s')

        for tc in range(ST):
            def _(t, tc=tc):
                cx.wait(t, 't', tok_n1, prev['s_pv_free'], prev['v_pv_free'])
                last = None
                for di in range(DT):
                    t.ldweights(sb['n1fm'][:, di, tc * 128:(tc + 1) * 128])
                    last = t.matmul(ps['PV'][:, 0:512], sb['n1fm'][:, di, tc * 128:(tc + 1) * 128],
                                    sb['wvt'][:, di, 0:512], start=(di == 0), stop=(di == DT - 1))
                    last = t.matmul(ps['PV'][:, 512:1024], sb['n1fm'][:, di, tc * 128:(tc + 1) * 128],
                                    sb['wvt'][:, di, 512:1024], start=(di == 0), stop=(di == DT - 1))
                cx.inc('pe', last)
            block.tensor(_)
            tok_mm = cx.tok('pe')

            def _(s, tc=tc):
                cx.wait(s, 's', tok_mm)
                cx.inc('s', s.activation(out=sb['vtm'][:, tc, :], in_=ps['PV'][:, 0:1024],
                                         func=AF.Identity))
            block.scalar(_)
            prev['s_pv_free'] = cx.tok('s')
        tok_v = cx.tok('s')

        for qt in range(ST):
            kmax = (qt + 1) * 128
            kmv = []
            off = 0
            while off < kmax:
                w = min(512, kmax - off)
                kmv.append((off, w))
                off += w

            def _(t, qt=qt, kmv=kmv):
                cx.wait(t, 't', prev['s_psA_free'], prev['v_psA_free'])
                last = None
                for di in range(DT):
                    t.ldweights(sb['qfm'][:, di, qt * 128:(qt + 1) * 128])
                    for (off, w) in kmv:
                        last = t.matmul(ps['A'][:, off:off + w],
                                        sb['qfm'][:, di, qt * 128:(qt + 1) * 128],
                                        sb['n1fm'][:, di, off:off + w],
                                        start=(di == 0), stop=(di == DT - 1))
                cx.inc('pe', last)
            block.tensor(_)
            tok_sc = cx.tok('pe')

            def _(v, qt=qt, kmax=kmax):
                cx.wait(v, 'v', tok_sc, prev['exp_done'])
                off0 = 2048 - qt * 128
                cx.inc('v', v.scalar_tensor_tensor(sb['srow'][:, 0:kmax],
                                                   sb['dwide'][:, off0:off0 + kmax],
                                                   sb['slope128'][:, 0:1],
                                                   ps['A'][:, 0:kmax], ALU.mult, ALU.add))
                cx.wait(v, 'v', cx.tok('v'))
                cx.inc('v', v.tensor_tensor(sb['srow'][:, qt * 128:(qt + 1) * 128],
                                            sb['srow'][:, qt * 128:(qt + 1) * 128],
                                            sb['mask'][:, :], ALU.add))
                cx.wait(v, 'v', cx.tok('v'))
                cx.inc('v', v.reduce_max(sb['rmax'][:], sb['srow'][:, 0:kmax], axis=AX.X,
                                         negate=True))
            block.vector(_)
            prev['v_psA_free'] = cx.tok('v')
            tok_srow = cx.tok('v')

            def _(s, qt=qt, kmax=kmax):
                cx.wait(s, 's', tok_srow, prev['tp_done'])
                cx.inc('s', s.activation(out=sb['prow'][:, 0:kmax], in_=sb['srow'][:, 0:kmax],
                                         func=AF.Exp, bias=sb['rmax'][:, 0:1],
                                         accum_out=sb['rsum'][:, 0:1]))
            block.scalar(_)
            prev['exp_done'] = cx.tok('s')
            tok_exp = cx.tok('s')

            def _(v):
                cx.wait(v, 'v', tok_exp)
                cx.inc('v', v.reciprocal(sb['rrec'][:], sb['rsum'][:]))
            block.vector(_)
            tok_rec = cx.tok('v')

            tok_tp_evac = None
            for kc in range(qt + 1):
                def _(t, kc=kc):
                    cx.wait(t, 't', tok_exp, cx.tok('ptv'))
                    cx.inc('pe', t.transpose(ps['T'][:, :], sb['prow'][:, kc * 128:(kc + 1) * 128],
                                             sb['ident'][:, :]))
                block.tensor(_)
                tok_tp = cx.tok('pe')

                def _(v, kc=kc):
                    cx.wait(v, 'v', tok_tp)
                    cx.inc('ptv', v.tensor_copy(sb['pT'][:, kc, :], ps['T'][:, :]))
                block.vector(_)
            prev['tp_done'] = cx.tok('ptv')
            tok_ptv = cx.tok('ptv')

            def _(y, qt=qt):
                cx.wait(y, 'y', prev['h0_used'])
                cx.inc('dh0', y.dma_start(out=sb['h0t'][:],
                                          in_=dram['h0'][base + qt * 128:base + (qt + 1) * 128, :]), 16)
            block.sync(_)
            tok_h0 = cx.tok('dh0')

            def _(t, qt=qt):
                cx.wait(t, 't', tok_ptv, tok_v, prev['v_pv_free'], prev['s_pv_free'])
                last = None
                for kc in range(qt + 1):
                    t.ldweights(sb['pT'][:, kc, :])
                    last = t.matmul(ps['PV'][:, 0:512], sb['pT'][:, kc, :],
                                    sb['vtm'][:, kc, 0:512], start=(kc == 0), stop=(kc == qt))
                    last = t.matmul(ps['PV'][:, 512:1024], sb['pT'][:, kc, :],
                                    sb['vtm'][:, kc, 512:1024], start=(kc == 0), stop=(kc == qt))
                cx.inc('pe', last)
            block.tensor(_)
            tok_pv = cx.tok('pe')

            def _(v, qt=qt):
                cx.wait(v, 'v', tok_pv, tok_h0, tok_rec, prev['h1_store'])
                cx.inc('v', v.scalar_tensor_tensor(sb['h1t'][:, 0:512], ps['PV'][:, 0:512],
                                                   sb['rrec'][:, 0:1], sb['h0t'][:, 0:512],
                                                   ALU.mult, ALU.add))
                cx.wait(v, 'v', cx.tok('v'))
                cx.inc('v', v.scalar_tensor_tensor(sb['h1t'][:, 512:1024], ps['PV'][:, 512:1024],
                                                   sb['rrec'][:, 0:1], sb['h0t'][:, 512:1024],
                                                   ALU.mult, ALU.add))
            block.vector(_)
            prev['v_pv_free'] = cx.tok('v')
            prev['h0_used'] = cx.tok('v')
            tok_h1 = cx.tok('v')

            def _(y, qt=qt):
                cx.wait(y, 'y', tok_h1)
                cx.inc('dh1', y.dma_start(out=dram['h1'][base + qt * 128:base + (qt + 1) * 128, :],
                                          in_=sb['h1t'][:]), 16)
            block.sync(_)
            prev['h1_store'] = cx.tok('dh1')

        prev['pe_n1_readers'] = cx.tok('pe')
    return cx.tok('dh1')


def alloc_common(nc, E, cx, S):
    ST = S // 128
    sb, ps = {}, {}
    sb['n1fm'] = E(nc.sbuf_tensor("n1fm", [128, DT, S], BF16))
    sb['qfm'] = E(nc.sbuf_tensor("qfm", [128, DT, S], BF16))
    sb['vtm'] = E(nc.sbuf_tensor("vtm", [128, ST, D], BF16))
    sb['wqk'] = E(nc.sbuf_tensor("wqk", [128, DT, D], BF16))
    sb['wvt'] = E(nc.sbuf_tensor("wvt", [128, DT, D], BF16))
    sb['lnin'] = E(nc.sbuf_tensor("lnin", [128, D], F32))
    sb['lnout'] = E(nc.sbuf_tensor("lnout", [128, D], BF16))
    sb['srow'] = E(nc.sbuf_tensor("srow", [128, S], F32))
    sb['prow'] = E(nc.sbuf_tensor("prow", [128, S], BF16))
    sb['pT'] = E(nc.sbuf_tensor("pT", [128, ST, 128], BF16))
    sb['dwide'] = E(nc.sbuf_tensor("dwide", [128, 2176], F32))
    sb['mask'] = E(nc.sbuf_tensor("mask", [128, 128], F32))
    sb['kkrow'] = E(nc.sbuf_tensor("kkrow", [128, 128], F32))
    sb['qqcol'] = E(nc.sbuf_tensor("qqcol", [128, 1], F32))
    sb['ident'] = E(nc.sbuf_tensor("ident", [128, 128], BF16))
    sb['slope128'] = E(nc.sbuf_tensor("slope128", [128, 1], F32))
    sb['slope1'] = E(nc.sbuf_tensor("slope1", [1, 1], F32))
    sb['ones128'] = E(nc.sbuf_tensor("ones128", [1, 128], F32))
    sb['rmax'] = E(nc.sbuf_tensor("rmax", [128, 1], F32))
    sb['rsum'] = E(nc.sbuf_tensor("rsum", [128, 1], F32))
    sb['rrec'] = E(nc.sbuf_tensor("rrec", [128, 1], F32))
    sb['h0t'] = E(nc.sbuf_tensor("h0t", [128, D], F32))
    sb['h1t'] = E(nc.sbuf_tensor("h1t", [128, D], F32))
    for nm in ['ln_mean', 'ln_sq', 'ln_var', 'ln_rstd', 'ln_nb',
               'ln_meanb', 'ln_sqb', 'ln_varb', 'ln_rstdb', 'ln_nbb']:
        sb[nm] = E(nc.sbuf_tensor(nm, [128, 1], F32))
    sb['ln_sqs'] = E(nc.sbuf_tensor("ln_sqs", [128, D], F32))
    ps["A"] = E(nc.psum_tensor("psA", [128, 2048], F32))
    ps['PV'] = E(nc.psum_tensor("psPV", [128, 1024], F32))
    ps['T'] = E(nc.psum_tensor("psT", [128, 128], BF16))
    ps['B'] = E(nc.psum_tensor("psB", [128, 512], F32))
    return sb, ps


def emit_preamble(cx, block, sb, ps, ins):
    def _(g):
        g.iota(sb['dwide'][:], [[1, 2176]], base=-2048, channel_multiplier=-1,
               allow_small_or_imprecise_dtypes=True)
        g.iota(sb['kkrow'][:], [[1, 128]], base=0, channel_multiplier=0,
               allow_small_or_imprecise_dtypes=True)
        g.iota(sb['qqcol'][:], [[1, 1]], base=0, channel_multiplier=1,
               allow_small_or_imprecise_dtypes=True)
        g.memset(sb['ident'][:], 0.0)
        g.affine_select(sb['ident'][:], sb['ident'][:], [[-1, 128]], ALU.not_equal, 1.0,
                        base=0, channel_multiplier=1)
        g.memset(sb['ones128'][:], 1.0)
        cx.inc('g', g.memset(sb['prow'][:], 0.0))
        cx.inc('dw', g.dma_start(out=sb['wqk'][:],
                                 in_=ins['w_qk'].rearrange("(c p) o -> p c o", p=128)), 16)
        cx.inc('dw', g.dma_start(out=sb['wvt'][:],
                                 in_=ins['w_v_t'].rearrange("(c p) o -> p c o", p=128)), 16)
    block.gpsimd(_)

    def _(y):
        cx.inc('dw2', y.dma_start(out=sb['slope1'][:], in_=ins['alibi'][:, :]), 16)
    block.sync(_)
    tok_w = cx.tok('dw')
    tok_w2 = cx.tok('dw2')
    tok_g0 = cx.tok('g')

    def _(t):
        cx.wait(t, 't', tok_w, tok_w2, tok_g0)
        cx.inc('pe', t.matmul(ps['B'][:, 0:1], sb['ones128'][:, :], sb['slope1'][:, :],
                              start=True, stop=True))
    block.tensor(_)
    tok_sl = cx.tok('pe')

    def _(v):
        cx.wait(v, 'v', tok_sl)
        cx.inc('v', v.tensor_copy(sb['slope128'][:], ps['B'][:, 0:1]))
        cx.wait(v, 'v', cx.tok('g'))
        cx.inc('v', v.tensor_scalar(sb['mask'][:], sb['kkrow'][:], sb['qqcol'][:, 0:1], NEG,
                                    ALU.is_gt, ALU.mult))
        cx.inc('ptv', v.nop())
    block.vector(_)
    return cx.tok('v')






BIG = 1.0e6
OOBJ = 99999.0


def balanced_if(cx, engname, body):
    def _(eng):
        snap = dict(cx.cnt)
        wsnap = dict(cx.wc)
        r = cx.sbn[engname + '_lreg']
        with eng.If_lt(r, 256):
            body(eng)
        cx.wc = wsnap
        with eng.Else():
            for nm in list(cx.cnt):
                d = cx.cnt[nm] - snap.get(nm, 0)
                if d > 0:
                    if nm in cx.swdge_sems:
                        for _i in range(d // 16):
                            eng.dma_start(out=cx.dummy_dram[0:1, 0:1],
                                          in_=cx.dummy_sb[0:1, 0:1]).then_inc(cx.sem(nm), 16)
                    else:
                        eng.sem_inc(cx.sem(nm), d)
    return _


def emit_flag_write(cx, block, fid, shr, sbn, only_role=None):
    def _(g):
        if only_role is not None:
            with g.If_eq(sbn['g_rreg'], only_role):
                cx.inc('dfl', g.dma_start(out=shr['flags'][fid, only_role:only_role + 1, :],
                                          in_=sbn['noncebuf'][:]), 16)
            with g.Else():
                g.dma_start(out=cx.dummy_dram[0:1, 0:1],
                            in_=cx.dummy_sb[0:1, 0:1]).then_inc(cx.sem('dfl'), 16)
        else:
            with g.If_eq(sbn['g_rreg'], 0):
                cx.inc('dfl', g.dma_start(out=shr['flags'][fid, 0:1, :],
                                          in_=sbn['noncebuf'][:]), 16)
            with g.Else():
                g.dma_start(out=shr['flags'][fid, 1:2, :],
                            in_=sbn['noncebuf'][:]).then_inc(cx.sem('dfl'), 16)
        g.wait_ge(cx.sem('dfl'), cx.cnt['dfl'])
    block.gpsimd(_)


def emit_flag_poll(cx, block, fid, shr, sbn, only_role=None, from_role=None):
    def _(g):
        def poll_body():
            g.reg_mov(sbn['pr'], 1)
            with g.While(sbn['pr']):
                if from_role is not None:
                    g.dma_start(out=sbn['flagrd'][:], in_=shr['flags'][fid, from_role:from_role + 1, :]
                                ).then_inc(cx.sem('dpoll'), 16)
                else:
                    with g.If_eq(sbn['g_rreg'], 0):
                        g.dma_start(out=sbn['flagrd'][:], in_=shr['flags'][fid, 1:2, :]
                                    ).then_inc(cx.sem('dpoll'), 16)
                    with g.Else():
                        g.dma_start(out=sbn['flagrd'][:], in_=shr['flags'][fid, 0:1, :]
                                    ).then_inc(cx.sem('dpoll'), 16)
                g.reg_add(sbn['pc'], sbn['pc'], 16)
                g.wait_ge(cx.sem('dpoll'), sbn['pc'])
                g.reg_load(sbn['fv'], sbn['flagrd'][0:1, 0:1])
                g.reg_alu(sbn['pr'], sbn['fv'], sbn['nreg'], ALU.not_equal)
        if only_role is not None:
            with g.If_eq(sbn['g_rreg'], only_role):
                poll_body()
        else:
            poll_body()
        cx.inc('g', g.nop())
    block.gpsimd(_)
    return cx.tok('g')


def emit_moe(cx, block, sb, ps, shr, sbn, dram, S, NBC, cid):
    nc = cx.nc
    ST = S // 128
    BT = ST
    TOKC = NBC * S
    TOKI = 2 * TOKC
    NSLOT = NE * CAP
    hi_st = sb['vtm'][:, :, :].rearrange("p a b -> p (a b)")
    lo_st = sb['n1fm'][:, :, :].rearrange("p a b -> p (a b)")
    F_N2, F_TOT, F_TZ, F_T, F_YW = [cid * 5 + i for i in range(5)]

    def _(v):
        cx.wait(v, 'v', cx.tok('moe_prev_done'))
        cx.inc('v', v.memset(sb['n2f32'][:, :], 0.0))
    block.vector(_)
    tok_tz0 = cx.tok('v')

    def _(g):
        cx.wait(g, 'g', tok_tz0)
        half = NSLOT // 2
        zsrc = sb['n2f32'][:, 0:(half // 64)].rearrange("p (a w) -> p a w", w=2)
        with g.If_eq(sbn['g_rreg'], 0):
            cx.inc('dsh', g.dma_start(
                out=shr['Tsh'][0:half, :].rearrange("(a p) w -> p a w", p=128),
                in_=zsrc), 16)
        with g.Else():
            g.dma_start(
                out=shr['Tsh'][half:NSLOT, :].rearrange("(a p) w -> p a w", p=128),
                in_=zsrc).then_inc(cx.sem('dsh'), 16)
        g.wait_ge(cx.sem('dsh'), cx.cnt['dsh'])
    block.gpsimd(_)
    emit_flag_write(cx, block, F_TZ, shr, sbn)

    st_used = None
    for b in range(NBC):
        base = b * S
        for tt in range(ST):
            pb = (b * ST + tt) % 2
            srcb = sb['lnin'] if pb == 0 else sb['lnin_b']
            dstb = sb['n2f32'] if pb == 0 else sb['n2f32b']
            kfree = ('m_lnfree', max(0, cx.cnt.get('m_lnfree', 0) - 1))
            dfree = ('m_lofree', max(0, cx.cnt.get('m_lofree', 0) - 1))

            def _(y, tt=tt, b=b, srcb=srcb, kfree=kfree):
                cx.wait(y, 'y', kfree)
                cx.inc('dln', y.dma_start(out=srcb[:],
                                          in_=dram['hin'][base + tt * 128:base + (tt + 1) * 128, :]), 16)
            block.sync(_)
            tok_ld = cx.tok('dln')
            tok_sw = cx.tok('dn1')

            def _(v, kfree=kfree, dfree=dfree):
                cx.wait(v, 'v', tok_ld, st_used, tok_sw, kfree, dfree)
            block.vector(_)

            def _(s2, kfree=kfree, dfree=dfree):
                cx.wait(s2, 's', tok_ld, st_used, tok_sw, kfree, dfree)
            block.scalar(_)
            tok_ln = emit_ln_tile(cx, block, sb, srcb, dstb, dfree,
                                  scr=('' if pb == 0 else 'b'))

            def _(s2, tt=tt, dstb=dstb):
                cx.wait(s2, 's', tok_ln)
                cx.inc('s', s2.activation(out=hi_st[:, tt * D:(tt + 1) * D], in_=dstb[:],
                                          func=AF.Identity))
                cx.inc('m_lnfree', s2.nop())
            block.scalar(_)
            tok_hi = cx.tok('s')

            def _(v, tt=tt, dstb=dstb):
                cx.wait(v, 'v', tok_hi)
                cx.inc('m_lofree', v.tensor_tensor(lo_st[:, tt * D:(tt + 1) * D], dstb[:],
                                                   hi_st[:, tt * D:(tt + 1) * D], ALU.subtract))
            block.vector(_)
        tok_stage = cx.tok('m_lofree')
        tok_stage_s = cx.tok('s')

        def _(y, b=b):
            cx.wait(y, 'y', tok_stage, tok_stage_s)
            cx.inc('dn1', y.dma_start(
                out=dram['n2l'].rearrange("(a p) d -> p a d", p=128)[:, b * BT:(b + 1) * BT, :],
                in_=sb['vtm'][:, :, :]), 16)
            cx.inc('dn1', y.dma_start(
                out=dram['n2lo'].rearrange("(a p) d -> p a d", p=128)[:, b * BT:(b + 1) * BT, :],
                in_=sb['n1fm'][:, :, :].rearrange("p a b -> p (a b)").rearrange(
                    "p (a d) -> p a d", d=D)), 16)
        block.sync(_)

        def _(g, b=b):
            cx.wait(g, 'g', tok_stage, tok_stage_s)
            with g.If_eq(sbn['g_rreg'], 0):
                cx.inc('dsh', g.dma_start(
                    out=shr['n2sh'].rearrange("(a p) d -> p a d", p=128)[:, b * BT:(b + 1) * BT, :],
                    in_=sb['vtm'][:, :, :]), 16)
            with g.Else():
                g.dma_start(
                    out=shr['n2sh'].rearrange("(a p) d -> p a d", p=128)[:, (NBC + b) * BT:(NBC + b + 1) * BT, :],
                    in_=sb['vtm'][:, :, :]).then_inc(cx.sem('dsh'), 16)
            cx.inc('m_stfree', g.nop())
            g.wait_ge(cx.sem('dsh'), cx.cnt['dsh'] - 16)
        block.gpsimd(_)
        st_used = cx.tok('m_stfree')

    def _(g):
        cx.wait(g, 'g', cx.tok('dn1'))
        g.wait_ge(cx.sem('dsh'), cx.cnt['dsh'])
    block.gpsimd(_)
    emit_flag_write(cx, block, F_N2, shr, sbn)

    def seg_bc2(ap16):
        return ap16.unsqueeze(2).broadcast_to([128, ST, 64])

    prev_gmm = None
    for b in range(NBC):
        base = b * S

        def _(y, b=b):
            cx.wait(y, 'y', cx.tok('dn1'), cx.tok('m_gfree'), prev_gmm)
            for dc in range(DT):
                cx.inc('dtp', y.dma_start(out=sb['qfm'][:, dc, 0:S],
                                          in_=dram['n2l'][base:base + S, dc * 128:(dc + 1) * 128],
                                          transpose=True), 16)
                cx.inc('dtp', y.dma_start(out=sb['n1fm'][:, dc, 0:S],
                                          in_=dram['n2lo'][base:base + S, dc * 128:(dc + 1) * 128],
                                          transpose=True), 16)
        block.sync(_)
        tok_fm = cx.tok('dtp')

        for tt in range(ST):
            def _(t, tt=tt):
                cx.wait(t, 't', tok_fm, cx.tok('m_psB_free'))
                last = None
                for di in range(DT):
                    t.ldweights(sb['qfm'][:, di, tt * 128:(tt + 1) * 128])
                    last = t.matmul(ps['B'][:, 0:128], sb['qfm'][:, di, tt * 128:(tt + 1) * 128],
                                    sb['gwcat'][:, di, :], start=(di == 0), stop=False)
                for di in range(DT):
                    t.ldweights(sb['n1fm'][:, di, tt * 128:(tt + 1) * 128])
                    last = t.matmul(ps['B'][:, 0:128], sb['n1fm'][:, di, tt * 128:(tt + 1) * 128],
                                    sb['gwcat2'][:, di, :], start=False, stop=(di == DT - 1))
                cx.inc('pe', last)
            block.tensor(_)
            tok_gmm = cx.tok('pe')

            def _(s2, tt=tt):
                cx.wait(s2, 's', tok_gmm)
                cx.inc('s', s2.activation(out=sb['gsc'][:, :], in_=ps['B'][:, 64:128],
                                          func=AF.Identity))
            block.scalar(_)
            tok_evg = cx.tok('s')

            def _(v, tt=tt):
                cx.wait(v, 'v', tok_evg)
                cx.inc('m_psB_free', v.tensor_tensor(sb['lgb'][:, tt, :], ps['B'][:, 0:64],
                                                     sb['gsc'][:, :], ALU.add))
            block.vector(_)
        prev_gmm = cx.tok('m_psB_free')

        def _(v, b=b):
            cx.wait(v, 'v', prev_gmm)
            cx.inc('v', v.tensor_reduce(sb['m16'][:, :], sb['lgb'][:, :, :], axis=AX.X, op=ALU.max))
            cx.wait(v, 'v', cx.tok('v'))
            cx.inc('v', v.tensor_tensor(sb['prb'][:, :, :], sb['lgb'][:, :, :],
                                        seg_bc2(sb['m16'][:, :]), ALU.subtract))
        block.vector(_)
        tok_sm1 = cx.tok('v')

        def _(s2):
            cx.wait(s2, 's', tok_sm1)
            cx.inc('s', s2.activation(out=sb['prb'][:, :, :], in_=sb['prb'][:, :, :], func=AF.Exp))
        block.scalar(_)
        tok_expb = cx.tok('s')

        def _(v, b=b):
            cx.wait(v, 'v', tok_expb)
            cx.inc('v', v.tensor_reduce(sb['m16'][:, :], sb['prb'][:, :, :], axis=AX.X, op=ALU.add))
            cx.wait(v, 'v', cx.tok('v'))
            cx.inc('v', v.reciprocal(sb['r16'][:, :], sb['m16'][:, :]))
            cx.wait(v, 'v', cx.tok('v'))
            cx.inc('v', v.tensor_tensor(sb['prb'][:, :, :], sb['prb'][:, :, :],
                                        seg_bc2(sb['r16'][:, :]), ALU.mult))
            cx.wait(v, 'v', cx.tok('v'))
            cx.inc('v', v.tensor_copy(sb['prc'][:, :, :], sb['prb'][:, :, :]))
            for k in range(TOPK):
                cx.wait(v, 'v', cx.tok('v'))
                cx.inc('v', v.tensor_reduce(sb['m16'][:, :], sb['prc'][:, :, :], axis=AX.X,
                                            op=ALU.max))
                cx.wait(v, 'v', cx.tok('v'))
                cx.inc('v', v.tensor_tensor(sb['msk'][:, :, :], sb['prc'][:, :, :],
                                            seg_bc2(sb['m16'][:, :]), ALU.not_equal))
                cx.wait(v, 'v', cx.tok('v'))
                cx.inc('v', v.tensor_tensor(sb['prc'][:, :, :], sb['prc'][:, :, :],
                                            sb['msk'][:, :, :], ALU.mult))
            cx.wait(v, 'v', cx.tok('v'))
            cx.inc('v', v.tensor_tensor(sb['Hb'][:, :, :], sb['prb'][:, :, :], sb['prc'][:, :, :],
                                        ALU.not_equal))
            cx.wait(v, 'v', cx.tok('v'))
            cx.inc('v', v.tensor_tensor(sb['whb'][:, :, :], sb['prb'][:, :, :], sb['Hb'][:, :, :],
                                        ALU.mult))
            cx.wait(v, 'v', cx.tok('v'))
            cx.inc('v', v.tensor_reduce(sb['m16'][:, :], sb['whb'][:, :, :], axis=AX.X, op=ALU.add))
            cx.wait(v, 'v', cx.tok('v'))
            cx.inc('v', v.reciprocal(sb['r16'][:, :], sb['m16'][:, :]))
            cx.wait(v, 'v', cx.tok('v'))
            cx.inc('v', v.tensor_tensor(sb['whb'][:, :, :], sb['whb'][:, :, :],
                                        seg_bc2(sb['r16'][:, :]), ALU.mult))
            cx.wait(v, 'v', cx.tok('v'))
            cx.inc('v', v.tensor_copy(sb['Hbb'][:, :, :], sb['Hb'][:, :, :]))
        block.vector(_)
        tok_top = cx.tok('v')

        def _(y, b=b):
            cx.wait(y, 'y', tok_top)
            cx.inc('dHW', y.dma_start(
                out=dram['H'].rearrange("(a p) e -> p a e", p=128)[:, b * BT:(b + 1) * BT, :],
                in_=sb['Hbb'][:, :, :]), 16)
            cx.inc('dHW', y.dma_start(
                out=dram['Wh'].rearrange("(a p) e -> p a e", p=128)[:, b * BT:(b + 1) * BT, :],
                in_=sb['whb'][:, :, :]), 16)
            cx.inc('m_gfree', y.nop())
        block.sync(_)

    def _(v):
        cx.wait(v, 'v', cx.tok('dHW'))
        cx.inc('v', v.memset(sb['Osb'][:, :], 0.0))
        cx.inc('v', v.memset(sb['Osb2'][:, :], 0.0))
        cx.wait(v, 'v', cx.tok('v'))
        cx.inc('v', v.memset(sb['carry'][:, :], 0.0))
    block.vector(_)
    tok_oz = cx.tok('v')

    emit_flag_poll(cx, block, F_TOT, shr, sbn, only_role=1, from_role=0)

    def _(g):
        cx.wait(g, 'g', tok_oz)
        with g.If_ne(sbn['g_rreg'], 0):
            cx.inc('dto', g.dma_start(out=sb['Osb2'][0:1, :], in_=shr['tot'][cid, 0:1, :]), 16)
        with g.Else():
            g.dma_start(out=cx.dummy_dram[0:1, 0:1],
                        in_=cx.dummy_sb[0:1, 0:1]).then_inc(cx.sem('dto'), 16)
    block.gpsimd(_)
    tok_oload = cx.tok('dto')

    def _(v):
        cx.wait(v, 'v', tok_oload)
        cx.inc('v', v.scalar_tensor_tensor(sb['Osb'][0:1, :], sb['Osb2'][0:1, :],
                                           sb['rolef1'][0:1, 0:1],
                                           sb['Osb'][0:1, :], ALU.mult, ALU.add))
    block.vector(_)
    tok_oinit = cx.tok('v')

    for b in range(NBC):
        def _(y, b=b):
            cx.wait(y, 'y', cx.tok('m_cfree'), cx.tok('m_gfree'))
            cx.inc('dHl', y.dma_start(
                out=sb['Hbb'][:, :, :],
                in_=dram['H'].rearrange("(a p) e -> p a e", p=128)[:, b * BT:(b + 1) * BT, :]), 16)
        block.sync(_)
        tok_hl = cx.tok('dHl')

        def _(v, b=b):
            cx.wait(v, 'v', tok_oinit, cx.tok('m_carry'))
            cx.inc('v', v.tensor_reduce(sb['fmin'][0:1, 0:1], sb['Osb'][0:1, :], axis=AX.X,
                                        op=ALU.min))
            cx.wait(v, 'v', cx.tok('v'))
            cx.inc('v', v.tensor_copy(sb['flagbuf_i'][0:1, b + 16 * cid:b + 16 * cid + 1],
                                      sb['fmin'][0:1, 0:1]))
            cx.wait(v, 'v', cx.tok('v'))
            cx.inc('v', v.memset(sb['carry'][0:1, :], 0.0))
        block.vector(_)
        tok_fl = cx.tok('v')

        for tt in range(ST):
            def _(t, tt=tt, b=b):
                cx.wait(t, 't', tok_fl, tok_hl, cx.tok('m_cevac'), cx.tok('m_carry'))
                t.matmul(ps['B'][:, 0:64], sb['tri'][:, :], sb['Hbb'][:, tt, :],
                         start=True, stop=False)
                if tt > 0:
                    t.matmul(ps['B'][:, 0:64], sb['ones128'][0:1, :], sb['carry'][0:1, :],
                             start=False, stop=False)
                last = t.matmul(ps['B'][:, 0:64], sb['ones128'][0:1, :], sb['Osb'][0:1, :],
                                start=False, stop=True)
                cx.inc('pe', last)
                last = t.matmul(ps['PV'][0:1, 0:64], sb['onescol'][:, 0:1], sb['Hbb'][:, tt, :],
                                start=True, stop=True)
                cx.inc('pe', last)
            block.tensor(_)
            tok_cs = cx.tok('pe')

            def _(v, tt=tt, b=b):
                cx.wait(v, 'v', tok_cs)
                cx.inc('m_cevac', v.tensor_copy(sb['Cb'][:, tt, :], ps['B'][:, 0:64]))
                cx.wait(v, 'v', cx.tok('m_cevac'))
                cx.inc('m_carry', v.tensor_tensor(sb['carry'][0:1, :], sb['carry'][0:1, :],
                                                  ps['PV'][0:1, 0:64], ALU.add))
            block.vector(_)

        def _(v, b=b):
            cx.wait(v, 'v', cx.tok('m_carry'))
            cx.inc('v', v.tensor_tensor(sb['Osb'][0:1, :], sb['Osb'][0:1, :], sb['carry'][0:1, :],
                                        ALU.add))
        block.vector(_)

        def _(y, b=b):
            cx.wait(y, 'y', cx.tok('m_cevac'))
            cx.inc('dC', y.dma_start(
                out=dram['C'].rearrange("(a p) e -> p a e", p=128)[:, b * BT:(b + 1) * BT, :],
                in_=sb['Cb'][:, :, :]), 16)
            cx.inc('m_cfree', y.nop())
        block.sync(_)

    def _(v):
        cx.wait(v, 'v', cx.tok('v'))
        cx.inc('v', v.tensor_copy(sb['Osb2'][0:1, :], sb['Osb'][0:1, :]))
    block.vector(_)
    tok_totc = cx.tok('v')

    def _(g):
        cx.wait(g, 'g', tok_totc)
        with g.If_eq(sbn['g_rreg'], 0):
            cx.inc('dsh', g.dma_start(out=shr['tot'][cid, 0:1, :], in_=sb['Osb2'][0:1, :]), 16)
        with g.Else():
            g.dma_start(out=cx.dummy_dram[0:1, 0:1],
                        in_=cx.dummy_sb[0:1, 0:1]).then_inc(cx.sem('dsh'), 16)
        g.wait_ge(cx.sem('dsh'), cx.cnt['dsh'])
    block.gpsimd(_)
    emit_flag_write(cx, block, F_TOT, shr, sbn, only_role=0)
    emit_flag_poll(cx, block, F_TZ, shr, sbn)

    tok_flags = cx.tok('v')
    for b in range(NBC):
        fslot = b + 16 * cid
        for en, ek in (('vector', 'v'), ('gpsimd', 'g'), ('sync', 'y')):
            def _(eng, ek=ek, fslot=fslot):
                cx.wait(eng, ek, tok_flags)
                eng.reg_load(cx.sbn[ek + '_lreg'], sb['flagbuf_i'][0:1, fslot:fslot + 1])
            getattr(block, en)(_)

        def ybody(y, b=b):
            cx.wait(y, 'y', cx.tok('m_xfree'), cx.tok('m_cfree'))
            cx.inc('dCl', y.dma_start(
                out=sb['Cb'][:, :, :],
                in_=dram['C'].rearrange("(a p) e -> p a e", p=128)[:, b * BT:(b + 1) * BT, :]), 16)
            cx.inc('dCl', y.dma_start(
                out=sb['whb'][:, :, :],
                in_=dram['Wh'].rearrange("(a p) e -> p a e", p=128)[:, b * BT:(b + 1) * BT, :]), 16)
        block.sync(balanced_if(cx, 'y', ybody))
        tok_cl = cx.tok('dCl')

        def gbody(g, b=b):
            cx.wait(g, 'g', cx.tok('m_ilfree'))
            cx.inc('dHx', g.dma_start(
                out=sb['Hb'][:, :, :],
                in_=dram['H'].rearrange("(a p) e -> p a e", p=128)[:, b * BT:(b + 1) * BT, :]), 16)
            cx.inc('g', g.iota(sb['ilocb'][:, :], [[128, ST]], base=b * S + 1,
                               channel_multiplier=1, allow_small_or_imprecise_dtypes=True))
        block.gpsimd(balanced_if(cx, 'g', gbody))
        tok_il = cx.tok('g')
        tok_hx = cx.tok('dHx')

        def vbody(v, b=b):
            cx.wait(v, 'v', tok_cl, tok_il, tok_hx, cx.tok('m_scat'))
            cx.inc('v', v.tensor_copy(sb['prc'][:, :, :], sb['Hb'][:, :, :]))
            cx.wait(v, 'v', cx.tok('v'))
            cx.inc('v', v.tensor_scalar(sb['ilocb'][:, :], sb['ilocb'][:, :],
                                        sb['rolecol'][:, 0:1], None, ALU.add))
            for k in range(TOPK):
                cx.wait(v, 'v', cx.tok('v'))
                cx.inc('v', v.tensor_tensor(sb['msk'][:, :, :], sb['prc'][:, :, :],
                                            sb['iotaMB'][:, :].unsqueeze(1).broadcast_to([128, ST, 64]),
                                            ALU.mult))
                cx.wait(v, 'v', cx.tok('v'))
                cx.inc('v', v.tensor_reduce(sb['m16'][:, :], sb['msk'][:, :, :], axis=AX.X,
                                            op=ALU.min))
                cx.wait(v, 'v', cx.tok('v'))
                cx.inc('v', v.tensor_tensor(sb['eqm'][:, :, :], sb['msk'][:, :, :],
                                            seg_bc2(sb['m16'][:, :]), ALU.is_equal))
                cx.wait(v, 'v', cx.tok('v'))
                cx.inc('v', v.tensor_tensor(sb['prc'][:, :, :], sb['prc'][:, :, :],
                                            sb['eqm'][:, :, :], ALU.subtract))
                cx.wait(v, 'v', cx.tok('v'))
                cx.inc('v', v.tensor_scalar(sb['ekb'][:, :, k], sb['m16'][:, :], BIG, None,
                                            ALU.add))
                cx.wait(v, 'v', cx.tok('v'))
                cx.inc('v', v.tensor_tensor(sb['msk'][:, :, :], sb['eqm'][:, :, :],
                                            sb['Cb'][:, :, :], ALU.mult))
                cx.wait(v, 'v', cx.tok('v'))
                cx.inc('v', v.tensor_reduce(sb['posb'][:, :, k], sb['msk'][:, :, :], axis=AX.X,
                                            op=ALU.add))
                cx.wait(v, 'v', cx.tok('v'))
                cx.inc('v', v.tensor_tensor(sb['msk'][:, :, :], sb['eqm'][:, :, :],
                                            sb['whb'][:, :, :], ALU.mult))
                cx.wait(v, 'v', cx.tok('v'))
                cx.inc('v', v.tensor_reduce(sb['pay'][:, :, k, 1], sb['msk'][:, :, :], axis=AX.X,
                                            op=ALU.add))
            cx.wait(v, 'v', cx.tok('v'))
            cx.inc('v', v.scalar_tensor_tensor(sb['jfb'][:, :, :], sb['ekb'][:, :, :], float(CAP),
                                               sb['posb'][:, :, :], ALU.mult, ALU.add))
            cx.wait(v, 'v', cx.tok('v'))
            cx.inc('v', v.tensor_scalar(sb['eqm'][:, :, 0:TOPK], sb['posb'][:, :, :], float(CAP),
                                        None, ALU.is_ge))
            cx.wait(v, 'v', cx.tok('v'))
            cx.inc('v', v.scalar_tensor_tensor(sb['jfb'][:, :, :], sb['eqm'][:, :, 0:TOPK],
                                               OOBJ, sb['jfb'][:, :, :], ALU.mult, ALU.add))
            cx.wait(v, 'v', cx.tok('v'))
            cx.inc('v', v.tensor_copy(sb['jib'][:, :, :], sb['jfb'][:, :, :]))
            for k in range(TOPK):
                cx.wait(v, 'v', cx.tok('v'))
                cx.inc('v', v.tensor_copy(sb['pay'][:, :, k, 0], sb['ilocb'][:, :]))
        block.vector(balanced_if(cx, 'v', vbody))
        tok_x = cx.tok('v')

        def ybody2(y, b=b):
            cx.wait(y, 'y', tok_x)
            cx.inc('dJ', y.dma_start(
                out=dram['jd'].rearrange("(a p) k -> p a k", p=128)[:, b * BT:(b + 1) * BT, :],
                in_=sb['jib'][:, :, :]), 16)
            cx.inc('m_xfree', y.nop())
        block.sync(balanced_if(cx, 'y', ybody2))

        def gbody2(g, b=b):
            cx.wait(g, 'g', tok_x)
            for tt in range(ST):
                for k in range(TOPK):
                    cx.inc('dTs', g.indirect_dma_start(
                        shr['Tsh'][:, :],
                        IndirectOffsetOnAxis(sb['jib'][:, tt, k].unsqueeze(1), 0),
                        sb['pay'][:, tt, k, :], None,
                        bounds_check=sbn['bnd_slot'], oob_is_err=False), 16)
            cx.inc('m_ilfree', g.nop())
            cx.inc('m_scat', g.nop())
        block.gpsimd(balanced_if(cx, 'g', gbody2))

        def _(g):
            g.wait_ge(cx.sem('dTs'), cx.cnt['dTs'])
            g.wait_ge(cx.sem('dHx'), cx.cnt['dHx'])
        block.gpsimd(_)

    def _(g):
        g.wait_ge(cx.sem('dTs'), cx.cnt['dTs'])
    block.gpsimd(_)
    emit_flag_write(cx, block, F_T, shr, sbn)
    emit_flag_poll(cx, block, F_T, shr, sbn)
    tok_tbar = cx.tok('g')

    def _(v):
        cx.wait(v, 'v', tok_tbar)
        cx.inc('v', v.memset(sb['dtm'][:, :, :], 0.0))
    block.vector(_)
    tok_dz = cx.tok('v')

    def _(y):
        cx.wait(y, 'y', cx.tok('dC'))
        cx.inc('dwg', y.dma_start(out=sb['wgux'][:, :, 0:INTER],
                                  in_=dram['wg'].rearrange("e (c p) i -> e p c i", p=128)[0]), 16)
        cx.inc('dwg', y.dma_start(out=sb['wgux'][:, :, INTER:2 * INTER],
                                  in_=dram['wu'].rearrange("e (c p) i -> e p c i", p=128)[0]), 16)
        cx.inc('dwd', y.dma_start(out=sb['wdx'][:, :, :],
                                  in_=dram['wd'].rearrange("e (c p) h -> e p c h", p=128)[0]), 16)
    block.sync(_)

    for e in range(NEC):
        def _(g, e=e):
            cx.wait(g, 'g', tok_tbar, cx.tok('m_tsl'))
            with g.If_eq(sbn['g_rreg'], 0):
                cx.inc('dtl', g.dma_start(
                    out=sb['tsl'][:, :, :],
                    in_=shr['Tsh'].rearrange("(s p) w -> p s w", p=128)[:, e * 2:(e + 1) * 2, :]), 16)
            with g.Else():
                g.dma_start(
                    out=sb['tsl'][:, :, :],
                    in_=shr['Tsh'].rearrange("(s p) w -> p s w", p=128)[:, (NEC + e) * 2:(NEC + e + 1) * 2, :]
                ).then_inc(cx.sem('dtl'), 16)
        block.gpsimd(_)
        tok_tl = cx.tok('dtl')

        def _(v, e=e):
            cx.wait(v, 'v', tok_tl, tok_dz, cx.tok('m_dtm_free'))
            cx.inc('v', v.tensor_scalar(sb['eqz'][:, :], sb['tsl'][:, :, 0], 0.0, None,
                                        ALU.is_equal))
            cx.wait(v, 'v', cx.tok('v'))
            cx.inc('v', v.tensor_scalar(sb['idxf'][:, :], sb['tsl'][:, :, 0], 1.0, None,
                                        ALU.subtract))
            cx.wait(v, 'v', cx.tok('v'))
            cx.inc('v', v.scalar_tensor_tensor(sb['idxf'][:, :], sb['eqz'][:, :], BIG,
                                               sb['idxf'][:, :], ALU.mult, ALU.add))
            cx.wait(v, 'v', cx.tok('v'))
            cx.inc('v', v.tensor_copy(sb['idxi'][:, :], sb['idxf'][:, :]))
        block.vector(_)
        tok_idx = cx.tok('v')

        def _(g, e=e):
            cx.wait(g, 'g', tok_idx)
            for c in range(2):
                cx.inc('dgx', g.indirect_dma_start(
                    sb['dtm'][:, c, :], None,
                    shr['n2sh'][:, :],
                    IndirectOffsetOnAxis(sb['idxi'][:, c].unsqueeze(1), 0),
                    bounds_check=sbn['bnd_tok'], oob_is_err=False), 16)
            cx.inc('m_tsl', g.nop())
        block.gpsimd(_)
        tok_gx = cx.tok('dgx')

        for c in range(2):
            for dc in range(DT):
                def _(t, c=c, dc=dc):
                    cx.wait(t, 't', tok_gx, cx.tok('m_dfm_ev'))
                    cx.inc('pe', t.transpose(ps['T'][:, :],
                                             sb['dtm'][:, c, dc * 128:(dc + 1) * 128],
                                             sb['ident'][:, :]))
                block.tensor(_)
                tok_tp = cx.tok('pe')

                def _(v, c=c, dc=dc):
                    cx.wait(v, 'v', tok_tp)
                    cx.inc('m_dfm_ev', v.tensor_copy(sb['dfm'][:, dc, c * 128:(c + 1) * 128],
                                                     ps['T'][:, :]))
                block.vector(_)
        tok_dfm = cx.tok('m_dfm_ev')

        def _(v, e=e):
            cx.wait(v, 'v', tok_dfm)
            cx.inc('m_dtm_free', v.nop())
        block.vector(_)

        for sc in range(2):
            def _(t, sc=sc, e=e):
                cx.wait(t, 't', tok_dfm, cx.tok('dwg'), cx.tok('m_gu_ev'))
                last = None
                for di in range(DT):
                    t.ldweights(sb['dfm'][:, di, sc * 128:(sc + 1) * 128])
                    for (w0, w1, p0) in ((0, 512, 0), (512, 768, 512),
                                         (768, 1280, 1024), (1280, 1536, 1536)):
                        last = t.matmul(ps['A'][:, p0:p0 + (w1 - w0)],
                                        sb['dfm'][:, di, sc * 128:(sc + 1) * 128],
                                        sb['wgux'][:, di, w0:w1],
                                        start=(di == 0), stop=(di == DT - 1))
                cx.inc('pe', last)
            block.tensor(_)
            tok_gu = cx.tok('pe')

            def _(s2, sc=sc):
                cx.wait(s2, 's', tok_gu)
                cx.inc('s', s2.activation(out=sb['gsl'][:, :], in_=ps['A'][:, 0:INTER],
                                          func=AF.Silu))
            block.scalar(_)
            tok_sil = cx.tok('s')

            def _(v, sc=sc):
                cx.wait(v, 'v', tok_sil)
                cx.inc('m_gu_ev', v.tensor_tensor(
                    sb['gusl'][:, sc, :],
                    ps['A'][:, 1024:1024 + INTER].rearrange("p (a b) -> p a b", a=2).rearrange(
                        "p a b -> p (a b)") if False else ps['A'][:, 1024:1024 + INTER],
                    sb['gsl'][:, :], ALU.mult))
            block.vector(_)
        tok_gud = cx.tok('m_gu_ev')

        if e + 1 < NEC:
            def _(y, e=e):
                cx.wait(y, 'y', tok_gud)
                cx.inc('dwg', y.dma_start(
                    out=sb['wgux'][:, :, 0:INTER],
                    in_=dram['wg'].rearrange("e (c p) i -> e p c i", p=128)[e + 1]), 16)
                cx.inc('dwg', y.dma_start(
                    out=sb['wgux'][:, :, INTER:2 * INTER],
                    in_=dram['wu'].rearrange("e (c p) i -> e p c i", p=128)[e + 1]), 16)
            block.sync(_)

        for sc in range(2):
            for ic in range(INTER // 128):
                def _(t, sc=sc, ic=ic):
                    cx.wait(t, 't', tok_gud, cx.tok('m_gufm_ev'))
                    cx.inc('pe', t.transpose(ps['T'][:, :],
                                             sb['gusl'][:, sc, ic * 128:(ic + 1) * 128],
                                             sb['ident'][:, :]))
                block.tensor(_)
                tok_tp2 = cx.tok('pe')

                def _(v, sc=sc, ic=ic):
                    cx.wait(v, 'v', tok_tp2)
                    cx.inc('m_gufm_ev', v.tensor_copy(sb['gufm'][:, ic, sc * 128:(sc + 1) * 128],
                                                      ps['T'][:, :]))
                block.vector(_)
        tok_gufm = cx.tok('m_gufm_ev')

        for sc in range(2):
            def _(t, sc=sc, e=e):
                cx.wait(t, 't', tok_gufm, cx.tok('dwd'), cx.tok('m_y_ev'))
                last = None
                for ic in range(INTER // 128):
                    t.ldweights(sb['gufm'][:, ic, sc * 128:(sc + 1) * 128])
                    last = t.matmul(ps['PV'][:, 0:512], sb['gufm'][:, ic, sc * 128:(sc + 1) * 128],
                                    sb['wdx'][:, ic, 0:512],
                                    start=(ic == 0), stop=(ic == INTER // 128 - 1))
                    last = t.matmul(ps['PV'][:, 512:1024], sb['gufm'][:, ic, sc * 128:(sc + 1) * 128],
                                    sb['wdx'][:, ic, 512:1024],
                                    start=(ic == 0), stop=(ic == INTER // 128 - 1))
                cx.inc('pe', last)
            block.tensor(_)
            tok_dn = cx.tok('pe')

            def _(v, sc=sc):
                cx.wait(v, 'v', tok_dn, tok_idx)
                cx.inc('m_y_ev', v.tensor_scalar(sb['ytm'][:, sc, :], ps['PV'][:, 0:1024],
                                                 sb['tsl'][:, sc, 1:2], None, ALU.mult))
            block.vector(_)
        tok_y = cx.tok('m_y_ev')

        if e + 1 < NEC:
            def _(y, e=e):
                cx.wait(y, 'y', tok_y)
                cx.inc('dwd', y.dma_start(
                    out=sb['wdx'][:, :, :],
                    in_=dram['wd'].rearrange("e (c p) h -> e p c h", p=128)[e + 1]), 16)
            block.sync(_)

        def _(g, e=e):
            cx.wait(g, 'g', tok_y)
            with g.If_eq(sbn['g_rreg'], 0):
                cx.inc('dyw', g.dma_start(
                    out=shr['Ywsh'].rearrange("(s p) d -> p s d", p=128)[:, e * 2:(e + 1) * 2, :],
                    in_=sb['ytm'][:, :, :]), 16)
            with g.Else():
                g.dma_start(
                    out=shr['Ywsh'].rearrange("(s p) d -> p s d", p=128)[:, (NEC + e) * 2:(NEC + e + 1) * 2, :],
                    in_=sb['ytm'][:, :, :]).then_inc(cx.sem('dyw'), 16)
            cx.inc('m_yfree', g.nop())
            g.wait_ge(cx.sem('dyw'), cx.cnt['dyw'] - 16)
            g.wait_ge(cx.sem('dgx'), cx.cnt['dgx'])
            g.wait_ge(cx.sem('dtl'), cx.cnt['dtl'])
        block.gpsimd(_)

        def _(v, e=e):
            cx.wait(v, 'v', cx.tok('m_yfree'))
        block.vector(_)

    def _(g):
        g.wait_ge(cx.sem('dyw'), cx.cnt['dyw'])
    block.gpsimd(_)
    emit_flag_write(cx, block, F_YW, shr, sbn)
    emit_flag_poll(cx, block, F_YW, shr, sbn)
    tok_ywbar = cx.tok('g')

    for b in range(NBC):
        fslot = b + 16 * cid
        for en, ek in (('vector', 'v'), ('gpsimd', 'g'), ('sync', 'y')):
            def _(eng, ek=ek, fslot=fslot):
                cx.wait(eng, ek, tok_ywbar)
                eng.reg_load(cx.sbn[ek + '_lreg'], sb['flagbuf_i'][0:1, fslot:fslot + 1])
            getattr(block, en)(_)

        for tt in range(ST):
            trow = b * S + tt * 128

            def _(y, trow=trow):
                cx.wait(y, 'y', cx.tok('m_h1used'))
                cx.inc('dJl', y.dma_start(out=sb['h0t'][:, :],
                                          in_=dram['hin'][trow:trow + 128, :]), 16)
            block.sync(_)

            def ybody3(y, trow=trow):
                cx.inc('dJl', y.dma_start(out=sb['jib'][:, 0, :],
                                          in_=dram['jd'][trow:trow + 128, :]), 16)
            block.sync(balanced_if(cx, 'y', ybody3))
            tok_jl = cx.tok('dJl')

            def vbody3(v):
                cx.wait(v, 'v', cx.tok('m_kb_used'))
                cx.inc('v', v.memset(sb['kbuf'][:, :, :], 0.0))
            block.vector(balanced_if(cx, 'v', vbody3))
            tok_kz = cx.tok('v')

            def gbody3(g, trow=trow):
                cx.wait(g, 'g', tok_jl, tok_kz)
                for k in range(TOPK):
                    cx.inc('dKg', g.indirect_dma_start(
                        sb['kbuf'][:, k, :], None,
                        shr['Ywsh'][:, :],
                        IndirectOffsetOnAxis(sb['jib'][:, 0, k].unsqueeze(1), 0),
                        bounds_check=sbn['bnd_slot'], oob_is_err=False), 16)
                cx.inc('m_kgat', g.nop())
            block.gpsimd(balanced_if(cx, 'g', gbody3))

            def _(g):
                g.wait_ge(cx.sem('dKg'), cx.cnt['dKg'])
            block.gpsimd(_)
            tok_kg = cx.tok('dKg')
            tok_kgat = cx.tok('m_kgat')

            def _(v, trow=trow):
                cx.wait(v, 'v', tok_kg, tok_kgat, cx.tok('dJl'))
                snap = dict(cx.cnt)
                wsnap = dict(cx.wc)
                with v.If_lt(cx.sbn['v_lreg'], 256):
                    cx.inc('v', v.tensor_tensor(sb['kb4'][:, 0, :], sb['kbuf'][:, 0, :],
                                                sb['kbuf'][:, 1, :], ALU.add))
                    cx.inc('v', v.tensor_tensor(sb['kb4'][:, 1, :], sb['kbuf'][:, 2, :],
                                                sb['kbuf'][:, 3, :], ALU.add))
                    cx.inc('v', v.tensor_tensor(sb['kb4'][:, 2, :], sb['kbuf'][:, 4, :],
                                                sb['kbuf'][:, 5, :], ALU.add))
                    cx.inc('v', v.tensor_tensor(sb['kb4'][:, 3, :], sb['kbuf'][:, 6, :],
                                                sb['kbuf'][:, 7, :], ALU.add))
                    v.wait_ge(cx.sem('v'), cx.cnt['v'])
                    cx.inc('v', v.tensor_tensor(sb['kb4'][:, 0, :], sb['kb4'][:, 0, :],
                                                sb['kb4'][:, 1, :], ALU.add))
                    cx.inc('v', v.tensor_tensor(sb['kb4'][:, 2, :], sb['kb4'][:, 2, :],
                                                sb['kb4'][:, 3, :], ALU.add))
                    v.wait_ge(cx.sem('v'), cx.cnt['v'])
                    cx.inc('v', v.tensor_tensor(sb['kb4'][:, 1, :], sb['kb4'][:, 0, :],
                                                sb['kb4'][:, 2, :], ALU.add))
                    v.wait_ge(cx.sem('v'), cx.cnt['v'])
                    cx.inc('m_kb_used', v.tensor_tensor(sb['h1t'][:, :], sb['h0t'][:, :],
                                                        sb['kb4'][:, 1, :], ALU.add))
                    cx.inc('m_h1used', v.nop())
                cx.wc = wsnap
                with v.Else():
                    dv = cx.cnt['v'] - snap.get('v', 0)
                    v.tensor_copy(sb['h1t'][:, :], sb['h0t'][:, :]).then_inc(
                        cx.sem('m_kb_used'), 1)
                    v.sem_inc(cx.sem('v'), dv)
                    v.sem_inc(cx.sem('m_h1used'), 1)
            block.vector(_)
            tok_h2 = cx.tok('m_kb_used')

            def _(y, trow=trow):
                cx.wait(y, 'y', tok_h2)
                cx.inc('dh2', y.dma_start(out=dram['hout'][trow:trow + 128, :],
                                          in_=sb['h1t'][:, :]), 16)
            block.sync(_)

    def _(g):
        cx.wait(g, 'g', cx.tok('dh2'))
        cx.inc('moe_prev_done', g.nop())
    block.gpsimd(_)
    return cx.tok('dh2')





def alloc_moe(nc, E, cx, sb, S, NBC, do_moe=True):
    ST = S // 128
    TOKC = NBC * S
    full = (S == 2048)
    if not do_moe:
        pass
    elif full:
        sb['lgb'] = sb['srow'][:, 0:ST * 64].rearrange("p (a e) -> p a e", e=64)
        sb['prb'] = sb['srow'][:, ST * 64:2 * ST * 64].rearrange("p (a e) -> p a e", e=64)
        sb['prc'] = sb['dwide'][:, 0:ST * 64].rearrange("p (a e) -> p a e", e=64)
        sb['msk'] = sb['dwide'][:, ST * 64:2 * ST * 64].rearrange("p (a e) -> p a e", e=64)
        sb['eqm'] = sb['ln_sqs'][:, 0:ST * 64].rearrange("p (a e) -> p a e", e=64)
        sb['Hbb'] = sb['wqk'][:, 0, 0:ST * 64].rearrange("p (a e) -> p a e", e=64)
        sb['dtm'] = sb['pT'][:, :, :].rearrange("p a b -> p (a b)").rearrange(
            "p (c d) -> p c d", d=D)
        sb['dfm'] = sb['prow'][:, 0:2048].rearrange("p (c d) -> p c d", d=256)
        sb['ytm'] = sb['wvt'][:, 0:2, :]
        sb['gsl'] = sb['wvt'][:, 2, 0:INTER]
        sb['gusl'] = sb['wvt'][:, 3:5, 0:INTER]
        sb['gufm'] = sb['wvt'][:, 5:7, :].rearrange("p a b -> p (a b)")[:, 0:1536].rearrange("p (c d) -> p c d", d=256)
        sb['wgux'] = sb['n1fm'][:, :, 0:2 * INTER]
        sb['wdx'] = sb['qfm'][:, 0:6, 0:1024]
        sb['kbuf'] = sb['vtm'][:, 0:8, :]
        sb['kb4'] = sb['vtm'][:, 8:16, :].rearrange("p a b -> p (a b)").bitcast(
            F32).rearrange("p (a b) -> p a b", b=1024)
    else:
        sb['lgb'] = E(nc.sbuf_tensor("x_lgb", [128, ST, 64], F32))
        sb['prb'] = E(nc.sbuf_tensor("x_prb", [128, ST, 64], F32))
        sb['prc'] = E(nc.sbuf_tensor("x_prc", [128, ST, 64], F32))
        sb['msk'] = E(nc.sbuf_tensor("x_msk", [128, ST, 64], F32))
        sb['eqm'] = E(nc.sbuf_tensor("x_eqm", [128, ST, 64], F32))
        sb['Hbb'] = E(nc.sbuf_tensor("x_Hbb", [128, ST, 64], BF16))
        sb['dtm'] = E(nc.sbuf_tensor("x_dtm", [128, 2, D], BF16))
        sb['dfm'] = E(nc.sbuf_tensor("x_dfm", [128, 8, 256], BF16))
        sb['ytm'] = E(nc.sbuf_tensor("x_ytm", [128, 2, D], BF16))
        sb['gsl'] = E(nc.sbuf_tensor("x_gsl", [128, INTER], BF16))
        sb['gusl'] = E(nc.sbuf_tensor("x_gusl", [128, 2, INTER], BF16))
        sb['gufm'] = E(nc.sbuf_tensor("x_gufm", [128, 6, 256], BF16))
        sb['wgux'] = E(nc.sbuf_tensor("x_wgux", [128, DT, 2 * INTER], BF16))
        sb['wdx'] = E(nc.sbuf_tensor("x_wdx", [128, 6, 1024], BF16))
        sb['kbuf'] = E(nc.sbuf_tensor("x_kbuf", [128, 8, D], BF16))
        sb['kb4'] = E(nc.sbuf_tensor("x_kb4", [128, 4, D], F32))
    sb['n2f32'] = E(nc.sbuf_tensor("n2f32", [128, D], F32))
    sb['n2f32b'] = sb['qfm'][:, :, :].rearrange("p a b -> p (a b)").bitcast(
        F32)[:, 0:D]
    sb['moe1'] = E(nc.sbuf_tensor("moe1", [128, 3, ST * 64], F32))
    sb['Hb'] = sb['moe1'][:, 0, :].rearrange("p (a e) -> p a e", e=64)
    sb['whb'] = sb['moe1'][:, 1, :].rearrange("p (a e) -> p a e", e=64)
    sb['Cb'] = sb['moe1'][:, 2, :].rearrange("p (a e) -> p a e", e=64)
    sb['gwcat'] = E(nc.sbuf_tensor("gwcat", [128, DT, 128], BF16))
    sb['gwcat2'] = E(nc.sbuf_tensor("gwcat2", [128, DT, 128], BF16))
    sb['gwf'] = E(nc.sbuf_tensor("gwf", [128, DT, 64], F32))
    sb['gsc'] = E(nc.sbuf_tensor("gsc", [128, 64], F32))
    sb['m16'] = E(nc.sbuf_tensor("m16", [128, ST], F32))
    sb['r16'] = E(nc.sbuf_tensor("r16", [128, ST], F32))
    sb['fmin'] = E(nc.sbuf_tensor("fmin", [1, 1], F32))
    sb['Osb'] = E(nc.sbuf_tensor("Osb", [1, 64], F32))
    sb['Osb2'] = E(nc.sbuf_tensor("Osb2", [1, 64], F32))
    sb['carry'] = E(nc.sbuf_tensor("carry", [1, 64], F32))
    sb['tri'] = E(nc.sbuf_tensor("tri", [128, 128], BF16))
    sb['onescol'] = E(nc.sbuf_tensor("onescol", [128, 1], BF16))
    sb['iotaMB'] = E(nc.sbuf_tensor("iotaMB", [128, 64], F32))
    sb['rolecol'] = E(nc.sbuf_tensor("rolecol", [128, 1], F32))
    sb['rolef1'] = E(nc.sbuf_tensor("rolef1", [1, 1], F32))
    sb['ilocb'] = E(nc.sbuf_tensor("ilocb", [128, ST], F32))
    sb['ekb'] = E(nc.sbuf_tensor("ekb", [128, ST, 8], F32))
    sb['posb'] = E(nc.sbuf_tensor("posb", [128, ST, 8], F32))
    sb['jfb'] = E(nc.sbuf_tensor("jfb", [128, ST, 8], F32))
    sb['jib'] = E(nc.sbuf_tensor("jib", [128, ST, 8], I32))
    sb['pay'] = E(nc.sbuf_tensor("pay", [128, ST, 8, 2], F32))
    sb['flagbuf_i'] = E(nc.sbuf_tensor("flagbuf_i", [1, 32], I32))
    sb['zeros2'] = E(nc.sbuf_tensor("zeros2", [1, 2], F32))
    sb['tsl'] = E(nc.sbuf_tensor("tsl", [128, 2, 2], F32))
    sb['eqz'] = E(nc.sbuf_tensor("eqz", [128, 2], F32))
    sb['idxf'] = E(nc.sbuf_tensor("idxf", [128, 2], F32))
    sb['idxi'] = E(nc.sbuf_tensor("idxi", [128, 2], I32))
    sb['noncebuf'] = E(nc.sbuf_tensor("noncebuf", [1, 4], I32))
    sb['rsb_i'] = E(nc.sbuf_tensor("rsb_i", [1, 1], I32))
    sb['nsb_i'] = E(nc.sbuf_tensor("nsb_i", [1, 1], I32))
    sb['flagrd'] = E(nc.sbuf_tensor("flagrd", [1, 4], I32))
    sb['dummy_sb'] = E(nc.sbuf_tensor("dummy_sb", [1, 4], F32))


def emit_preamble_moe(cx, block, sb, ps, shr, sbn, ins, S, NBC, do_moe=True):
    TOKC = NBC * S
    nc = cx.nc

    def _(y):
        if do_moe:
            cx.inc('dw2', y.dma_start(out=sb['gwf'][:, :, :],
                                      in_=ins['gate_wt'].rearrange("(c p) e -> p c e", p=128)), 16)
        cx.inc('dw2', y.dma_start(out=sb['rolef1'][:, :], in_=ins['rolef'][:, :]), 16)
        cx.inc('dw2', y.dma_start(out=sb['rsb_i'][:, :], in_=ins['role'][:, :]), 16)
        cx.inc('dw2', y.dma_start(out=sb['nsb_i'][:, :], in_=ins['nonce'][:, :]), 16)
    block.sync(_)
    tok_l = cx.tok('dw2')

    def _(g):
        cx.wait(g, 'g', tok_l)
        g.reg_load(sbn['g_rreg'], sb['rsb_i'][0:1, 0:1])
        g.reg_load(sbn['nreg'], sb['nsb_i'][0:1, 0:1])
        g.reg_mov(sbn['pc'], 0)
        g.reg_mov(sbn['bnd_slot'], NE * CAP - 1)
        g.reg_mov(sbn['bnd_tok'], 2 * TOKC - 1)
        cx.inc('g', g.iota(sb['iotaMB'][:, :], [[1, 64]], base=-1000000, channel_multiplier=0,
                           allow_small_or_imprecise_dtypes=True))
        if do_moe:
            for e in range(NEC):
                cx.inc('dwc', g.dma_start(out=cx.dram_wg[e], in_=ins['wg'][e]), 16)
                cx.inc('dwc', g.dma_start(out=cx.dram_wu[e], in_=ins['wu'][e]), 16)
                cx.inc('dwc', g.dma_start(out=cx.dram_wd[e], in_=ins['wd'][e]), 16)
    block.gpsimd(_)
    tok_g = cx.tok('g')

    def _(v):
        cx.wait(v, 'v', tok_l, tok_g)
        cx.inc('v', v.memset(sb['noncebuf'][:, :], 0))
        cx.wait(v, 'v', cx.tok('v'))
        cx.inc('v', v.tensor_copy(sb['noncebuf'][0:1, 0:1], sb['nsb_i'][0:1, 0:1]))
        cx.inc('v', v.memset(sb['zeros2'][:, :], 0.0))
        cx.inc('v', v.memset(sb['onescol'][:, :], 1.0))
        cx.inc('v', v.memset(sb['dummy_sb'][:, :], 0.0))
        cx.inc('v', v.tensor_scalar(sb['tri'][:, :], sb['kkrow'][:, :], sb['qqcol'][:, 0:1],
                                    None, ALU.is_gt))
        if do_moe:
            cx.inc('v', v.tensor_copy(sb['gwcat'][:, :, 0:64], sb['gwf'][:, :, :]))
            cx.wait(v, 'v', cx.tok('v'))
            cx.inc('v', v.tensor_tensor(sb['gwcat'][:, :, 64:128], sb['gwf'][:, :, :],
                                        sb['gwcat'][:, :, 0:64], ALU.subtract))
            cx.wait(v, 'v', cx.tok('v'))
            cx.inc('v', v.tensor_copy(sb['gwcat2'][:, :, 0:64], sb['gwcat'][:, :, 0:64]))
            cx.inc('v', v.memset(sb['gwcat2'][:, :, 64:128], 0.0))
    block.vector(_)
    tok_v = cx.tok('v')

    def _(t):
        cx.wait(t, 't', tok_v)
        cx.inc('pe', t.matmul(ps['B'][:, 0:1], sb['ones128'][:, :], sb['rolef1'][:, :],
                              start=True, stop=True))
    block.tensor(_)
    tok_rb = cx.tok('pe')

    def _(v):
        cx.wait(v, 'v', tok_rb)
        cx.inc('v', v.tensor_scalar(sb['rolecol'][:, :], ps['B'][:, 0:1], float(TOKC), None,
                                    ALU.mult))
    block.vector(_)





def emit_attn_b(cx, block, sb, ps, shr, sbn, dram, ins, S, NBC, E2):
    nc = cx.nc
    ST = S // 128
    BT = ST
    TOKC = NBC * S
    NH2 = NHEAD // 2
    F_X0, F_SC, F_XG, F_AR = 10, 11, 12, 13
    LMT = LM // 128

    def _(y):
        cx.wait(y, 'y', cx.tok('dh2'), cx.tok('dh1'))
        cx.inc('b_lnfree', y.nop())
        cx.inc('dw2', y.dma_start(out=sb['wvb1'][:, :], in_=ins['wv_b'][:, :]), 16)
    block.sync(_)

    def _(g):
        cx.wait(g, 'g', cx.tok('dh2'), cx.tok('dw'), cx.tok('dwb'))
        g.reg_mov(sbn['bnd_S2'], S * S - 1)
        cx.inc('dwb', g.dma_start(out=sb['wvw'][:, :, :],
                                  in_=ins['wv_w'].rearrange("(c p) o -> p c o", p=128)), 16)
        cx.inc('dwb', g.dma_start(out=sb['outwh'][:, :, :],
                                  in_=ins['outw'].rearrange("(h p) o -> p h o", p=64)), 16)
        cx.inc('g', g.iota(sb['idt'][:, :], [[1, 256]], base=0, channel_multiplier=-1,
                           allow_small_or_imprecise_dtypes=True))
        cx.inc('g', g.iota(sb['rtrow'][:, :], [[1, ST]], base=0, channel_multiplier=0,
                           allow_small_or_imprecise_dtypes=True))
    block.gpsimd(_)
    tok_wv = cx.tok('dwb')
    tok_idt = cx.tok('g')

    def _(v):
        cx.wait(v, 'v', tok_idt)
        cx.inc('v', v.tensor_scalar(sb['idI'][:, 0, :], sb['idt'][:, :], 0.0, None,
                                    ALU.is_equal))
        cx.wait(v, 'v', cx.tok('v'))
        cx.inc('v', v.tensor_scalar(sb['idI'][:, 1, :], sb['idt'][:, :], 128.0, None,
                                    ALU.is_equal))
        cx.wait(v, 'v', cx.tok('b_lnfree'))
        cx.inc('v', v.memset(sb['h0t'][:, :], 0.0))
        cx.inc('v', v.memset(sb['onescolF'][:, :], 1.0))
    block.vector(_)
    tok_zb = cx.tok('v')

    def _(y):
        cx.wait(y, 'y', tok_zb)
        for zt in range(S // 128):
            cx.inc('dxz', y.dma_start(out=dram['xgp'][zt * 128:(zt + 1) * 128, :],
                                      in_=sb['h0t'][:, :]), 16)
    block.sync(_)

    def _(t):
        cx.wait(t, 't', cx.tok('dw2'), cx.tok('v'), cx.tok('s'), cx.tok('pe'))
        t.matmul(ps['PV'][:, 0:512], sb['ones128'][0:1, :], sb['wvb1'][0:1, 0:512],
                 start=True, stop=True)
        last = t.matmul(ps['PV'][:, 512:1024], sb['ones128'][0:1, :],
                        sb['wvb1'][0:1, 512:1024], start=True, stop=True)
        cx.inc('pe', last)
    block.tensor(_)
    tok_bb = cx.tok('pe')

    def _(v):
        cx.wait(v, 'v', tok_bb)
        cx.inc('b_pv', v.tensor_copy(sb['wvbb'][:, :], ps['PV'][:, 0:1024]))
    block.vector(_)

    st_used = None
    for b in range(NBC):
        base = b * S
        for tt in range(ST):
            pb = (b * ST + tt) % 2
            srcb = sb['lnin'] if pb == 0 else sb['lnin_b']
            dstb = sb['n2f32'] if pb == 0 else sb['n2f32b']
            kfree = ('b_lnfree', max(0, cx.cnt.get('b_lnfree', 0) - 1))
            dfree = ('dn3f', max(0, cx.cnt.get('dn3f', 0) - 16))

            def _(y, tt=tt, b=b, srcb=srcb, kfree=kfree):
                cx.wait(y, 'y', kfree)
                cx.inc('dln', y.dma_start(out=srcb[:],
                                          in_=dram['hin'][base + tt * 128:base + (tt + 1) * 128, :]), 16)
            block.sync(_)
            tok_ld = cx.tok('dln')

            def _(v, kfree=kfree, dfree=dfree):
                cx.wait(v, 'v', tok_ld, st_used, kfree, dfree)
            block.vector(_)

            def _(s2, kfree=kfree, dfree=dfree):
                cx.wait(s2, 's', tok_ld, st_used, dfree, tok_bb, kfree)
            block.scalar(_)
            tok_ln = emit_ln_tile(cx, block, sb, srcb, dstb, dfree,
                                  scr=('' if pb == 0 else 'b'))

            def _(s2, tt=tt, dstb=dstb):
                cx.wait(s2, 's', tok_ln)
                cx.inc('s', s2.activation(out=sb['vtm'][:, tt, :], in_=dstb[:],
                                          func=AF.Identity))
                cx.inc('b_lnfree', s2.nop())
            block.scalar(_)
            tok_hi = cx.tok('s')

            def _(y, tt=tt, b=b, dstb=dstb):
                cx.wait(y, 'y', tok_hi)
                cx.inc('dn3f', y.dma_start(out=dram['n3f'][base + tt * 128:base + (tt + 1) * 128, :],
                                           in_=dstb[:]), 16)
            block.sync(_)
        tok_stage_s = cx.tok('s')

        def _(y, b=b):
            cx.wait(y, 'y', tok_stage_s)
            cx.inc('dn1', y.dma_start(
                out=dram['n3l'].rearrange("(a p) d -> p a d", p=128)[:, b * BT:(b + 1) * BT, :],
                in_=sb['vtm'][:, :, :]), 16)
        block.sync(_)
        st_used = cx.tok('dn1')

    def _(g):
        cx.wait(g, 'g', cx.tok('dn1'))
        cx.inc('dwb', g.dma_start(out=sb['qkvw2'][:, :, :],
                                  in_=ins['qkv_w'].rearrange("(c p) o -> p c o", p=128)), 16)
        cx.inc('dwb', g.dma_start(out=sb['qkvh'][:, :, :, :],
                                  in_=ins['qkvh'].rearrange("(c p) h w -> p h c w", p=128)), 16)
    block.gpsimd(_)
    tok_qw2 = cx.tok('dwb')

    def _(v):
        cx.wait(v, 'v', st_used)
        for nm in ['px', 'pxT', 'pz', 'pzT', 'pxz', 'pxzT', 'ptA', 'ptB',
                   'a2sb', 'a2bf', 'a3b', 'a1b', 'a1T', 'a3T', 'nlmkb', 'vT', 'a3v']:
            cx.inc('v', v.memset(sb[nm][:, :, :], 0.0))
    block.vector(_)

    def _(g):
        cx.wait(g, 'g', cx.tok('dn1'), cx.tok('dn3f'))
        with g.If_eq(sbn['g_rreg'], 0):
            cx.inc('dsh', g.dma_start(out=shr['x0sh'][:, :], in_=dram['n3l'][0:S, :]), 16)
        with g.Else():
            g.dma_start(out=cx.dummy_dram[0:1, 0:1],
                        in_=cx.dummy_sb[0:1, 0:1]).then_inc(cx.sem('dsh'), 16)
        g.wait_ge(cx.sem('dsh'), cx.cnt['dsh'])
    block.gpsimd(_)
    emit_flag_write(cx, block, F_X0, shr, sbn, only_role=0)
    emit_flag_poll(cx, block, F_X0, shr, sbn, only_role=1, from_role=0)
    if B_STOP == 'b0':
        return cx.tok('dh3')

    def _(v):
        cx.wait(v, 'v', cx.tok('dn1'))
        cx.inc('v', v.memset(sb['scmax'][:, :], 0.0))
    block.vector(_)

    for b in range(NBC):
        base = b * S

        def _(y, b=b):
            cx.wait(y, 'y', cx.tok('dn1'), cx.tok('pe'))
            for dc in range(DT):
                cx.inc('dtp', y.dma_start(out=sb['n1fm'][:, dc, 0:S],
                                          in_=dram['n3l'][base:base + S, dc * 128:(dc + 1) * 128],
                                          transpose=True), 16)
        block.sync(_)
        tok_fm = cx.tok('dtp')

        for tc in range(ST):
            def _(t, tc=tc):
                cx.wait(t, 't', tok_fm, cx.tok('b_pv'), tok_wv)
                last = None
                for di in range(DT):
                    t.ldweights(sb['n1fm'][:, di, tc * 128:(tc + 1) * 128])
                    last = t.matmul(ps['PV'][:, 0:512], sb['n1fm'][:, di, tc * 128:(tc + 1) * 128],
                                    sb['wvw'][:, di, 0:512], start=(di == 0), stop=(di == DT - 1))
                    last = t.matmul(ps['PV'][:, 512:1024], sb['n1fm'][:, di, tc * 128:(tc + 1) * 128],
                                    sb['wvw'][:, di, 512:1024], start=(di == 0), stop=(di == DT - 1))
                cx.inc('pe', last)
            block.tensor(_)
            tok_mm = cx.tok('pe')

            def _(v, tc=tc):
                cx.wait(v, 'v', tok_mm, cx.tok('dval'))
                cx.inc('b_pv', v.tensor_tensor(sb['h1t'][:, :], ps['PV'][:, 0:1024],
                                               sb['wvbb'][:, :], ALU.add))
            block.vector(_)
            tok_ev = cx.tok('b_pv')

            def _(y, tc=tc, b=b):
                cx.wait(y, 'y', tok_ev)
                cx.inc('dval', y.dma_start(
                    out=dram['val'][base + tc * 128:base + (tc + 1) * 128, :],
                    in_=sb['h1t'][:, :]), 16)
            block.sync(_)

        nlm = S // LGRP

        def _(v, b=b):
            cx.wait(v, 'v', tok_fm, cx.tok('b_a2done'), cx.tok('dn3f'))
            for dc in range(DT):
                cx.wait(v, 'v', cx.tok('v'))
                cx.inc('v', v.tensor_reduce(
                    sb['n2f32'][:, 0:nlm],
                    sb['n1fm'][:, dc, 0:S].rearrange("p (l g) -> p l g", g=LGRP),
                    axis=AX.X, op=ALU.add))
                cx.wait(v, 'v', cx.tok('v'))
                cx.inc('v', v.tensor_scalar(sb['xlf'][:, dc, 0:nlm], sb['n2f32'][:, 0:nlm],
                                            1.0 / LGRP, None, ALU.mult))
        block.vector(_)
        tok_xl = cx.tok('v')
        for qc in range(8):
            def _(t, qc=qc):
                cx.wait(t, 't', tok_xl, cx.tok('b_qle'), tok_qw2)
                last = None
                for di in range(DT):
                    t.ldweights(sb['qkvw2'][:, di, qc * 128:(qc + 1) * 128])
                    last = t.matmul(ps['B'][:, 0:nlm], sb['qkvw2'][:, di, qc * 128:(qc + 1) * 128],
                                    sb['xlf'][:, di, 0:nlm], start=(di == 0), stop=(di == DT - 1))
                cx.inc('pe', last)
            block.tensor(_)
            tok_qmm = cx.tok('pe')

            def _(v, qc=qc):
                cx.wait(v, 'v', tok_qmm)
                sc = (DH ** -0.5) if qc < 4 else 1.0
                cx.inc('b_qle', v.tensor_scalar(sb['qlkl'][:, qc, 0:nlm], ps['B'][:, 0:nlm], sc,
                                                None, ALU.mult))
            block.vector(_)
        tok_qlkl = cx.tok('b_qle')

        for h in range(NHEAD):
            hc, ho = divmod(h * DH, 128)
            nl2 = (nlm + 127) // 128
            for lc in range(nl2):
                lw = min(128, nlm - lc * 128)

                def _(t, hc=hc, ho=ho, lc=lc, lw=lw):
                    cx.wait(t, 't', tok_qlkl, cx.tok('b_a2e'))
                    cx.inc('pe', t.matmul(ps['B'][0:lw, 0:nlm],
                                          sb['qlkl'][ho:ho + DH, hc, lc * 128:lc * 128 + lw],
                                          sb['qlkl'][ho:ho + DH, 4 + hc, 0:nlm],
                                          start=True, stop=True))
                block.tensor(_)
                tok_a2 = cx.tok('pe')

                def _(v, lc=lc, lw=lw):
                    cx.wait(v, 'v', tok_a2)
                    cx.inc('b_a2e', v.tensor_copy(sb['a2sb'][0:lw, lc, 0:nlm],
                                                  ps['B'][0:lw, 0:nlm]))
                block.vector(_)
            tok_a2ev = cx.tok('b_a2e')

            def _(v, h=h):
                cx.wait(v, 'v', tok_a2ev)
                nl2 = (nlm + 127) // 128
                cx.inc('v', v.tensor_reduce(sb['m16'][:, 0:nl2], sb['a2sb'][:, 0:nl2, 0:nlm],
                                            axis=AX.X, op=ALU.max))
                cx.wait(v, 'v', cx.tok('v'))
                cx.inc('v', v.tensor_tensor(sb['a2sb'][:, 0:nl2, 0:nlm],
                                            sb['a2sb'][:, 0:nl2, 0:nlm],
                                            sb['m16'][:, 0:nl2].unsqueeze(2).broadcast_to(
                                                [128, nl2, nlm]), ALU.subtract))
            block.vector(_)
            tok_sm = cx.tok('v')

            def _(s2, h=h):
                cx.wait(s2, 's', tok_sm)
                cx.inc('s', s2.activation(out=sb['a2sb'][:, 0:(nlm + 127) // 128, 0:nlm],
                                          in_=sb['a2sb'][:, 0:(nlm + 127) // 128, 0:nlm],
                                          func=AF.Exp))
            block.scalar(_)
            tok_ex = cx.tok('s')

            def _(v, h=h):
                cx.wait(v, 'v', tok_ex)
                nl2 = (nlm + 127) // 128
                cx.inc('v', v.tensor_reduce(sb['m16'][:, 0:nl2], sb['a2sb'][:, 0:nl2, 0:nlm],
                                            axis=AX.X, op=ALU.add))
                cx.wait(v, 'v', cx.tok('v'))
                cx.inc('v', v.reciprocal(sb['r16'][:, 0:nl2], sb['m16'][:, 0:nl2]))
                cx.wait(v, 'v', cx.tok('v'))
                cx.inc('v', v.tensor_tensor(sb['a2bf'][:, 0:nl2, 0:nlm],
                                            sb['a2sb'][:, 0:nl2, 0:nlm],
                                            sb['r16'][:, 0:nl2].unsqueeze(2).broadcast_to(
                                                [128, nl2, nlm]), ALU.mult))
            block.vector(_)
            tok_pr = cx.tok('v')

            def _(t, h=h):
                cx.wait(t, 't', tok_pr, cx.tok('b_csum'))
                nl2 = (nlm + 127) // 128
                last = None
                for lc in range(nl2):
                    lw = min(128, nlm - lc * 128)
                    last = t.matmul(ps['A'][0:1, 0:nlm], sb['onescol'][0:lw, 0:1],
                                    sb['a2bf'][0:lw, lc, 0:nlm], start=(lc == 0),
                                    stop=(lc == nl2 - 1))
                cx.inc('pe', last)
            block.tensor(_)
            tok_cs2 = cx.tok('pe')

            def _(v, h=h):
                cx.wait(v, 'v', tok_cs2)
                cx.inc('v', v.tensor_reduce(sb['sc1'][0:1, 0:1], ps['A'][0:1, 0:nlm],
                                            axis=AX.X, op=ALU.max))
                cx.wait(v, 'v', cx.tok('v'))
                cx.inc('b_csum', v.tensor_max(sb['scmax'][0:1, 0:1], sb['scmax'][0:1, 0:1],
                                              sb['sc1'][0:1, 0:1]))
                cx.inc('b_a2done', v.nop())
            block.vector(_)

    def _(g):
        cx.wait(g, 'g', cx.tok('v'))
        with g.If_eq(sbn['g_rreg'], 0):
            cx.inc('dsh', g.dma_start(out=shr['scsh'][0, :, :], in_=sb['scmax'][0:1, 0:4]), 16)
        with g.Else():
            g.dma_start(out=shr['scsh'][1, :, :], in_=sb['scmax'][0:1, 0:4]).then_inc(
                cx.sem('dsh'), 16)
        g.wait_ge(cx.sem('dsh'), cx.cnt['dsh'])
    block.gpsimd(_)
    emit_flag_write(cx, block, F_SC, shr, sbn)
    emit_flag_poll(cx, block, F_SC, shr, sbn)
    tok_scb = cx.tok('g')

    def _(g):
        cx.wait(g, 'g', tok_scb)
        with g.If_eq(sbn['g_rreg'], 0):
            cx.inc('dto', g.dma_start(out=sb['sc2'][0:1, :], in_=shr['scsh'][1, :, :]), 16)
        with g.Else():
            g.dma_start(out=sb['sc2'][0:1, :], in_=shr['scsh'][0, :, :]).then_inc(
                cx.sem('dto'), 16)
    block.gpsimd(_)
    tok_sc2 = cx.tok('dto')

    def _(v):
        cx.wait(v, 'v', tok_sc2)
        cx.inc('v', v.tensor_max(sb['scmax'][0:1, 0:4], sb['scmax'][0:1, 0:4], sb['sc2'][0:1, 0:4]))
        cx.wait(v, 'v', cx.tok('v'))
        cx.inc('v', v.reciprocal(sb['rscale'][0:1, 0:1], sb['scmax'][0:1, 0:1]))
    block.vector(_)
    tok_rsc0 = cx.tok('v')

    def _(t):
        cx.wait(t, 't', tok_rsc0)
        cx.inc('pe', t.matmul(ps['B'][:, 0:1], sb['ones128'][:, :], sb['rscale'][0:1, 0:1],
                              start=True, stop=True))
    block.tensor(_)
    tok_rsb = cx.tok('pe')

    def _(v):
        cx.wait(v, 'v', tok_rsb)
        cx.inc('v', v.tensor_copy(sb['rscale128'][:, 0:1], ps['B'][:, 0:1]))
    block.vector(_)
    tok_rsc = cx.tok('v')
    if B_STOP == 'b1':
        return cx.tok('dh3')

    def _(y):
        cx.wait(y, 'y', cx.tok('pe'), cx.tok('v'))
        for dc in range(DT):
            cx.inc('dtp', y.dma_start(out=sb['n1fm'][:, dc, 0:S],
                                      in_=shr['x0sh'][0:S, dc * 128:(dc + 1) * 128],
                                      transpose=True), 16)
    block.sync(_)
    tok_x0 = cx.tok('dtp')

    def _(t):
        cx.wait(t, 't', tok_x0, cx.tok('dw2'))
        cx.inc('pe', t.matmul(ps['B'][0:64, 0:NH2 * 33],
                              sb['ones128'][0:1, 0:64], sb['convw1'][0:1, :],
                              start=True, stop=True))
    block.tensor(_)
    tok_cb = cx.tok('pe')

    def _(v):
        cx.wait(v, 'v', tok_cb)
        cx.inc('v', v.tensor_copy(sb['convb'][:, :], ps['B'][0:64, 0:NH2 * 33]))
    block.vector(_)
    tok_cbv = cx.tok('v')

    def _(g):
        cx.wait(g, 'g', cx.tok('dxz'))
    block.gpsimd(_)

    for hh in range(NH2):
        emit_nystrom_head(cx, block, sb, ps, shr, sbn, dram, S, hh, tok_x0, tok_rsc)

    def _(g):
        cx.wait(g, 'g', cx.tok('dxg'))
        with g.If_eq(sbn['g_rreg'], 0):
            cx.inc('dsh', g.dma_start(out=shr['xgsh'][0, :, :], in_=dram['xgp'][:, :]), 16)
        with g.Else():
            g.dma_start(out=shr['xgsh'][1, :, :], in_=dram['xgp'][:, :]).then_inc(
                cx.sem('dsh'), 16)
        g.wait_ge(cx.sem('dsh'), cx.cnt['dsh'])
    block.gpsimd(_)
    emit_flag_write(cx, block, F_XG, shr, sbn)
    emit_flag_poll(cx, block, F_XG, shr, sbn)
    tok_xgb = cx.tok('g')
    if B_STOP == 'b2':
        return cx.tok('dh3')

    def _(y):
        cx.wait(y, 'y', cx.tok('g'))
        cx.inc('dw2', y.dma_start(out=sb['outb1'][:, :], in_=ins['out_b'][:, :]), 16)
    block.sync(_)

    def _(t):
        cx.wait(t, 't', tok_xgb, cx.tok('dw2'), cx.tok('pe'), cx.tok('v'), cx.tok('s'))
        t.matmul(ps['PV'][:, 0:512], sb['ones128'][0:1, :], sb['outb1'][0:1, 0:512],
                 start=True, stop=True)
        last = t.matmul(ps['PV'][:, 512:1024], sb['ones128'][0:1, :],
                        sb['outb1'][0:1, 512:1024], start=True, stop=True)
        cx.inc('pe', last)
    block.tensor(_)
    tok_obm = cx.tok('pe')

    def _(v):
        cx.wait(v, 'v', tok_obm)
        cx.inc('v', v.tensor_copy(sb['outbb'][:, :], ps['PV'][:, 0:1024]))
    block.vector(_)
    tok_ob = cx.tok('v')

    for tt in range(ST):
        def _(y, tt=tt):
            cx.wait(y, 'y', tok_xgb, cx.tok('b_e0'))
            cx.inc('dxa', y.dma_start(out=sb['lnin'][:, :], in_=shr['xgsh'][0, tt * 128:(tt + 1) * 128, :]), 16)
            cx.inc('dxa', y.dma_start(out=sb['h0t'][:, :], in_=shr['xgsh'][1, tt * 128:(tt + 1) * 128, :]), 16)
        block.sync(_)
        tok_xg2 = cx.tok('dxa')

        def _(g, tt=tt):
            cx.wait(g, 'g', tok_xgb, cx.tok('b_e0'), tok_obm)
            with g.If_eq(sbn['g_rreg'], 0):
                cx.inc('dxb', g.dma_start(out=sb['n2f32'][:, :],
                                          in_=dram['n3f'][tt * 128:(tt + 1) * 128, :]), 16)
            with g.Else():
                g.dma_start(out=sb['n2f32'][:, :],
                            in_=shr['x0sh'][tt * 128:(tt + 1) * 128, :]).then_inc(
                    cx.sem('dxb'), 16)
        block.gpsimd(_)
        tok_x0t = cx.tok('dxb')

        def _(v, tt=tt):
            cx.wait(v, 'v', tok_xg2, tok_x0t, tok_ob, cx.tok('dxgs'))
            cx.inc('v', v.tensor_tensor(sb['h1t'][:, :], sb['lnin'][:, :], sb['h0t'][:, :],
                                        ALU.add))
            cx.wait(v, 'v', cx.tok('v'))
            cx.inc('v', v.tensor_tensor(sb['h1t'][:, :], sb['h1t'][:, :], sb['outbb'][:, :],
                                        ALU.add))
            cx.wait(v, 'v', cx.tok('v'))
            cx.inc('v', v.tensor_tensor(sb['h0t'][:, :], sb['h1t'][:, :], sb['n2f32'][:, :],
                                        ALU.add))
            cx.wait(v, 'v', cx.tok('v'))
            cx.inc('v', v.tensor_copy(sb['lnout'][:, :], sb['h0t'][:, :]))
            cx.wait(v, 'v', cx.tok('v'))
            cx.inc('b_e0', v.tensor_tensor(sb['enclo'][:, :], sb['h0t'][:, :],
                                           sb['lnout'][:, :], ALU.subtract))
        block.vector(_)
        tok_e0 = cx.tok('b_e0')

        def _(y, tt=tt):
            cx.wait(y, 'y', tok_e0)
            cx.inc('dxgs', y.dma_start(out=dram['xg0'][tt * 128:(tt + 1) * 128, :],
                                       in_=sb['h1t'][:, :]), 16)
            cx.inc('dxgs', y.dma_start(out=dram['enc0'][tt * 128:(tt + 1) * 128, :],
                                       in_=sb['lnout'][:, :]), 16)
            cx.inc('dxgs', y.dma_start(out=dram['enc0l'][tt * 128:(tt + 1) * 128, :],
                                       in_=sb['enclo'][:, :]), 16)
        block.sync(_)
    tok_enc = cx.tok('dxgs')

    def _(y):
        cx.wait(y, 'y', cx.tok('g'))
        cx.inc('dw2', y.dma_start(out=sb['wqf'][:, :, :],
                                  in_=ins['wq'].rearrange("(c p) o -> p c o", p=128)), 16)
    block.sync(_)
    tok_wqf = cx.tok('dw2')

    def _(v):
        cx.wait(v, 'v', tok_wqf)
        cx.inc('v', v.tensor_copy(sb['wqh'][:, :, :], sb['wqf'][:, :, :]))
        cx.wait(v, 'v', cx.tok('v'))
        cx.inc('v', v.tensor_tensor(sb['wql'][:, :, :], sb['wqf'][:, :, :],
                                    sb['wqh'][:, :, :], ALU.subtract))
    block.vector(_)
    tok_wqs = cx.tok('v')

    def _(y):
        cx.wait(y, 'y', tok_wqs)
        cx.inc('dw2', y.dma_start(out=sb['wqf'][:, :, :],
                                  in_=ins['wk'].rearrange("(c p) o -> p c o", p=128)), 16)
    block.sync(_)
    tok_wkf = cx.tok('dw2')

    def _(v):
        cx.wait(v, 'v', tok_wkf)
        cx.inc('v', v.tensor_copy(sb['wkh'][:, :, :], sb['wqf'][:, :, :]))
        cx.wait(v, 'v', cx.tok('v'))
        cx.inc('v', v.tensor_tensor(sb['wkl'][:, :, :], sb['wqf'][:, :, :],
                                    sb['wkh'][:, :, :], ALU.subtract))
    block.vector(_)
    tok_wsp = cx.tok('v')

    def _(y):
        cx.wait(y, 'y', tok_enc, cx.tok('pe'), cx.tok('v'))
        for dc in range(DT):
            cx.inc('dtp', y.dma_start(out=sb['n1fm'][:, dc, 0:S],
                                      in_=dram['enc0'][0:S, dc * 128:(dc + 1) * 128],
                                      transpose=True), 16)
            cx.inc('dtp', y.dma_start(out=sb['elfm'][:, dc, 0:S],
                                      in_=dram['enc0l'][0:S, dc * 128:(dc + 1) * 128],
                                      transpose=True), 16)
    block.sync(_)
    tok_e0fm = cx.tok('dtp')

    for which in ('q', 'k'):
        wh, wl = ('wqh', 'wql') if which == 'q' else ('wkh', 'wkl')
        dh_, dl_ = ('q0h', 'q0l') if which == 'q' else ('k0h', 'k0l')
        for oc in range(2):
            def _(t, wh=wh, wl=wl, oc=oc):
                cx.wait(t, 't', tok_e0fm, tok_wsp, cx.tok('b_qk'))
                last = None
                chains = [(wh, 'n1fm'), (wl, 'n1fm'), (wh, 'elfm'), (wl, 'elfm')]
                for ci, (wn, en) in enumerate(chains):
                    for di in range(DT):
                        t.ldweights(sb[wn][:, di, oc * 128:(oc + 1) * 128])
                        for mv in range(S // 512):
                            last = t.matmul(ps['A'][:, mv * 512:(mv + 1) * 512],
                                            sb[wn][:, di, oc * 128:(oc + 1) * 128],
                                            sb[en][:, di, mv * 512:(mv + 1) * 512],
                                            start=(ci == 0 and di == 0),
                                            stop=(ci == 3 and di == DT - 1))
                cx.inc('pe', last)
            block.tensor(_)
            tok_qk = cx.tok('pe')

            def _(s2, oc=oc):
                cx.wait(s2, 's', tok_qk)
                cx.inc('s', s2.activation(out=sb['q0f'][:, 0:S], in_=ps['A'][:, 0:S],
                                          func=AF.Identity))
            block.scalar(_)
            tok_qe = cx.tok('s')

            def _(v, dh_=dh_, dl_=dl_, oc=oc):
                cx.wait(v, 'v', tok_qe)
                cx.inc('v', v.tensor_copy(sb[dh_][:, oc, 0:S], sb['q0f'][:, 0:S]))
                cx.wait(v, 'v', cx.tok('v'))
                cx.inc('b_qk', v.tensor_tensor(sb[dl_][:, oc, 0:S], sb['q0f'][:, 0:S],
                                               sb[dh_][:, oc, 0:S], ALU.subtract))
            block.vector(_)
    tok_q0 = cx.tok('b_qk')

    for it in range(ST):
        def _(t, it=it):
            cx.wait(t, 't', tok_q0, cx.tok('b_sev'))
            last = None
            chains = [('q0h', 'k0h'), ('q0h', 'k0l'), ('q0l', 'k0h'), ('q0l', 'k0l')]
            for ci, (qn, kn) in enumerate(chains):
                for oc in range(2):
                    t.ldweights(sb[qn][:, oc, it * 128:(it + 1) * 128])
                    for mv in range(S // 512):
                        last = t.matmul(ps['A'][:, mv * 512:(mv + 1) * 512],
                                        sb[qn][:, oc, it * 128:(it + 1) * 128],
                                        sb[kn][:, oc, mv * 512:(mv + 1) * 512],
                                        start=(ci == 0 and oc == 0),
                                        stop=(ci == 3 and oc == 1))
                cx.inc('pe', last)
        block.tensor(_)
        tok_sm2 = cx.tok('pe')

        def _(s2, it=it):
            cx.wait(s2, 's', tok_sm2)
            cx.inc('b_sev', s2.activation(out=sb['srow2'][:, 0:S], in_=ps['A'][:, 0:S],
                                          func=AF.Identity, scale=1.0 / 16.0))
        block.scalar(_)
        tok_sev = cx.tok('b_sev')

        def _(y, it=it):
            cx.wait(y, 'y', tok_sev)
            cx.inc('dS', y.dma_start(out=dram['Sd'][it * 128:(it + 1) * 128, :],
                                     in_=sb['srow2'][:, 0:S]), 16)
        block.sync(_)
    tok_Sd = cx.tok('dS')
    if B_STOP == 'b3':
        return cx.tok('dh3')

    NCH = E2 // 128
    def _(g):
        cx.wait(g, 'g', tok_Sd)
        cx.inc('dxb', g.dma_start(out=sb['erow'][:, :], in_=dram['er'][:, :]), 16)
        cx.inc('dxb', g.dma_start(out=sb['ecol'][:, :], in_=dram['ec'][:, :]), 16)
        cx.inc('dxb', g.dma_start(out=sb['eval'][:, :], in_=dram['ev'][:, :]), 16)
    block.gpsimd(_)
    tok_eg = cx.tok('dxb')

    def _(v):
        cx.wait(v, 'v', tok_eg)
        cx.inc('v', v.tensor_scalar(sb['elo_i'][:, :], sb['erow'][:, :], 127, None,
                                    ALU.bitwise_and))
        cx.inc('v', v.tensor_scalar(sb['ehi_i'][:, :], sb['erow'][:, :], 7, None,
                                    ALU.arith_shift_right))
        cx.inc('v', v.tensor_scalar(sb['eidx'][:, :], sb['erow'][:, :], S - 1, None,
                                    ALU.min))
        cx.inc('v', v.tensor_scalar(sb['ecol'][:, :], sb['ecol'][:, :], S - 1, None,
                                    ALU.min))
        cx.wait(v, 'v', cx.tok('v'))
        cx.inc('v', v.tensor_scalar(sb['eidx'][:, :], sb['eidx'][:, :], S, None, ALU.mult))
        cx.inc('v', v.tensor_copy(sb['elo'][:, :], sb['elo_i'][:, :]))
        cx.inc('v', v.tensor_copy(sb['ehi'][:, :], sb['ehi_i'][:, :]))
        cx.wait(v, 'v', cx.tok('v'))
        cx.inc('v', v.tensor_tensor(sb['eidx'][:, :], sb['eidx'][:, :], sb['ecol'][:, :],
                                    ALU.add))
    block.vector(_)
    tok_ei = cx.tok('v')

    for ch in range(NCH):
        def _(g, ch=ch):
            cx.wait(g, 'g', tok_ei, cx.tok('b_ctr'))
            cx.inc('dgx', g.indirect_dma_start(
                sb['ctrb'][:, ch % 64:ch % 64 + 1], None,
                dram['Sd'].rearrange("a b -> (a b)").rearrange("(a b) -> a b", b=1),
                IndirectOffsetOnAxis(sb['eidx'][:, ch:ch + 1], 0),
                bounds_check=sbn['bnd_S2'], oob_is_err=False), 16)
            if ch % 16 == 15:
                g.wait_ge(cx.sem('dgx'), cx.cnt['dgx'])
        block.gpsimd(_)
        tok_ct = cx.tok('dgx')

        def _(v, ch=ch):
            cx.wait(v, 'v', tok_ct, cx.tok('b_oh'))
            c6 = ch % 64
            c2 = ch % 2
            cx.inc('v', v.tensor_tensor(sb['cvfb'][:, c6:c6 + 1], sb['ctrb'][:, c6:c6 + 1],
                                        sb['eval'][:, ch:ch + 1], ALU.mult))
            cx.wait(v, 'v', cx.tok('v'))
            cx.inc('v', v.tensor_scalar(sb['ehh'][:, c2 * ST:c2 * ST + ST],
                                        sb['rtrow'][:, 0:ST], sb['ehi'][:, ch:ch + 1],
                                        None, ALU.is_equal))
            cx.inc('b_ctr', v.tensor_scalar(sb['eLb'][:, c2 * 128:c2 * 128 + 128],
                                            sb['kkrow'][:, :], sb['elo'][:, ch:ch + 1],
                                            sb['cvfb'][:, c6:c6 + 1], ALU.is_equal, ALU.mult))
        block.vector(_)
        tok_oh = cx.tok('b_ctr')

        def _(t, ch=ch):
            cx.wait(t, 't', tok_oh)
            c2 = ch % 2
            last = t.matmul(ps['B'][:, 0:ST], sb['eLb'][:, c2 * 128:c2 * 128 + 128],
                            sb['ehh'][:, c2 * ST:c2 * ST + ST],
                            start=(ch == 0), stop=(ch == NCH - 1))
            cx.inc('b_oh', last)
        block.tensor(_)
    tok_ar = cx.tok('b_oh')

    def _(v):
        cx.wait(v, 'v', tok_ar)
        cx.inc('v', v.tensor_copy(sb['alphat'][:, :], ps['B'][:, 0:ST]))
    block.vector(_)
    tok_arp = cx.tok('v')

    def _(g):
        cx.wait(g, 'g', tok_arp)
        with g.If_eq(sbn['g_rreg'], 0):
            cx.inc('dsh', g.dma_start(out=shr['arsh'][0, :, :], in_=sb['alphat'][:, :]), 16)
        with g.Else():
            g.dma_start(out=shr['arsh'][1, :, :], in_=sb['alphat'][:, :]).then_inc(
                cx.sem('dsh'), 16)
        g.wait_ge(cx.sem('dsh'), cx.cnt['dsh'])
    block.gpsimd(_)
    emit_flag_write(cx, block, F_AR, shr, sbn)
    emit_flag_poll(cx, block, F_AR, shr, sbn)
    tok_arb = cx.tok('g')

    def _(g):
        cx.wait(g, 'g', tok_arb)
        with g.If_eq(sbn['g_rreg'], 0):
            cx.inc('dxb', g.dma_start(out=sb['arp2b'][:, :], in_=shr['arsh'][1, :, :]), 16)
        with g.Else():
            g.dma_start(out=sb['arp2b'][:, :], in_=shr['arsh'][0, :, :]).then_inc(
                cx.sem('dxb'), 16)
    block.gpsimd(_)
    tok_ar2 = cx.tok('dxb')
    if B_STOP == 'b3b':
        return cx.tok('dh3')

    def _(v):
        cx.wait(v, 'v', tok_ar2)
        cx.inc('v', v.tensor_tensor(sb['alphat'][:, :], sb['alphat'][:, :],
                                    sb['arp2b'][:, :], ALU.add))
        cx.wait(v, 'v', cx.tok('v'))
        cx.inc('v', v.tensor_reduce(sb['sc128'][:, 0:1], sb['alphat'][:, :], axis=AX.X,
                                    op=ALU.max, negate=True))
        cx.wait(v, 'v', cx.tok('v'))
        cx.inc('v', v.tensor_copy(sb['mbneg'][:, 0:1], sb['sc128'][:, 0:1]))
    block.vector(_)
    tok_am = cx.tok('v')

    def _(t):
        cx.wait(t, 't', tok_am, cx.tok('pe'), cx.tok('b_psT'))
        cx.inc('pe', t.transpose(ps['T'][0:1, :], sb['mbneg'][:, 0:1], sb['ident'][:, :]))
    block.tensor(_)
    tok_amt = cx.tok('pe')

    def _(v):
        cx.wait(v, 'v', tok_amt)
        cx.inc('v', v.tensor_copy(sb['mrow'][0:1, :], ps['T'][0:1, :]))
        cx.wait(v, 'v', cx.tok('v'))
        cx.inc('v', v.tensor_reduce(sb['sc1'][0:1, 0:1], sb['mrow'][0:1, :], axis=AX.X,
                                    op=ALU.min))
    block.vector(_)
    tok_c0 = cx.tok('v')

    def _(t):
        cx.wait(t, 't', tok_c0)
        cx.inc('pe', t.matmul(ps['B'][:, 256:257], sb['ones128'][0:1, :], sb['sc1'][0:1, 0:1],
                              start=True, stop=True))
    block.tensor(_)
    tok_cb2 = cx.tok('pe')

    def _(v):
        cx.wait(v, 'v', tok_cb2)
        cx.inc('v', v.tensor_copy(sb['sc128'][:, 3:4], ps['B'][:, 256:257]))
    block.vector(_)
    tok_cbb = cx.tok('v')

    def _(s2):
        cx.wait(s2, 's', tok_cbb)
        cx.inc('s', s2.activation(out=sb['alphat'][:, :], in_=sb['alphat'][:, :], func=AF.Exp,
                                  bias=sb['sc128'][:, 3:4], accum_out=sb['asum'][:, 0:1]))
    block.scalar(_)
    tok_ae = cx.tok('s')

    def _(t):
        cx.wait(t, 't', tok_ae)
        cx.inc('pe', t.matmul(ps['B'][0:1, 257:258], sb['asum'][:, 0:1],
                              sb['onescolF'][:, 0:1], start=True, stop=True))
    block.tensor(_)
    tok_ts = cx.tok('pe')

    def _(v):
        cx.wait(v, 'v', tok_ts)
        cx.inc('v', v.reciprocal(sb['sc1'][0:1, 1:2], ps['B'][0:1, 257:258]))
    block.vector(_)
    tok_rc = cx.tok('v')

    def _(t):
        cx.wait(t, 't', tok_rc)
        cx.inc('pe', t.matmul(ps['B'][:, 258:259], sb['ones128'][0:1, :], sb['sc1'][0:1, 1:2],
                              start=True, stop=True))
    block.tensor(_)
    tok_rb2 = cx.tok('pe')

    def _(v):
        cx.wait(v, 'v', tok_rb2)
        cx.inc('v', v.tensor_copy(sb['sc128'][:, 2:3], ps['B'][:, 258:259]))
        cx.wait(v, 'v', cx.tok('v'))
        cx.inc('v', v.tensor_scalar(sb['alphat'][:, :], sb['alphat'][:, :],
                                    sb['sc128'][:, 2:3], None, ALU.mult))
    block.vector(_)
    tok_alt = cx.tok('v')
    if B_STOP == 'b4':
        return cx.tok('dh3')

    nbc5 = {'b5a': 1, 'b5b': 4, 'b5c': 8}.get(B_STOP, NBC)
    bufs = [dict(lnin=sb['lnin'], h0t=sb['h0t'], h1t=sb['h1t'], n2=sb['n2f32'],
                 xow=sb['xow']),
            dict(lnin=sb['lnin2'], h0t=sb['h0t2'], h1t=sb['h1t2'], n2=sb['n2f32c'],
                 xow=sb['xow2'])]
    for b in range(nbc5):
        base = b * S
        for tt in range(ST):
            trow = base + tt * 128
            bf = bufs[(b * ST + tt) % 2]
            kfree = ('dh3', max(0, cx.cnt.get('dh3', 0) - 16))
            vfree = ('b_xo', max(0, cx.cnt.get('b_xo', 0) - 1))

            def _(y, tt=tt, trow=trow, bf=bf, kfree=kfree, vfree=vfree):
                cx.wait(y, 'y', tok_alt, vfree, kfree)
                cx.inc('dxo', y.dma_start(out=bf['lnin'][:, :],
                                          in_=dram['val'][trow:trow + 128, :]), 16)
                cx.inc('dxo', y.dma_start(out=bf['h0t'][:, :],
                                          in_=dram['n3f'][trow:trow + 128, :]), 16)
                cx.inc('dxo', y.dma_start(out=bf['h1t'][:, :],
                                          in_=dram['xg0'][tt * 128:(tt + 1) * 128, :]), 16)
                cx.inc('dxo', y.dma_start(out=bf['n2'][:, :],
                                          in_=dram['hin'][trow:trow + 128, :]), 16)
            block.sync(_)
            tok_lx = cx.tok('dxo')

            def _(v, tt=tt, bf=bf, vfree=vfree):
                cx.wait(v, 'v', tok_lx, vfree, cx.tok('s'))
                cx.inc('v', v.tensor_tensor(bf['h1t'][:, :], bf['h1t'][:, :], bf['h0t'][:, :],
                                            ALU.add))
                cx.inc('v', v.tensor_scalar(bf['lnin'][:, :], bf['lnin'][:, :],
                                            sb['alphat'][:, tt:tt + 1], None, ALU.mult))
            block.vector(_)
            tok_xl2 = cx.tok('v')

            def _(s2, tt=tt, bf=bf):
                cx.wait(s2, 's', tok_xl2)
                cx.inc('s', s2.activation(out=bf['xow'][:, :], in_=bf['lnin'][:, :],
                                          func=AF.Sigmoid, scale=-1.0))
            block.scalar(_)
            tok_sw = cx.tok('s')

            def _(v, tt=tt, trow=trow, bf=bf):
                cx.wait(v, 'v', tok_sw)
                cx.inc('v', v.tensor_tensor(bf['xow'][:, :], bf['xow'][:, :], bf['xow'][:, :],
                                            ALU.mult))
                cx.wait(v, 'v', cx.tok('v'))
                cx.inc('v', v.tensor_tensor(bf['lnin'][:, :], bf['lnin'][:, :], bf['h1t'][:, :],
                                            ALU.subtract))
                cx.wait(v, 'v', cx.tok('v'))
                cx.inc('v', v.tensor_tensor(bf['lnin'][:, :], bf['lnin'][:, :], bf['xow'][:, :],
                                            ALU.mult))
                cx.wait(v, 'v', cx.tok('v'))
                cx.inc('v', v.scalar_tensor_tensor(bf['h1t'][:, :], bf['h1t'][:, :], 2.0,
                                                   bf['n2'][:, :], ALU.mult, ALU.add))
                cx.wait(v, 'v', cx.tok('v'))
                cx.inc('b_xo', v.scalar_tensor_tensor(bf['h1t'][:, :], bf['lnin'][:, :], 2.0,
                                                      bf['h1t'][:, :], ALU.mult, ALU.add))
            block.vector(_)
            tok_h3 = cx.tok('b_xo')

            def _(y, trow=trow, bf=bf):
                cx.wait(y, 'y', tok_h3)
                cx.inc('dh3', y.dma_start(out=dram['hout'][trow:trow + 128, :],
                                          in_=bf['h1t'][:, :]), 16)
            block.sync(_)
    return cx.tok('dh3')


def emit_nystrom_head(cx, block, sb, ps, shr, sbn, dram, S, hh, tok_x0, tok_rsc):
    nc = cx.nc
    ST = S // 128
    nlm = S // LGRP
    nl2 = (nlm + 127) // 128
    NH2 = NHEAD // 2
    for which in range(3):
        def _(t, hh=hh, which=which):
            cx.wait(t, 't', tok_x0, cx.tok('b_nyev'))
            last = None
            for di in range(DT):
                t.ldweights(sb['qkvh'][:, hh, di, which * 64:(which + 1) * 64])
                for mv in range(S // 512):
                    last = t.matmul(ps['A'][0:64, mv * 512:(mv + 1) * 512],
                                    sb['qkvh'][:, hh, di, which * 64:(which + 1) * 64],
                                    sb['n1fm'][:, di, mv * 512:(mv + 1) * 512],
                                    start=(di == 0), stop=(di == DT - 1))
            cx.inc('pe', last)
        block.tensor(_)
        tok_nmm = cx.tok('pe')

        def _(v, which=which):
            cx.wait(v, 'v', tok_nmm)
            sc = (DH ** -0.5) if which == 0 else 1.0
            cx.inc('b_nyev', v.tensor_scalar(sb['nqkvb'][0:64, which, 0:S],
                                             ps['A'][0:64, 0:S], sc, None, ALU.mult))
        block.vector(_)
    tok_nqkv = cx.tok('b_nyev')

    def _(v, hh=hh):
        cx.wait(v, 'v', tok_nqkv)
        for which in range(2):
            cx.inc('v', v.tensor_reduce(
                sb['n2f32'][0:64, which * nlm:(which + 1) * nlm],
                sb['nqkvb'][0:64, which, 0:S].rearrange("p (l g) -> p l g", g=LGRP),
                axis=AX.X, op=ALU.add))
        cx.wait(v, 'v', cx.tok('v'))
        for which in range(2):
            cx.inc('v', v.tensor_scalar(sb['nlmkb'][0:64, which, 0:nlm],
                                        sb['n2f32'][0:64, which * nlm:(which + 1) * nlm],
                                        1.0 / LGRP, None, ALU.mult))
    block.vector(_)
    tok_nlm = cx.tok('v')

    for tc in range(ST):
        def _(t, tc=tc):
            cx.wait(t, 't', tok_nlm, cx.tok('b_a1'))
            cx.inc('pe', t.matmul(ps['B'][:, 0:nlm], sb['nqkvb'][0:64, 0, tc * 128:(tc + 1) * 128],
                                  sb['nlmkb'][0:64, 1, 0:nlm], start=True, stop=True))
        block.tensor(_)
        tok_a1m = cx.tok('pe')
        tok_a1sm = emit_softmax_rows(cx, block, sb, ps['B'], nlm, 'b_a1', tc)
    tok_a1 = cx.tok('b_a1')

    for lc in range(nl2):
        lw = min(128, nlm - lc * 128)
        def _(t, lc=lc, lw=lw):
            cx.wait(t, 't', tok_nlm, cx.tok('b_a3'))
            last = None
            for mv in range(S // 512):
                last = t.matmul(ps['A'][0:lw, mv * 512:(mv + 1) * 512],
                                sb['nlmkb'][0:64, 0, lc * 128:lc * 128 + lw],
                                sb['nqkvb'][0:64, 1, mv * 512:(mv + 1) * 512],
                                start=True, stop=True)
            cx.inc('pe', last)
        block.tensor(_)
        tok_a3m = cx.tok('pe')
        emit_softmax_rows_wide(cx, block, sb, ps['A'], S, 'b_a3', lc, lw)
    tok_a3 = cx.tok('b_a3')

    for lc in range(nl2):
        lw = min(128, nlm - lc * 128)
        def _(t, lc=lc, lw=lw):
            cx.wait(t, 't', tok_nlm, cx.tok('b_a2x'), cx.tok('b_a1'))
            cx.inc('pe', t.matmul(ps['B'][0:lw, 0:nlm],
                                  sb['nlmkb'][0:64, 0, lc * 128:lc * 128 + lw],
                                  sb['nlmkb'][0:64, 1, 0:nlm], start=True, stop=True))
        block.tensor(_)
        emit_softmax_rows2(cx, block, sb, ps['B'], nlm, 'b_a2x', lc, lw)
    tok_a2x = cx.tok('b_a2x')

    emit_pinv(cx, block, sb, ps, S, tok_a2x, tok_rsc)
    tok_z = cx.tok('b_pz')

    for lc in range(nl2):
        for tc2 in range(ST):
            def _(t, lc=lc, tc2=tc2):
                cx.wait(t, 't', tok_a3, cx.tok('b_psT'))
                cx.inc('pe', t.transpose(ps['T'][:, :],
                                         sb['a3b'][0:128, lc, tc2 * 128:(tc2 + 1) * 128],
                                         sb['ident'][:, :]))
            block.tensor(_)
            tok_t1 = cx.tok('pe')

            def _(v, lc=lc, tc2=tc2):
                cx.wait(v, 'v', tok_t1)
                cx.inc('b_psT', v.tensor_copy(sb['a3T'][:, tc2, lc * 128:(lc + 1) * 128],
                                              ps['T'][:, :]))
            block.vector(_)
    for tc2 in range(ST):
        def _(t, tc2=tc2):
            cx.wait(t, 't', tok_nqkv, cx.tok('b_psT'))
            cx.inc('pe', t.transpose(ps['T'][:, 0:64], sb['nqkvb'][0:64, 2, tc2 * 128:(tc2 + 1) * 128],
                                     sb['ident'][0:64, 0:64]))
        block.tensor(_)
        tok_t2 = cx.tok('pe')

        def _(v, tc2=tc2):
            cx.wait(v, 'v', tok_t2)
            cx.inc('b_psT', v.tensor_copy(sb['vT'][:, tc2, 0:64], ps['T'][:, 0:64]))
        block.vector(_)
    tok_vT = cx.tok('b_psT')
    tok_a3T = cx.tok('b_psT')

    for lc in range(nl2):
        lw = min(128, nlm - lc * 128)
        def _(t, lc=lc, lw=lw):
            cx.wait(t, 't', tok_a3T, tok_vT, cx.tok('b_a3v'))
            last = None
            for tc2 in range(ST):
                t.ldweights(sb['a3T'][:, tc2, lc * 128:lc * 128 + lw])
                last = t.matmul(ps['B'][0:lw, 0:64], sb['a3T'][:, tc2, lc * 128:lc * 128 + lw],
                                sb['vT'][:, tc2, 0:64], start=(tc2 == 0), stop=(tc2 == ST - 1))
            cx.inc('pe', last)
        block.tensor(_)
        tok_a3vm = cx.tok('pe')

        def _(v, lc=lc, lw=lw):
            cx.wait(v, 'v', tok_a3vm)
            cx.inc('b_a3v', v.tensor_copy(sb['a3v'][0:lw, lc, 0:64], ps['B'][0:lw, 0:64]))
        block.vector(_)
    tok_a3v = cx.tok('b_a3v')

    for lc in range(nl2):
        lw = min(128, nlm - lc * 128)
        def _(t, lc=lc, lw=lw):
            cx.wait(t, 't', tok_z, tok_a3v, cx.tok('b_t6'))
            last = None
            for kc in range(nl2):
                kw = min(128, nlm - kc * 128)
                t.ldweights(sb['pzT'][0:kw, kc, lc * 128:lc * 128 + lw])
                last = t.matmul(ps['B'][0:lw, 0:64], sb['pzT'][0:kw, kc, lc * 128:lc * 128 + lw],
                                sb['a3v'][0:kw, kc, 0:64], start=(kc == 0), stop=(kc == nl2 - 1))
            cx.inc('pe', last)
        block.tensor(_)
        tok_t6m = cx.tok('pe')

        def _(v, lc=lc, lw=lw):
            cx.wait(v, 'v', tok_t6m)
            cx.inc('b_t6', v.tensor_copy(sb['a3v'][0:lw, nl2 + lc, 0:64], ps['B'][0:lw, 0:64]))
        block.vector(_)
    tok_t6 = cx.tok('b_t6')

    def _(v, hh=hh):
        cx.wait(v, 'v', tok_vT, cx.tok('b_t6'))
        cx.inc('v', v.memset(sb['cacc'][0:64, 0:S], 0.0))
        for tp in range(CONVK):
            off = tp - CONVK // 2
            s0 = max(0, -off)
            s1 = min(S, S - off)
            cx.wait(v, 'v', cx.tok('v'))
            cx.inc('v', v.scalar_tensor_tensor(
                sb['cacc'][0:64, s0:s1],
                sb['nqkvb'][0:64, 2, s0 + off:s1 + off],
                sb['convb'][0:64, hh * CONVK + tp:hh * CONVK + tp + 1],
                sb['cacc'][0:64, s0:s1], ALU.mult, ALU.add))
    block.vector(_)
    tok_cv = cx.tok('v')

    for lc in range(nl2):
        for tc in range(ST):
            def _(t, lc=lc, tc=tc):
                cx.wait(t, 't', tok_a1, cx.tok('b_psT'))
                lw = min(128, nlm - lc * 128)
                cx.inc('pe', t.transpose(ps['T'][0:lw, :], sb['a1b'][:, tc, lc * 128:lc * 128 + lw],
                                         sb['ident'][:, :]))
            block.tensor(_)
            tok_t3 = cx.tok('pe')

            def _(v, lc=lc, tc=tc):
                cx.wait(v, 'v', tok_t3)
                lw = min(128, nlm - lc * 128)
                cx.inc('b_psT', v.tensor_copy(sb['a1T'][0:lw, lc, tc * 128:(tc + 1) * 128],
                                              ps['T'][0:lw, :]))
            block.vector(_)
    tok_a1T = cx.tok('b_psT')

    def _(t):
        cx.wait(t, 't', tok_a1T, tok_t6, cx.tok('b_of'))
        last = None
        for lc in range(nl2):
            lw = min(128, nlm - lc * 128)
            t.ldweights(sb['a3v'][0:lw, nl2 + lc, 0:64])
            for mv in range(S // 512):
                last = t.matmul(ps['A'][0:64, mv * 512:(mv + 1) * 512],
                                sb['a3v'][0:lw, nl2 + lc, 0:64],
                                sb['a1T'][0:lw, lc, mv * 512:(mv + 1) * 512],
                                start=(lc == 0), stop=(lc == nl2 - 1))
        cx.inc('pe', last)
    block.tensor(_)
    tok_of = cx.tok('pe')

    def _(v):
        cx.wait(v, 'v', tok_of, tok_cv)
        cx.inc('b_of', v.tensor_tensor(sb['outh'][0:64, 0:S], ps['A'][0:64, 0:S],
                                       sb['cacc'][0:64, 0:S], ALU.add))
    block.vector(_)
    tok_oh2 = cx.tok('b_of')

    for tc in range(ST):
        def _(t, tc=tc, hh=hh):
            cx.wait(t, 't', tok_oh2, cx.tok('b_xgp'))
            last = None
            t.ldweights(sb['outh'][0:64, tc * 128:(tc + 1) * 128])
            for mv in range(2):
                last = t.matmul(ps['PV'][:, mv * 512:(mv + 1) * 512],
                                sb['outh'][0:64, tc * 128:(tc + 1) * 128],
                                sb['outwh'][0:64, hh, mv * 512:(mv + 1) * 512],
                                start=True, stop=True)
            cx.inc('pe', last)
        block.tensor(_)
        tok_xgm = cx.tok('pe')

        def _(v, tc=tc):
            cx.wait(v, 'v', tok_xgm)
            cx.inc('b_xgp', v.tensor_copy(sb['h1t'][:, :], ps['PV'][:, 0:1024]))
        block.vector(_)
        tok_xge = cx.tok('b_xgp')

        def _(g, tc=tc):
            cx.wait(g, 'g', tok_xge)
            cx.inc('dxg', g.dma_start(out=dram['xgp'][tc * 128:(tc + 1) * 128, :],
                                      in_=sb['h1t'][:, :], accum_op=ALU.add), 16)
            g.wait_ge(cx.sem('dxg'), cx.cnt['dxg'])
            cx.inc('b_xgp2', g.nop())
        block.gpsimd(_)

        def _(v, tc=tc):
            cx.wait(v, 'v', cx.tok('b_xgp2'))
        block.vector(_)


def emit_softmax_rows(cx, block, sb, psap, width, donesem, tc):
    def _(v, tc=tc):
        cx.wait(v, 'v', cx.tok('pe'))
        cx.inc('v', v.reduce_max(sb['sc128'][:, 0:1], psap[:, 0:width], axis=AX.X, negate=True))
    block.vector(_)
    tok_m = cx.tok('v')

    def _(s2, tc=tc):
        cx.wait(s2, 's', tok_m)
        cx.inc('s', s2.activation(out=sb['e128'][:, 0:width], in_=psap[:, 0:width], func=AF.Exp,
                                  bias=sb['sc128'][:, 0:1], accum_out=sb['sc128'][:, 1:2]))
    block.scalar(_)
    tok_e = cx.tok('s')

    def _(v, tc=tc):
        cx.wait(v, 'v', tok_e)
        cx.inc('v', v.reciprocal(sb['sc128'][:, 2:3], sb['sc128'][:, 1:2]))
        cx.wait(v, 'v', cx.tok('v'))
        cx.inc(donesem, v.tensor_scalar(sb['a1b'][:, tc, 0:width], sb['e128'][:, 0:width],
                                        sb['sc128'][:, 2:3], None, ALU.mult))
    block.vector(_)
    return cx.tok(donesem)


def emit_softmax_rows_wide(cx, block, sb, psap, width, donesem, lc, lw):
    def _(v):
        cx.wait(v, 'v', cx.tok('pe'))
        cx.inc('v', v.reduce_max(sb['sc128'][0:lw, 0:1], psap[0:lw, 0:width], axis=AX.X,
                                 negate=True))
    block.vector(_)
    tok_m = cx.tok('v')

    def _(s2):
        cx.wait(s2, 's', tok_m)
        cx.inc('s', s2.activation(out=sb['e128'][0:lw, 0:width], in_=psap[0:lw, 0:width],
                                  func=AF.Exp, bias=sb['sc128'][0:lw, 0:1],
                                  accum_out=sb['sc128'][0:lw, 1:2]))
    block.scalar(_)
    tok_e = cx.tok('s')

    def _(v, lc=lc, lw=lw):
        cx.wait(v, 'v', tok_e)
        cx.inc('v', v.reciprocal(sb['sc128'][0:lw, 2:3], sb['sc128'][0:lw, 1:2]))
        cx.wait(v, 'v', cx.tok('v'))
        cx.inc(donesem, v.tensor_scalar(sb['a3b'][0:lw, lc, 0:width], sb['e128'][0:lw, 0:width],
                                        sb['sc128'][0:lw, 2:3], None, ALU.mult))
    block.vector(_)
    return cx.tok(donesem)


def emit_softmax_rows2(cx, block, sb, psap, width, donesem, lc, lw):
    def _(v):
        cx.wait(v, 'v', cx.tok('pe'))
        cx.inc('v', v.reduce_max(sb['sc128'][0:lw, 0:1], psap[0:lw, 0:width], axis=AX.X,
                                 negate=True))
    block.vector(_)
    tok_m = cx.tok('v')

    def _(s2):
        cx.wait(s2, 's', tok_m)
        cx.inc('s', s2.activation(out=sb['e128'][0:lw, 0:width], in_=psap[0:lw, 0:width],
                                  func=AF.Exp, bias=sb['sc128'][0:lw, 0:1],
                                  accum_out=sb['sc128'][0:lw, 1:2]))
    block.scalar(_)
    tok_e = cx.tok('s')

    def _(v, lc=lc, lw=lw):
        cx.wait(v, 'v', tok_e)
        cx.inc('v', v.reciprocal(sb['sc128'][0:lw, 2:3], sb['sc128'][0:lw, 1:2]))
        cx.wait(v, 'v', cx.tok('v'))
        cx.inc(donesem, v.tensor_scalar(sb['px'][0:lw, lc, 0:width], sb['e128'][0:lw, 0:width],
                                        sb['sc128'][0:lw, 2:3], None, ALU.mult))
    block.vector(_)
    return cx.tok(donesem)


def emit_pinv(cx, block, sb, ps, S, tok_x, tok_rsc):
    nc = cx.nc
    nlm = S // LGRP
    nl2 = (nlm + 127) // 128

    def tposes(src3, dst3, tokdep, warsem):
        for a in range(nl2):
            aw = min(128, nlm - a * 128)
            for bb in range(nl2):
                bw = min(128, nlm - bb * 128)
                def _(t, a=a, bb=bb, aw=aw, bw=bw):
                    cx.wait(t, 't', tokdep, cx.tok('b_psT'))
                    cx.inc('pe', t.transpose(ps['T'][0:bw, 0:aw], sb[src3][0:aw, a, bb * 128:bb * 128 + bw],
                                             sb['ident'][0:aw, 0:aw]))
                block.tensor(_)
                tok_tp = cx.tok('pe')

                def _(v, a=a, bb=bb, aw=aw, bw=bw):
                    cx.wait(v, 'v', tok_tp)
                    cx.inc('b_psT', v.tensor_copy(sb[dst3][0:bw, bb, a * 128:a * 128 + aw],
                                                 ps['T'][0:bw, 0:aw]))
                block.vector(_)
        return cx.tok('b_psT')

    def matmul_sq(lhsT3, rhs3, outps, tokdep, extra=None):
        toks = []
        for ic in range(nl2):
            iw = min(128, nlm - ic * 128)
            def _(t, ic=ic, iw=iw):
                cx.wait(t, 't', tokdep, cx.tok('b_pmm'))
                if extra is not None:
                    cx.wait(t, 't', extra)
                last = None
                for kc in range(nl2):
                    kw = min(128, nlm - kc * 128)
                    t.ldweights(sb[lhsT3][0:kw, kc, ic * 128:ic * 128 + iw])
                    last = t.matmul(outps[0:iw, ic * nlm:ic * nlm + nlm] if False else
                                    ps['A'][0:iw, ic * 512:ic * 512 + nlm],
                                    sb[lhsT3][0:kw, kc, ic * 128:ic * 128 + iw],
                                    sb[rhs3][0:kw, kc, 0:nlm],
                                    start=(kc == 0), stop=(kc == nl2 - 1))
                cx.inc('pe', last)
            block.tensor(_)
            toks.append(cx.tok('pe'))
        return toks

    tok_xT = tposes('px', 'pxT', tok_x, 'b_pxT')

    def _(v):
        cx.wait(v, 'v', tok_xT, tok_rsc)
        for a in range(nl2):
            cx.inc('v', v.tensor_scalar(sb['pz'][:, a, 0:nlm], sb['pxT'][:, a, 0:nlm],
                                        sb['rscale'][0:1, 0:1].partition_broadcast(128)
                                        if False else sb['rscale128'][:, 0:1], None, ALU.mult))
    block.vector(_)
    tok_z0 = cx.tok('v')

    prev_z = tok_z0
    for it in range(6):
        tok_zT = tposes('pz', 'pzT', prev_z, 'b_pzT')
        mm1 = matmul_sq('pxT', 'pz', None, prev_z)
        def _(v, mm1=mm1):
            cx.wait(v, 'v', *mm1)
            for ic in range(nl2):
                iw = min(128, nlm - ic * 128)
                cx.inc('b_pmm', v.tensor_copy(sb['pxz'][0:iw, ic, 0:nlm],
                                              ps['A'][0:iw, ic * 512:ic * 512 + nlm]))
        block.vector(_)
        tok_xz = cx.tok('b_pmm')
        tok_xzT = tposes('pxz', 'pxzT', tok_xz, 'b_pxzT')

        def _(v):
            cx.wait(v, 'v', tok_xz)
            for ic in range(nl2):
                cx.inc('v', v.scalar_tensor_tensor(sb['ptA'][:, ic, 0:nlm], sb['idI'][:, ic, 0:nlm],
                                                   7.0, sb['pxz'][:, ic, 0:nlm],
                                                   ALU.mult, ALU.subtract))
        block.vector(_)
        tok_tA = cx.tok('v')

        mm2 = matmul_sq('pxzT', 'ptA', None, tok_tA, extra=tok_xzT)
        def _(v, mm2=mm2):
            cx.wait(v, 'v', *mm2)
            for ic in range(nl2):
                iw = min(128, nlm - ic * 128)
                cx.inc('b_pmm', v.tensor_copy(sb['ptB'][0:iw, ic, 0:nlm],
                                              ps['A'][0:iw, ic * 512:ic * 512 + nlm]))
        block.vector(_)
        tok_tB = cx.tok('b_pmm')

        def _(v):
            cx.wait(v, 'v', tok_tB)
            for ic in range(nl2):
                cx.inc('v', v.scalar_tensor_tensor(sb['ptA'][:, ic, 0:nlm], sb['idI'][:, ic, 0:nlm],
                                                   15.0, sb['ptB'][:, ic, 0:nlm],
                                                   ALU.mult, ALU.subtract))
        block.vector(_)
        tok_tA2 = cx.tok('v')

        mm3 = matmul_sq('pxzT', 'ptA', None, tok_tA2)
        def _(v, mm3=mm3):
            cx.wait(v, 'v', *mm3)
            for ic in range(nl2):
                iw = min(128, nlm - ic * 128)
                cx.inc('b_pmm', v.tensor_copy(sb['ptB'][0:iw, ic, 0:nlm],
                                              ps['A'][0:iw, ic * 512:ic * 512 + nlm]))
        block.vector(_)
        tok_tB2 = cx.tok('b_pmm')

        def _(v):
            cx.wait(v, 'v', tok_tB2)
            for ic in range(nl2):
                cx.inc('v', v.scalar_tensor_tensor(sb['ptA'][:, ic, 0:nlm], sb['idI'][:, ic, 0:nlm],
                                                   13.0, sb['ptB'][:, ic, 0:nlm],
                                                   ALU.mult, ALU.subtract))
        block.vector(_)
        tok_tA3 = cx.tok('v')

        mm4 = matmul_sq('pzT', 'ptA', None, tok_tA3, extra=tok_zT)
        def _(v, mm4=mm4):
            cx.wait(v, 'v', *mm4)
            for ic in range(nl2):
                iw = min(128, nlm - ic * 128)
                cx.inc('b_pz', v.tensor_scalar(sb['pz'][0:iw, ic, 0:nlm],
                                               ps['A'][0:iw, ic * 512:ic * 512 + nlm], 0.25,
                                               None, ALU.mult))
        block.vector(_)
        prev_z = cx.tok('b_pz')
        def _(v):
            cx.wait(v, 'v', prev_z)
            cx.inc('b_pmm', v.nop())
        block.vector(_)
    tposes('pz', 'pzT', prev_z, 'b_pzT')
    def _(v):
        cx.wait(v, 'v', cx.tok('b_psT'))
        cx.inc('b_pz', v.nop())
    block.vector(_)




def alloc_attn_b(nc, E, cx, sb, S, NBC, E2):
    ST = S // 128
    nlm = S // LGRP
    nl2 = (nlm + 127) // 128
    NH2 = NHEAD // 2
    NCH = E2 // 128
    full = (S == 2048)
    sb['wvw'] = sb['wqk']
    vtm_bf = sb['vtm'][:, :, :].rearrange("p a b -> p (a b)")
    if full:
        sb['qkvw2'] = vtm_bf[:, 0:DT * 1536].rearrange("p (a b) -> p a b", b=1536)
        for i, nm in enumerate(['px', 'pxT', 'pz', 'pzT', 'pxz', 'pxzT', 'ptA', 'ptB']):
            sb[nm] = vtm_bf[:, DT * 1536 + i * 512:DT * 1536 + (i + 1) * 512].rearrange(
                "p (a b) -> p a b", b=256)
        sb['qlkl'] = sb['prow'][:, 0:2048].rearrange("p (a b) -> p a b", b=256)
        sb['xlf'] = sb['pT'][:, :, :].rearrange("p a b -> p (a b)")[:, 0:DT * 256].rearrange(
            "p (a b) -> p a b", b=256)
        sb['wvbb'] = sb['srow'][:, 0:1024]
        sb['outbb'] = sb['srow'][:, 0:1024]
        qkvh_bf = sb['moe1'][:, :, :].rearrange("p a b -> p (a b)").bitcast(BF16)
        sb['qkvh'] = qkvh_bf[:, 0:NH2 * DT * 192].rearrange(
            "p (h c w) -> p h c w", h=NH2, c=DT)
    else:
        sb['qkvw2'] = E(nc.sbuf_tensor("b_qkvw2", [128, DT, 1536], BF16))
        for nm in ['px', 'pxT', 'pz', 'pzT', 'pxz', 'pxzT', 'ptA', 'ptB']:
            sb[nm] = E(nc.sbuf_tensor("b_" + nm, [128, 2, 256], BF16))
        sb['qlkl'] = E(nc.sbuf_tensor("b_qlkl", [128, 8, 256], BF16))
        sb['xlf'] = E(nc.sbuf_tensor("b_xlf", [128, 8, 256], BF16))
        sb['wvbb'] = E(nc.sbuf_tensor("b_wvbb", [128, 1024], F32))
        sb['outbb'] = E(nc.sbuf_tensor("b_outbb", [128, 1024], F32))
        sb['qkvh'] = E(nc.sbuf_tensor("b_qkvh", [128, NH2, DT, 192], BF16))
    sb['outh'] = sb['prow'][0:64, 0:S]
    pT_flat = sb['pT'][:, :, :].rearrange("p a b -> p (a b)")
    sb['vT'] = pT_flat[:, 0:ST * 64].rearrange("p (a b) -> p a b", b=64)
    sb['a3v'] = pT_flat[:, ST * 64:ST * 64 + 4 * 64].rearrange("p (a b) -> p a b", b=64)
    sb['outwh'] = sb['wvt'][0:64, 0:NH2, :]
    wvt_bf = sb['wvt'][:, :, :].rearrange("p a b -> p (a b)")
    sb['wqh'] = wvt_bf[:, 0:2048].rearrange("p (a b) -> p a b", b=256)
    sb['wql'] = wvt_bf[:, 2048:4096].rearrange("p (a b) -> p a b", b=256)
    sb['wkh'] = wvt_bf[:, 4096:6144].rearrange("p (a b) -> p a b", b=256)
    sb['wkl'] = wvt_bf[:, 6144:8192].rearrange("p (a b) -> p a b", b=256)
    sb['wqf'] = sb['dwide'][:, 0:2048].rearrange("p (a b) -> p a b", b=256)
    qfm_flat = sb['qfm'][:, :, :].rearrange("p a b -> p (a b)")
    o1 = ST * 256
    o2 = nl2 * S
    sb['a1b'] = qfm_flat[:, 0:o1].rearrange("p (a b) -> p a b", b=256)
    sb['a1T'] = qfm_flat[:, o1:o1 + o2].rearrange("p (a b) -> p a b", b=S)
    sb['a3b'] = qfm_flat[:, o1 + o2:o1 + 2 * o2].rearrange("p (a b) -> p a b", b=S)
    sb['a3T'] = qfm_flat[:, o1 + 2 * o2:2 * o1 + 2 * o2].rearrange("p (a b) -> p a b", b=256)
    sb['q0h'] = qfm_flat[:, 0:2 * S].rearrange("p (a b) -> p a b", b=S)
    sb['q0l'] = qfm_flat[:, 2 * S:4 * S].rearrange("p (a b) -> p a b", b=S)
    sb['k0h'] = qfm_flat[:, 4 * S:6 * S].rearrange("p (a b) -> p a b", b=S)
    sb['k0l'] = qfm_flat[:, 6 * S:8 * S].rearrange("p (a b) -> p a b", b=S)
    sb['elfm'] = vtm_bf[:, 0:DT * S].rearrange("p (a b) -> p a b", b=S)
    sb['q0f'] = sb['srow'][:, 0:S]
    if full:
        sb['enclo'] = sb['prow'][:, 0:1024]
    else:
        sb['enclo'] = sb['qlkl'][:, :, :].rearrange("p a b -> p (a b)")[:, 0:1024]
    sb['nqkvb'] = vtm_bf[0:64, 0:3 * S].rearrange("p (a b) -> p a b", b=S)
    sb['a2sb'] = sb['ln_sqs'][:, 0:nl2 * nlm].rearrange("p (a b) -> p a b", b=nlm)
    sb['xow'] = sb['ln_sqs'][:, 0:1024]
    sb['a2bf'] = sb['lnout'][:, 0:nl2 * nlm].rearrange("p (a b) -> p a b", b=nlm)
    sb['cacc'] = sb['dwide'][0:64, 0:S]
    sb['e128'] = sb['srow'][:, 0:S]
    sb['srow2'] = sb['srow'][:, 0:S]
    vtm_i32 = vtm_bf.bitcast(I32)
    vtm_f32 = vtm_bf.bitcast(F32)
    sb['lnin2'] = sb['dwide'][:, 0:1024]
    sb['h0t2'] = sb['dwide'][:, 1024:2048]
    sb['xow2'] = vtm_f32[:, 0:1024]
    if full:
        sb['h1t2'] = sb['srow'][:, 0:1024]
        sb['n2f32c'] = sb['srow'][:, 1024:2048]
    else:
        sb['h1t2'] = E(nc.sbuf_tensor("b_h1t2", [128, D], F32))
        sb['n2f32c'] = E(nc.sbuf_tensor("b_n2f32c", [128, D], F32))
    sb['erow'] = vtm_i32[:, 0:NCH]
    sb['ecol'] = vtm_i32[:, NCH:2 * NCH]
    sb['eidx'] = vtm_i32[:, 2 * NCH:3 * NCH]
    sb['elo_i'] = vtm_i32[:, 3 * NCH:4 * NCH]
    sb['ehi_i'] = vtm_i32[:, 4 * NCH:5 * NCH]
    sb['eval'] = vtm_f32[:, 5 * NCH:6 * NCH]
    sb['elo'] = vtm_f32[:, 6 * NCH:7 * NCH]
    sb['ehi'] = vtm_f32[:, 7 * NCH:8 * NCH]
    sb['nlmkb'] = E(nc.sbuf_tensor("b_nlmkb", [64, 2, 256], BF16))
    sb['idI'] = E(nc.sbuf_tensor("b_idI", [128, 2, 256], BF16))
    sb['idt'] = E(nc.sbuf_tensor("b_idt", [128, 256], F32))
    sb['convb'] = E(nc.sbuf_tensor("b_convb", [64, NH2 * CONVK], F32))
    sb['convw1'] = E(nc.sbuf_tensor("b_convw1", [1, NH2 * CONVK], F32))
    sb['wvb1'] = sb['n2f32'][0:1, :]
    sb['outb1'] = sb['n2f32'][0:1, :]
    sb['ctrb'] = E(nc.sbuf_tensor("b_ctrb", [128, 64], F32))
    sb['cvfb'] = E(nc.sbuf_tensor("b_cvfb", [128, 64], F32))
    sb['eLb'] = E(nc.sbuf_tensor("b_eLb", [128, 256], F32))
    sb['ehh'] = E(nc.sbuf_tensor("b_ehh", [128, 2 * ST], F32))
    sb['rtrow'] = E(nc.sbuf_tensor("b_rtrow", [128, ST], F32))
    sb['alphat'] = E(nc.sbuf_tensor("b_alphat", [128, ST], F32))
    sb['arp2b'] = E(nc.sbuf_tensor("b_arp2b", [128, ST], F32))
    sb['asum'] = E(nc.sbuf_tensor("b_asum", [128, 1], F32))
    sb['mbneg'] = E(nc.sbuf_tensor("b_mbneg", [128, 1], BF16))
    sb['mrow'] = E(nc.sbuf_tensor("b_mrow", [1, 128], F32))
    sb['onescolF'] = E(nc.sbuf_tensor("b_onescolF", [128, 1], F32))
    sb['scmax'] = E(nc.sbuf_tensor("b_scmax", [1, 4], F32))
    sb['sc1'] = E(nc.sbuf_tensor("b_sc1", [1, 4], F32))
    sb['sc2'] = E(nc.sbuf_tensor("b_sc2", [1, 4], F32))
    sb['rscale'] = E(nc.sbuf_tensor("b_rscale", [1, 1], F32))
    sb['rscale128'] = E(nc.sbuf_tensor("b_rscale128", [128, 1], F32))
    sb['sc128'] = E(nc.sbuf_tensor("b_sc128", [128, 4], F32))


def emit_preamble_b(cx, block, sb, sbn, ins, S, NBC):
    def _(y):
        cx.inc('dw2', y.dma_start(out=sb['convw1'][:, :], in_=ins['conv'][:, :]), 16)
    block.sync(_)


def build_program(S, NBC, phases, E2=65536, dbg=False):
    nc = bass.Bass("TRN2", target_bir_lowering=False, detect_race_conditions=False)
    TOKC = NBC * S
    TOKI = 2 * TOKC
    NSLOT = NE * CAP
    do_m1 = "m1" in phases
    do_m2 = "m2" in phases
    do_b = "b" in phases
    do_moe = do_m1 or do_m2
    do_role = do_moe or do_b
    NH2 = NHEAD // 2

    x_in = nc.dram_tensor("x16", [TOKC, D], F32, kind="ExternalInput").ap()
    wqk_in = nc.dram_tensor("w_qk", [D, D], F32, kind="ExternalInput").ap()
    wvt_in = nc.dram_tensor("w_v_t", [D, D], F32, kind="ExternalInput").ap()
    alibi_in = nc.dram_tensor("alibi", [1, 1], F32, kind="ExternalInput").ap()
    ins = dict(w_qk=wqk_in, w_v_t=wvt_in, alibi=alibi_in)
    if do_role:
        ins['role'] = nc.dram_tensor("role", [1, 1], I32, kind="ExternalInput").ap()
        ins['rolef'] = nc.dram_tensor("rolef", [1, 1], F32, kind="ExternalInput").ap()
        ins['nonce'] = nc.dram_tensor("nonce", [1, 1], I32, kind="ExternalInput").ap()
    if do_moe:
        ins['gate_wt'] = nc.dram_tensor("gate_wt", [D, NE], F32, kind="ExternalInput").ap()
        ins['wg'] = nc.dram_tensor("wg_in", [NEC, D, INTER], F32, kind="ExternalInput").ap()
        ins['wu'] = nc.dram_tensor("wu_in", [NEC, D, INTER], F32, kind="ExternalInput").ap()
        ins['wd'] = nc.dram_tensor("wd_in", [NEC, INTER, D], F32, kind="ExternalInput").ap()
    if do_b:
        ins['qkv_w'] = nc.dram_tensor("qkv_w", [D, 3 * NHEAD * DH], F32,
                                      kind="ExternalInput").ap()
        ins['qkvh'] = nc.dram_tensor("qkvh_in", [D, NH2, 192], F32, kind="ExternalInput").ap()
        ins['wq'] = nc.dram_tensor("wq_in", [D, WQK], F32, kind="ExternalInput").ap()
        ins['wk'] = nc.dram_tensor("wk_in", [D, WQK], F32, kind="ExternalInput").ap()
        ins['wv_w'] = nc.dram_tensor("wvw_in", [D, D], F32, kind="ExternalInput").ap()
        ins['outw'] = nc.dram_tensor("outw_in", [NH2 * DH, D], F32, kind="ExternalInput").ap()
        ins['out_b'] = nc.dram_tensor("outb_in", [1, D], F32, kind="ExternalInput").ap()
        ins['wv_b'] = nc.dram_tensor("wvb_in", [1, D], F32, kind="ExternalInput").ap()
        ins['conv'] = nc.dram_tensor("conv_in", [1, NH2 * CONVK], F32,
                                     kind="ExternalInput").ap()

    outname = {"a": "out_h1", "m1": "out_h2", "b": "out_h3", "m2": "out_h4"}[phases[-1]]
    out_t = nc.dram_tensor(outname, [TOKC, D], F32, kind="ExternalOutput").ap()

    h1_dram = nc.dram_tensor("h1d", [TOKC, D], F32, kind="Internal").ap()
    n1_dram = nc.dram_tensor("n1d", [S, D], BF16, kind="Internal").ap()
    dram = dict(h0=x_in, h1=h1_dram, n1scr=n1_dram)
    if do_moe:
        dram['h2'] = nc.dram_tensor("h2d", [TOKC, D], F32, kind="Internal").ap()
        dram['n2l'] = nc.dram_tensor("n2ld", [TOKC, D], BF16, kind="Internal").ap()
        dram['n2lo'] = nc.dram_tensor("n2lod", [TOKC, D], BF16, kind="Internal").ap()
        dram['H'] = nc.dram_tensor("Hd", [TOKC, NE], BF16, kind="Internal").ap()
        dram['Wh'] = nc.dram_tensor("Whd", [TOKC, NE], F32, kind="Internal").ap()
        dram['C'] = nc.dram_tensor("Cd", [TOKC, NE], F32, kind="Internal").ap()
        dram['jd'] = nc.dram_tensor("jdd", [TOKC, TOPK], I32, kind="Internal").ap()
        dram['wg'] = nc.dram_tensor("wg_bf", [NEC, D, INTER], BF16, kind="Internal").ap()
        dram['wu'] = nc.dram_tensor("wu_bf", [NEC, D, INTER], BF16, kind="Internal").ap()
        dram['wd'] = nc.dram_tensor("wd_bf", [NEC, INTER, D], BF16, kind="Internal").ap()
    if do_b:
        dram['h3'] = nc.dram_tensor("h3d", [TOKC, D], F32, kind="Internal").ap()
        dram['n3l'] = nc.dram_tensor("n3ld", [TOKC, D], BF16, kind="Internal").ap()
        dram['n3f'] = nc.dram_tensor("n3fd", [TOKC, D], F32, kind="Internal").ap()
        dram['val'] = nc.dram_tensor("vald", [TOKC, D], F32, kind="Internal").ap()
        dram['Sd'] = nc.dram_tensor("Sdd", [S, S], F32, kind="Internal").ap()
        dram['xg0'] = nc.dram_tensor("xg0d", [S, D], F32, kind="Internal").ap()
        dram['enc0'] = nc.dram_tensor("enc0d", [S, D], BF16, kind="Internal").ap()
        dram['enc0l'] = nc.dram_tensor("enc0ld", [S, D], BF16, kind="Internal").ap()
        dram['xgp'] = nc.dram_tensor("xgpd", [S, D], F32, kind="Internal").ap()
        dram['er'] = nc.dram_tensor("er", [128, E2 // 128], I32, kind="ExternalInput").ap()
        dram['ec'] = nc.dram_tensor("ec", [128, E2 // 128], I32, kind="ExternalInput").ap()
        dram['ev'] = nc.dram_tensor("ev", [128, E2 // 128], F32, kind="ExternalInput").ap()

    shr = {}
    if do_role:
        shr['flags'] = nc.dram_tensor("flagsh", [24, 2, 4], I32, kind="Internal",
                                      addr_space="Shared").ap()
    if do_moe:
        shr['n2sh'] = nc.dram_tensor("n2sh", [TOKI, D], BF16, kind="Internal",
                                     addr_space="Shared").ap()
        shr['Ywsh'] = nc.dram_tensor("Ywsh", [NSLOT, D], BF16, kind="Internal",
                                     addr_space="Shared").ap()
        shr['Tsh'] = nc.dram_tensor("Tsh", [NSLOT, 2], F32, kind="Internal",
                                    addr_space="Shared").ap()
        shr['tot'] = nc.dram_tensor("totsh", [2, 1, NE], F32, kind="Internal",
                                    addr_space="Shared").ap()
    if do_b:
        shr['x0sh'] = nc.dram_tensor("x0sh", [S, D], BF16, kind="Internal",
                                     addr_space="Shared").ap()
        shr['xgsh'] = nc.dram_tensor("xgsh", [2, S, D], F32, kind="Internal",
                                     addr_space="Shared").ap()
        shr['scsh'] = nc.dram_tensor("scsh", [2, 1, 4], F32, kind="Internal",
                                     addr_space="Shared").ap()
        shr['arsh'] = nc.dram_tensor("arsh", [2, 128, S // 128], F32, kind="Internal",
                                     addr_space="Shared").ap()

    with ExitStack() as stack:
        E = stack.enter_context
        cx = CX(nc, stack)
        cx.swdge_sems = {'dsh', 'dfl', 'dto', 'dTs', 'dgx', 'dtl', 'dyw', 'dKg',
                         'dpoll', 'dw', 'dwc', 'dHx', 'dwb'}
        cx.dummy_dram = nc.dram_tensor("dmyd", [1, 4], F32, kind="Internal").ap()
        sb, ps = alloc_common(nc, E, cx, S)
        sbn = {}
        if do_role:
            alloc_moe(nc, E, cx, sb, S, NBC, do_moe=do_moe)
            sbn['g_rreg'] = nc.gpsimd.alloc_register("g_rreg")
            sbn['g_lreg'] = nc.gpsimd.alloc_register("g_lreg")
            sbn['nreg'] = nc.gpsimd.alloc_register("nreg")
            sbn['pr'] = nc.gpsimd.alloc_register("prr")
            sbn['fv'] = nc.gpsimd.alloc_register("fvr")
            sbn['pc'] = nc.gpsimd.alloc_register("pcr")
            sbn['v_lreg'] = nc.vector.alloc_register("v_lreg")
            sbn['bnd_slot'] = nc.gpsimd.alloc_register("bnd_slot")
            sbn['bnd_tok'] = nc.gpsimd.alloc_register("bnd_tok")
            sbn['y_lreg'] = nc.sync.alloc_register("y_lreg")
        if do_moe:
            cx.dram_wg = [dram['wg'][e] for e in range(NEC)]
            cx.dram_wu = [dram['wu'][e] for e in range(NEC)]
            cx.dram_wd = [dram['wd'][e] for e in range(NEC)]
        if do_b:
            alloc_attn_b(nc, E, cx, sb, S, NBC, E2)
            sbn['bnd_S2'] = nc.gpsimd.alloc_register("bnd_S2")
        if do_role and S == 2048:
            moe1f = sb['moe1'][:, :, :].rearrange("p a b -> p (a b)")
            sb['lnin_b'] = moe1f[:, 0:1024]
            sb['ln_sqsb'] = moe1f[:, 1024:2048]
            sb['lnoutb'] = moe1f[:, 2048:2560].bitcast(BF16)
        else:
            sb['lnin_b'] = E(nc.sbuf_tensor("lnin_b", [128, D], F32))
            sb['ln_sqsb'] = E(nc.sbuf_tensor("ln_sqsb", [128, D], F32))
            sb['lnoutb'] = E(nc.sbuf_tensor("lnoutb", [128, D], BF16))
        cx.sbn = sbn
        if do_role:
            cx.dummy_sb = sb['dummy_sb']
        block = E(nc.Block())

        emit_preamble(cx, block, sb, ps, ins)
        if do_role:
            sbn['noncebuf'] = sb['noncebuf']
            sbn['flagrd'] = sb['flagrd']
            emit_preamble_moe(cx, block, sb, ps, shr, sbn, ins, S, NBC, do_moe=do_moe)
        if do_b:
            emit_preamble_b(cx, block, sb, sbn, ins, S, NBC)

        if "a" in phases:
            emit_attn_a(cx, block, sb, ps, dram, S, NBC)
        else:
            def _(y):
                cx.inc('dh1', y.dma_start(out=h1_dram[:, :], in_=x_in[:, :]), 16)
            block.sync(_)
        cur = h1_dram

        if do_m1:
            def _(y):
                cx.wait(y, 'y', cx.tok('dh1'))
            block.sync(_)

            def _(g):
                cx.wait(g, 'g', cx.tok('dh1'), cx.tok('dwc'))
            block.gpsimd(_)
            dram_m1 = dict(dram)
            dram_m1['hin'] = cur
            dram_m1['hout'] = out_t if phases[-1] == "m1" else dram['h2']
            emit_moe(cx, block, sb, ps, shr, sbn, dram_m1, S, NBC, 0)
            cur = dram_m1['hout']

        if do_b:
            dram_b = dict(dram)
            dram_b['hin'] = cur
            dram_b['hout'] = out_t if phases[-1] == "b" else dram['h3']
            emit_attn_b(cx, block, sb, ps, shr, sbn, dram_b, ins, S, NBC, E2)
            cur = dram_b['hout']

        if do_m2:
            def _(y):
                cx.wait(y, 'y', cx.tok('dh3'))
            block.sync(_)

            def _(v):
                cx.wait(v, 'v', cx.tok('dh3'))
                cx.inc('moe_prev_done', v.nop())
            block.vector(_)

            def _(g):
                cx.wait(g, 'g', cx.tok('dh3'))
            block.gpsimd(_)
            dram_m2 = dict(dram)
            dram_m2['hin'] = cur
            dram_m2['hout'] = out_t
            emit_moe(cx, block, sb, ps, shr, sbn, dram_m2, S, NBC, 1)
            cur = out_t

        if dbg and do_b:
            dbgo = {}
            for nm, src in [("dbg_n3f", dram['n3f']), ("dbg_val", dram['val']),
                            ("dbg_Sd", dram['Sd']), ("dbg_xg0", dram['xg0']),
                            ("dbg_xgp", dram['xgp'])]:
                dbgo[nm] = nc.dram_tensor(nm, list(src.shape), F32,
                                          kind="ExternalOutput").ap()
            alph_o = nc.dram_tensor("dbg_alph", [128, S // 128], F32,
                                    kind="ExternalOutput").ap()
            sb_dumps = [("dbg_a1b", sb['a1b'][:, :, :], BF16),
                        ("dbg_a3b", sb['a3b'][:, :, :], BF16),
                        ("dbg_a3T", sb['a3T'][:, :, :], BF16),
                        ("dbg_vT", sb['vT'][:, :, :], BF16),
                        ("dbg_nlmk", sb['nlmkb'][:, :, :], BF16),
                        ("dbg_a1", sb['a1b'][:, :, :], BF16),
                        ("dbg_a3", sb['a3b'][:, :, :], BF16),
                        ("dbg_pz", sb['pz'][:, :, :], BF16),
                        ("dbg_px", sb['px'][:, :, :], BF16),
                        ("dbg_a3v", sb['a3v'][:, :, :], BF16),
                        ("dbg_outh", sb['outh'][:, :], BF16),
                        ("dbg_cacc", sb['cacc'][:, :], F32),
                        ("dbg_rsc", sb['rscale128'][:, :], F32)]
            sb_outs = {}
            for nm, ap, dt in sb_dumps:
                sb_outs[nm] = nc.dram_tensor(nm, list(ap.shape), dt,
                                             kind="ExternalOutput").ap()

            def _(y):
                cx.wait(y, 'y', cx.tok('dh3'), cx.tok('v'), cx.tok('s'))
                for nm, ap, dt in sb_dumps:
                    cx.inc('dout', y.dma_start(out=sb_outs[nm], in_=ap), 16)
                for nm, src in [("dbg_n3f", dram['n3f']), ("dbg_val", dram['val']),
                                ("dbg_Sd", dram['Sd']), ("dbg_xg0", dram['xg0']),
                                ("dbg_xgp", dram['xgp'])]:
                    cx.inc('dout', y.dma_start(out=dbgo[nm], in_=src), 16)
                cx.inc('dout', y.dma_start(out=alph_o, in_=sb['alphat'][:, :]), 16)
                y.wait_ge(cx.sem('dout'), cx.cnt['dout'])
            block.sync(_)

        def _(y):
            cx.wait(y, 'y', cx.tok('dh1'), cx.tok('dh2'), cx.tok('dh3'),
                    cx.tok('v'), cx.tok('s'), cx.tok('pe'), cx.tok('g'))
            if cur is h1_dram:
                nchunk = max(1, TOKC // 4096)
                step = TOKC // nchunk
                for i in range(nchunk):
                    cx.inc('dout', y.dma_start(out=out_t[i * step:(i + 1) * step, :],
                                               in_=cur[i * step:(i + 1) * step, :]), 16)
                y.wait_ge(cx.sem('dout'), cx.cnt['dout'])
        block.sync(_)

    return nc



def _host_shard(inputs):
    x = np.asarray(inputs["L0_attn_a_x"])
    B, S, Dm = x.shape
    NBC = B // 2
    w_qk = np.asarray(inputs["L0_attn_a_W_qk"])
    w_v = np.asarray(inputs["L0_attn_a_W_v"])
    alibi = np.asarray(inputs["L0_attn_a_alibi_slope"]).reshape(1, 1)
    gate_w = np.asarray(inputs["L0_ffn_gate_w"])
    w_gate = np.asarray(inputs["L0_ffn_w_gate"])
    w_up = np.asarray(inputs["L0_ffn_w_up"])
    w_down = np.asarray(inputs["L0_ffn_w_down"])
    qkv_w = np.asarray(inputs["L0_attn_b_qkv_w"])
    wq = np.asarray(inputs["L0_attn_b_wq"])
    wk = np.asarray(inputs["L0_attn_b_wk"])
    wv_w = np.asarray(inputs["L0_attn_b_wv_w"])
    wv_b = np.asarray(inputs["L0_attn_b_wv_b"]).reshape(1, Dm)
    out_w = np.asarray(inputs["L0_attn_b_out_w"])
    out_b = np.asarray(inputs["L0_attn_b_out_b"]).reshape(1, Dm)
    conv_w = np.asarray(inputs["L0_attn_b_conv_w"])
    er = np.ascontiguousarray(np.asarray(inputs["L0_attn_b_adj_rows"]).astype(np.int32))
    ec = np.ascontiguousarray(np.asarray(inputs["L0_attn_b_adj_cols"]).astype(np.int32))
    ev = np.ascontiguousarray(np.asarray(inputs["L0_attn_b_adj_vals"]).astype(np.float32))
    Efull = er.shape[0]
    E2 = Efull // 2

    w_v_t = np.ascontiguousarray(w_v.T)
    gate_wt = np.ascontiguousarray(gate_w.T)
    nonce = np.array([[np.random.randint(1, 2 ** 30)]], np.int32)
    NH2 = NHEAD // 2
    ims = []
    for role in range(2):
        x16 = np.ascontiguousarray(x[role * NBC:(role + 1) * NBC].reshape(-1, Dm))
        elo = role * NEC
        heads = [role * NH2 + hh for hh in range(NH2)]
        qkvh = np.stack([np.concatenate(
            [qkv_w[:, g * DH:(g + 1) * DH],
             qkv_w[:, 512 + g * DH:512 + (g + 1) * DH],
             qkv_w[:, 1024 + g * DH:1024 + (g + 1) * DH]], axis=1)
            for g in heads], axis=1)
        conv = np.ascontiguousarray(
            conv_w[role * NH2:(role + 1) * NH2, 0, :, 0].reshape(1, NH2 * CONVK))
        im = {
            "x16": x16, "w_qk": w_qk, "w_v_t": w_v_t, "alibi": alibi,
            "role": np.array([[role]], np.int32),
            "rolef": np.array([[float(role)]], np.float32),
            "nonce": nonce,
            "gate_wt": gate_wt,
            "wg_in": np.ascontiguousarray(np.swapaxes(w_gate[elo:elo + NEC], 1, 2)),
            "wu_in": np.ascontiguousarray(np.swapaxes(w_up[elo:elo + NEC], 1, 2)),
            "wd_in": np.ascontiguousarray(np.swapaxes(w_down[elo:elo + NEC], 1, 2)),
            "qkv_w": qkv_w,
            "qkvh_in": np.ascontiguousarray(qkvh),
            "wq_in": wq, "wk_in": wk, "wvw_in": wv_w,
            "outw_in": np.ascontiguousarray(out_w[role * NH2 * DH:(role + 1) * NH2 * DH]),
            "outb_in": out_b, "wvb_in": wv_b, "conv_in": conv,
            "er": np.ascontiguousarray(er[role * E2:(role + 1) * E2].reshape(-1, 128).T),
            "ec": np.ascontiguousarray(ec[role * E2:(role + 1) * E2].reshape(-1, 128).T),
            "ev": np.ascontiguousarray(ev[role * E2:(role + 1) * E2].reshape(-1, 128).T),
        }
        ims.append(im)
    return [dict(ims[c % 2]) for c in range(8)], E2


def kernel(**inputs):
    x = np.asarray(inputs["L0_attn_a_x"])
    B, S, Dm = x.shape
    NBC = B // 2
    ims, E2 = _host_shard(inputs)
    nc = build_program(S, NBC, ("a", "m1", "b", "m2"), E2=E2)
    r = run_bass_kernel_spmd(nc, ims, core_ids=list(range(8)))
    out = np.empty((B, S, Dm), np.float32)
    for c in range(8):
        k = c // 2
        role = c % 2
        h4 = r.results[c]["out_h4"].reshape(NBC, S, Dm)
        out[role * NBC + 4 * k: role * NBC + 4 * k + 4] = h4[4 * k:4 * k + 4]
    return out


# version 06
# speedup vs baseline: 1.1643x; 1.1643x over previous
import numpy as np
import ml_dtypes
from contextlib import ExitStack

import concourse.bass as bass
import concourse.mybir as mybir
from concourse.bass import IndirectOffsetOnAxis
from concourse.bass_utils import run_bass_kernel_spmd

F32 = mybir.dt.float32
BF16 = mybir.dt.bfloat16
I32 = mybir.dt.int32
AF = mybir.ActivationFunctionType
ALU = mybir.AluOpType
AX = mybir.AxisListType

CFG = dict(S=2048, NBC=16, PHASES=("a", "m1", "b", "m2"), NCORES=8)

D = 1024
NE, NEC, CAP, TOPK, INTER = 64, 32, 256, 8, 768
NHEAD, DH, LM = 8, 64, 256
CONVK, WQK = 33, 256
LGRP = 8
B_STOP = None
NEG = -1.0e30
DT = D // 128


class CX:

    def __init__(self, nc, stack):
        self.nc, self.stack = nc, stack
        self.sems, self.cnt, self.wc = {}, {}, {}

    def sem(self, name):
        if name not in self.sems:
            self.sems[name] = self.stack.enter_context(self.nc.semaphore(name))
            self.cnt[name] = 0
        return self.sems[name]

    def inc(self, name, inst, n=1):
        inst.then_inc(self.sem(name), n)
        self.cnt[name] += n
        return (name, self.cnt[name])

    def tok(self, name):
        self.sem(name)
        return (name, self.cnt[name])

    def wait(self, eng, ek, *toks):
        for tk in toks:
            if tk is None:
                continue
            nm, v = tk
            if v <= 0:
                continue
            if self.wc.get((ek, nm), 0) < v:
                eng.wait_ge(self.sem(nm), v)
                self.wc[(ek, nm)] = v


def emit_ln_tile(cx, block, sb, src, dst, pre_s_tok=None, scr=''):
    ln_mean = sb['ln_mean' + scr]
    ln_sqs = sb['ln_sqs' + scr]
    ln_sq = sb['ln_sq' + scr]
    ln_var = sb['ln_var' + scr]
    ln_rstd = sb['ln_rstd' + scr]
    ln_nb = sb['ln_nb' + scr]
    def _(v):
        cx.inc('v', v.reduce_sum(ln_mean[:], src[:], axis=AX.X))
        cx.inc('v', v.tensor_tensor(ln_sqs[:], src[:], src[:], ALU.mult))
        cx.wait(v, 'v', cx.tok('v'))
        cx.inc('v', v.tensor_scalar(ln_mean[:], ln_mean[:], 1.0 / D, None, ALU.mult))
        cx.inc('v', v.reduce_sum(ln_sq[:], ln_sqs[:], axis=AX.X))
        cx.wait(v, 'v', cx.tok('v'))
        cx.inc('v', v.tensor_tensor(ln_var[:], ln_mean[:], ln_mean[:], ALU.mult))
        cx.wait(v, 'v', cx.tok('v'))
        cx.inc('v', v.scalar_tensor_tensor(ln_var[:], ln_sq[:], 1.0 / D,
                                           ln_var[:], ALU.mult, ALU.subtract))
        cx.wait(v, 'v', cx.tok('v'))
        cx.inc('v', v.tensor_scalar(ln_var[:], ln_var[:], 1e-5, None, ALU.add))
    block.vector(_)
    tv2 = cx.tok('v')

    def _(s):
        cx.wait(s, 's', tv2)
        cx.inc('s', s.activation(out=ln_sq[:], in_=ln_var[:], func=AF.Sqrt))
    block.scalar(_)
    ts2 = cx.tok('s')

    def _(v):
        cx.wait(v, 'v', ts2)
        cx.inc('v', v.reciprocal(ln_rstd[:], ln_sq[:]))
        cx.wait(v, 'v', cx.tok('v'))
        cx.inc('v', v.tensor_tensor(ln_nb[:], ln_rstd[:], ln_mean[:],
                                    ALU.mult))
        cx.wait(v, 'v', cx.tok('v'))
        cx.inc('v', v.tensor_scalar(ln_nb[:], ln_nb[:], -1.0, None, ALU.mult))
    block.vector(_)
    tv3 = cx.tok('v')

    def _(s):
        cx.wait(s, 's', tv3, pre_s_tok)
        cx.inc('s', s.activation(out=dst[:], in_=src[:], func=AF.Identity,
                                 bias=ln_nb[:], scale=ln_rstd[:]))
    block.scalar(_)
    return cx.tok('s')


def emit_attn_a(cx, block, sb, ps, dram, S, NBC):
    nc = cx.nc
    ST = S // 128
    prev = dict(pe_n1_readers=None,
                v_psA_free=None,
                s_psA_free=None,
                s_pv_free=None,
                v_pv_free=None,
                tp_done=None,
                exp_done=None,
                n1_store=None,
                h1_store=None,
                h0_used=None)

    for b in range(NBC):
        base = b * S
        ln_done = None
        for tt in range(ST):
            pb = (b * ST + tt) % 2
            srcb = sb['lnin'] if pb == 0 else sb['lnin_b']
            dstb = sb['lnout'] if pb == 0 else sb['lnoutb']
            kfree = ('dn1', max(0, cx.cnt.get('dn1', 0) - 16))

            def _(y, tt=tt, srcb=srcb, kfree=kfree):
                cx.wait(y, 'y', kfree)
                cx.inc('dln', y.dma_start(
                    out=srcb[:],
                    in_=dram['h0'][base + tt * 128:base + (tt + 1) * 128, :]), 16)
            block.sync(_)
            tok_ld = cx.tok('dln')

            def _(v, kfree=kfree):
                cx.wait(v, 'v', tok_ld, kfree)
            block.vector(_)

            def _(s, kfree=kfree):
                cx.wait(s, 's', tok_ld, kfree)
            block.scalar(_)
            tok_ln = emit_ln_tile(cx, block, sb, srcb, dstb, kfree,
                                  scr=('' if pb == 0 else 'b'))

            def _(y, tt=tt, dstb=dstb):
                cx.wait(y, 'y', tok_ln)
                cx.inc('dn1', y.dma_start(out=dram['n1scr'][tt * 128:(tt + 1) * 128, :],
                                          in_=dstb[:]), 16)
            block.sync(_)
            prev['n1_store'] = cx.tok('dn1')

        def _(y):
            cx.wait(y, 'y', prev['n1_store'], prev['pe_n1_readers'])
            for dc in range(DT):
                cx.inc('dtp', y.dma_start(out=sb['n1fm'][:, dc, 0:S],
                                          in_=dram['n1scr'][0:S, dc * 128:(dc + 1) * 128],
                                          transpose=True), 16)
        block.sync(_)
        tok_n1 = cx.tok('dtp')

        nmv = S // 512 if S >= 512 else 1
        mvw = min(512, S)
        for do in range(DT):
            def _(t, do=do):
                cx.wait(t, 't', tok_n1, prev['s_psA_free'], prev['v_psA_free'])
                last = None
                for di in range(DT):
                    t.ldweights(sb['wqk'][:, di, do * 128:(do + 1) * 128])
                    for tm in range(nmv):
                        last = t.matmul(ps['A'][:, tm * mvw:(tm + 1) * mvw],
                                        sb['wqk'][:, di, do * 128:(do + 1) * 128],
                                        sb['n1fm'][:, di, tm * mvw:(tm + 1) * mvw],
                                        start=(di == 0), stop=(di == DT - 1))
                cx.inc('pe', last)
            block.tensor(_)
            tok_mm = cx.tok('pe')

            def _(s, do=do):
                cx.wait(s, 's', tok_mm)
                cx.inc('s', s.activation(out=sb['qfm'][:, do, 0:S], in_=ps['A'][:, 0:S],
                                         func=AF.Identity, scale=1.0 / 32.0))
            block.scalar(_)
            prev['s_psA_free'] = cx.tok('s')

        for tc in range(ST):
            def _(t, tc=tc):
                cx.wait(t, 't', tok_n1, prev['s_pv_free'], prev['v_pv_free'])
                last = None
                for di in range(DT):
                    t.ldweights(sb['n1fm'][:, di, tc * 128:(tc + 1) * 128])
                    last = t.matmul(ps['PV'][:, 0:512], sb['n1fm'][:, di, tc * 128:(tc + 1) * 128],
                                    sb['wvt'][:, di, 0:512], start=(di == 0), stop=(di == DT - 1))
                    last = t.matmul(ps['PV'][:, 512:1024], sb['n1fm'][:, di, tc * 128:(tc + 1) * 128],
                                    sb['wvt'][:, di, 512:1024], start=(di == 0), stop=(di == DT - 1))
                cx.inc('pe', last)
            block.tensor(_)
            tok_mm = cx.tok('pe')

            def _(s, tc=tc):
                cx.wait(s, 's', tok_mm)
                cx.inc('s', s.activation(out=sb['vtm'][:, tc, :], in_=ps['PV'][:, 0:1024],
                                         func=AF.Identity))
            block.scalar(_)
            prev['s_pv_free'] = cx.tok('s')
        tok_v = cx.tok('s')

        for qt in range(ST):
            kmax = (qt + 1) * 128
            kmv = []
            off = 0
            while off < kmax:
                w = min(512, kmax - off)
                kmv.append((off, w))
                off += w

            def _(t, qt=qt, kmv=kmv):
                cx.wait(t, 't', prev['s_psA_free'], prev['v_psA_free'])
                last = None
                for di in range(DT):
                    t.ldweights(sb['qfm'][:, di, qt * 128:(qt + 1) * 128])
                    for (off, w) in kmv:
                        last = t.matmul(ps['A'][:, off:off + w],
                                        sb['qfm'][:, di, qt * 128:(qt + 1) * 128],
                                        sb['n1fm'][:, di, off:off + w],
                                        start=(di == 0), stop=(di == DT - 1))
                cx.inc('pe', last)
            block.tensor(_)
            tok_sc = cx.tok('pe')

            def _(v, qt=qt, kmax=kmax):
                cx.wait(v, 'v', tok_sc, prev['exp_done'])
                off0 = 2048 - qt * 128
                cx.inc('v', v.scalar_tensor_tensor(sb['srow'][:, 0:kmax],
                                                   sb['dwide'][:, off0:off0 + kmax],
                                                   sb['slope128'][:, 0:1],
                                                   ps['A'][:, 0:kmax], ALU.mult, ALU.add))
                cx.wait(v, 'v', cx.tok('v'))
                cx.inc('v', v.tensor_tensor(sb['srow'][:, qt * 128:(qt + 1) * 128],
                                            sb['srow'][:, qt * 128:(qt + 1) * 128],
                                            sb['mask'][:, :], ALU.add))
                cx.wait(v, 'v', cx.tok('v'))
                cx.inc('v', v.reduce_max(sb['rmax'][:], sb['srow'][:, 0:kmax], axis=AX.X,
                                         negate=True))
            block.vector(_)
            prev['v_psA_free'] = cx.tok('v')
            tok_srow = cx.tok('v')

            def _(s, qt=qt, kmax=kmax):
                cx.wait(s, 's', tok_srow, prev['tp_done'])
                cx.inc('s', s.activation(out=sb['prow'][:, 0:kmax], in_=sb['srow'][:, 0:kmax],
                                         func=AF.Exp, bias=sb['rmax'][:, 0:1],
                                         accum_out=sb['rsum'][:, 0:1]))
            block.scalar(_)
            prev['exp_done'] = cx.tok('s')
            tok_exp = cx.tok('s')

            def _(v):
                cx.wait(v, 'v', tok_exp)
                cx.inc('v', v.reciprocal(sb['rrec'][:], sb['rsum'][:]))
            block.vector(_)
            tok_rec = cx.tok('v')

            tok_tp_evac = None
            for kc in range(qt + 1):
                def _(t, kc=kc):
                    cx.wait(t, 't', tok_exp, cx.tok('ptv'))
                    cx.inc('pe', t.transpose(ps['T'][:, :], sb['prow'][:, kc * 128:(kc + 1) * 128],
                                             sb['ident'][:, :]))
                block.tensor(_)
                tok_tp = cx.tok('pe')

                def _(v, kc=kc):
                    cx.wait(v, 'v', tok_tp)
                    cx.inc('ptv', v.tensor_copy(sb['pT'][:, kc, :], ps['T'][:, :]))
                block.vector(_)
            prev['tp_done'] = cx.tok('ptv')
            tok_ptv = cx.tok('ptv')

            def _(y, qt=qt):
                cx.wait(y, 'y', prev['h0_used'])
                cx.inc('dh0', y.dma_start(out=sb['h0t'][:],
                                          in_=dram['h0'][base + qt * 128:base + (qt + 1) * 128, :]), 16)
            block.sync(_)
            tok_h0 = cx.tok('dh0')

            def _(t, qt=qt):
                cx.wait(t, 't', tok_ptv, tok_v, prev['v_pv_free'], prev['s_pv_free'])
                last = None
                for kc in range(qt + 1):
                    t.ldweights(sb['pT'][:, kc, :])
                    last = t.matmul(ps['PV'][:, 0:512], sb['pT'][:, kc, :],
                                    sb['vtm'][:, kc, 0:512], start=(kc == 0), stop=(kc == qt))
                    last = t.matmul(ps['PV'][:, 512:1024], sb['pT'][:, kc, :],
                                    sb['vtm'][:, kc, 512:1024], start=(kc == 0), stop=(kc == qt))
                cx.inc('pe', last)
            block.tensor(_)
            tok_pv = cx.tok('pe')

            def _(v, qt=qt):
                cx.wait(v, 'v', tok_pv, tok_h0, tok_rec, prev['h1_store'])
                cx.inc('v', v.scalar_tensor_tensor(sb['h1t'][:, 0:512], ps['PV'][:, 0:512],
                                                   sb['rrec'][:, 0:1], sb['h0t'][:, 0:512],
                                                   ALU.mult, ALU.add))
                cx.wait(v, 'v', cx.tok('v'))
                cx.inc('v', v.scalar_tensor_tensor(sb['h1t'][:, 512:1024], ps['PV'][:, 512:1024],
                                                   sb['rrec'][:, 0:1], sb['h0t'][:, 512:1024],
                                                   ALU.mult, ALU.add))
            block.vector(_)
            prev['v_pv_free'] = cx.tok('v')
            prev['h0_used'] = cx.tok('v')
            tok_h1 = cx.tok('v')

            def _(y, qt=qt):
                cx.wait(y, 'y', tok_h1)
                cx.inc('dh1', y.dma_start(out=dram['h1'][base + qt * 128:base + (qt + 1) * 128, :],
                                          in_=sb['h1t'][:]), 16)
            block.sync(_)
            prev['h1_store'] = cx.tok('dh1')

        prev['pe_n1_readers'] = cx.tok('pe')
    return cx.tok('dh1')


def alloc_common(nc, E, cx, S):
    ST = S // 128
    sb, ps = {}, {}
    sb['n1fm'] = E(nc.sbuf_tensor("n1fm", [128, DT, S], BF16))
    sb['qfm'] = E(nc.sbuf_tensor("qfm", [128, DT, S], BF16))
    sb['vtm'] = E(nc.sbuf_tensor("vtm", [128, ST, D], BF16))
    sb['wqk'] = E(nc.sbuf_tensor("wqk", [128, DT, D], BF16))
    sb['wvt'] = E(nc.sbuf_tensor("wvt", [128, DT, D], BF16))
    sb['lnin'] = E(nc.sbuf_tensor("lnin", [128, D], F32))
    sb['lnout'] = E(nc.sbuf_tensor("lnout", [128, D], BF16))
    sb['srow'] = E(nc.sbuf_tensor("srow", [128, S], F32))
    sb['prow'] = E(nc.sbuf_tensor("prow", [128, S], BF16))
    sb['pT'] = E(nc.sbuf_tensor("pT", [128, ST, 128], BF16))
    sb['dwide'] = E(nc.sbuf_tensor("dwide", [128, 2176], F32))
    sb['mask'] = E(nc.sbuf_tensor("mask", [128, 128], F32))
    sb['kkrow'] = E(nc.sbuf_tensor("kkrow", [128, 128], F32))
    sb['qqcol'] = E(nc.sbuf_tensor("qqcol", [128, 1], F32))
    sb['ident'] = E(nc.sbuf_tensor("ident", [128, 128], BF16))
    sb['slope128'] = E(nc.sbuf_tensor("slope128", [128, 1], F32))
    sb['slope1'] = E(nc.sbuf_tensor("slope1", [1, 1], F32))
    sb['ones128'] = E(nc.sbuf_tensor("ones128", [1, 128], F32))
    sb['rmax'] = E(nc.sbuf_tensor("rmax", [128, 1], F32))
    sb['rsum'] = E(nc.sbuf_tensor("rsum", [128, 1], F32))
    sb['rrec'] = E(nc.sbuf_tensor("rrec", [128, 1], F32))
    sb['h0t'] = E(nc.sbuf_tensor("h0t", [128, D], F32))
    sb['h1t'] = E(nc.sbuf_tensor("h1t", [128, D], F32))
    for nm in ['ln_mean', 'ln_sq', 'ln_var', 'ln_rstd', 'ln_nb',
               'ln_meanb', 'ln_sqb', 'ln_varb', 'ln_rstdb', 'ln_nbb']:
        sb[nm] = E(nc.sbuf_tensor(nm, [128, 1], F32))
    sb['ln_sqs'] = E(nc.sbuf_tensor("ln_sqs", [128, D], F32))
    ps["A"] = E(nc.psum_tensor("psA", [128, 2048], F32))
    ps['PV'] = E(nc.psum_tensor("psPV", [128, 1024], F32))
    ps['T'] = E(nc.psum_tensor("psT", [128, 128], BF16))
    ps['B'] = E(nc.psum_tensor("psB", [128, 512], F32))
    return sb, ps


def emit_preamble(cx, block, sb, ps, ins):
    def _(g):
        g.iota(sb['dwide'][:], [[1, 2176]], base=-2048, channel_multiplier=-1,
               allow_small_or_imprecise_dtypes=True)
        g.iota(sb['kkrow'][:], [[1, 128]], base=0, channel_multiplier=0,
               allow_small_or_imprecise_dtypes=True)
        g.iota(sb['qqcol'][:], [[1, 1]], base=0, channel_multiplier=1,
               allow_small_or_imprecise_dtypes=True)
        g.memset(sb['ident'][:], 0.0)
        g.affine_select(sb['ident'][:], sb['ident'][:], [[-1, 128]], ALU.not_equal, 1.0,
                        base=0, channel_multiplier=1)
        g.memset(sb['ones128'][:], 1.0)
        cx.inc('g', g.memset(sb['prow'][:], 0.0))
        cx.inc('dw', g.dma_start(out=sb['wqk'][:],
                                 in_=ins['w_qk'].rearrange("(c p) o -> p c o", p=128)), 16)
        cx.inc('dw', g.dma_start(out=sb['wvt'][:],
                                 in_=ins['w_v_t'].rearrange("(c p) o -> p c o", p=128)), 16)
    block.gpsimd(_)

    def _(y):
        cx.inc('dw2', y.dma_start(out=sb['slope1'][:], in_=ins['alibi'][:, :]), 16)
    block.sync(_)
    tok_w = cx.tok('dw')
    tok_w2 = cx.tok('dw2')
    tok_g0 = cx.tok('g')

    def _(t):
        cx.wait(t, 't', tok_w, tok_w2, tok_g0)
        cx.inc('pe', t.matmul(ps['B'][:, 0:1], sb['ones128'][:, :], sb['slope1'][:, :],
                              start=True, stop=True))
    block.tensor(_)
    tok_sl = cx.tok('pe')

    def _(v):
        cx.wait(v, 'v', tok_sl)
        cx.inc('v', v.tensor_copy(sb['slope128'][:], ps['B'][:, 0:1]))
        cx.wait(v, 'v', cx.tok('g'))
        cx.inc('v', v.tensor_scalar(sb['mask'][:], sb['kkrow'][:], sb['qqcol'][:, 0:1], NEG,
                                    ALU.is_gt, ALU.mult))
        cx.inc('ptv', v.nop())
    block.vector(_)
    return cx.tok('v')






BIG = 1.0e6
OOBJ = 99999.0


def balanced_if(cx, engname, body):
    def _(eng):
        snap = dict(cx.cnt)
        wsnap = dict(cx.wc)
        r = cx.sbn[engname + '_lreg']
        with eng.If_lt(r, 256):
            body(eng)
        cx.wc = wsnap
        with eng.Else():
            for nm in list(cx.cnt):
                d = cx.cnt[nm] - snap.get(nm, 0)
                if d > 0:
                    if nm in cx.swdge_sems:
                        for _i in range(d // 16):
                            eng.dma_start(out=cx.dummy_dram[0:1, 0:1],
                                          in_=cx.dummy_sb[0:1, 0:1]).then_inc(cx.sem(nm), 16)
                    else:
                        eng.sem_inc(cx.sem(nm), d)
    return _


def emit_flag_write(cx, block, fid, shr, sbn, only_role=None):
    def _(g):
        if only_role is not None:
            with g.If_eq(sbn['g_rreg'], only_role):
                cx.inc('dfl', g.dma_start(out=shr['flags'][fid, only_role:only_role + 1, :],
                                          in_=sbn['noncebuf'][:]), 16)
            with g.Else():
                g.dma_start(out=cx.dummy_dram[0:1, 0:1],
                            in_=cx.dummy_sb[0:1, 0:1]).then_inc(cx.sem('dfl'), 16)
        else:
            with g.If_eq(sbn['g_rreg'], 0):
                cx.inc('dfl', g.dma_start(out=shr['flags'][fid, 0:1, :],
                                          in_=sbn['noncebuf'][:]), 16)
            with g.Else():
                g.dma_start(out=shr['flags'][fid, 1:2, :],
                            in_=sbn['noncebuf'][:]).then_inc(cx.sem('dfl'), 16)
        g.wait_ge(cx.sem('dfl'), cx.cnt['dfl'])
    block.gpsimd(_)


def emit_flag_poll(cx, block, fid, shr, sbn, only_role=None, from_role=None):
    def _(g):
        def poll_body():
            g.reg_mov(sbn['pr'], 1)
            with g.While(sbn['pr']):
                if from_role is not None:
                    g.dma_start(out=sbn['flagrd'][:], in_=shr['flags'][fid, from_role:from_role + 1, :]
                                ).then_inc(cx.sem('dpoll'), 16)
                else:
                    with g.If_eq(sbn['g_rreg'], 0):
                        g.dma_start(out=sbn['flagrd'][:], in_=shr['flags'][fid, 1:2, :]
                                    ).then_inc(cx.sem('dpoll'), 16)
                    with g.Else():
                        g.dma_start(out=sbn['flagrd'][:], in_=shr['flags'][fid, 0:1, :]
                                    ).then_inc(cx.sem('dpoll'), 16)
                g.reg_add(sbn['pc'], sbn['pc'], 16)
                g.wait_ge(cx.sem('dpoll'), sbn['pc'])
                g.reg_load(sbn['fv'], sbn['flagrd'][0:1, 0:1])
                g.reg_alu(sbn['pr'], sbn['fv'], sbn['nreg'], ALU.not_equal)
        if only_role is not None:
            with g.If_eq(sbn['g_rreg'], only_role):
                poll_body()
        else:
            poll_body()
        cx.inc('g', g.nop())
    block.gpsimd(_)
    return cx.tok('g')


def emit_moe(cx, block, sb, ps, shr, sbn, dram, S, NBC, cid):
    nc = cx.nc
    ST = S // 128
    BT = ST
    TOKC = NBC * S
    TOKI = 2 * TOKC
    NSLOT = NE * CAP
    hi_st = sb['vtm'][:, :, :].rearrange("p a b -> p (a b)")
    lo_st = sb['n1fm'][:, :, :].rearrange("p a b -> p (a b)")
    F_N2, F_TOT, F_TZ, F_T, F_YW = [cid * 5 + i for i in range(5)]

    def _(v):
        cx.wait(v, 'v', cx.tok('moe_prev_done'))
        cx.inc('v', v.memset(sb['n2f32'][:, :], 0.0))
    block.vector(_)
    tok_tz0 = cx.tok('v')

    def _(g):
        cx.wait(g, 'g', tok_tz0)
        half = NSLOT // 2
        zsrc = sb['n2f32'][:, 0:(half // 64)].rearrange("p (a w) -> p a w", w=2)
        with g.If_eq(sbn['g_rreg'], 0):
            cx.inc('dsh', g.dma_start(
                out=shr['Tsh'][0:half, :].rearrange("(a p) w -> p a w", p=128),
                in_=zsrc), 16)
        with g.Else():
            g.dma_start(
                out=shr['Tsh'][half:NSLOT, :].rearrange("(a p) w -> p a w", p=128),
                in_=zsrc).then_inc(cx.sem('dsh'), 16)
        g.wait_ge(cx.sem('dsh'), cx.cnt['dsh'])
    block.gpsimd(_)
    emit_flag_write(cx, block, F_TZ, shr, sbn)

    st_used = None
    for b in range(NBC):
        base = b * S
        for tt in range(ST):
            pb = (b * ST + tt) % 2
            srcb = sb['lnin'] if pb == 0 else sb['lnin_b']
            dstb = sb['n2f32'] if pb == 0 else sb['n2f32b']
            kfree = ('m_lnfree', max(0, cx.cnt.get('m_lnfree', 0) - 1))
            dfree = ('m_lofree', max(0, cx.cnt.get('m_lofree', 0) - 1))

            def _(y, tt=tt, b=b, srcb=srcb, kfree=kfree):
                cx.wait(y, 'y', kfree)
                cx.inc('dln', y.dma_start(out=srcb[:],
                                          in_=dram['hin'][base + tt * 128:base + (tt + 1) * 128, :]), 16)
            block.sync(_)
            tok_ld = cx.tok('dln')
            tok_sw = cx.tok('dn1')

            def _(v, kfree=kfree, dfree=dfree):
                cx.wait(v, 'v', tok_ld, st_used, tok_sw, kfree, dfree)
            block.vector(_)

            def _(s2, kfree=kfree, dfree=dfree):
                cx.wait(s2, 's', tok_ld, st_used, tok_sw, kfree, dfree)
            block.scalar(_)
            tok_ln = emit_ln_tile(cx, block, sb, srcb, dstb, dfree,
                                  scr=('' if pb == 0 else 'b'))

            def _(s2, tt=tt, dstb=dstb):
                cx.wait(s2, 's', tok_ln)
                cx.inc('s', s2.activation(out=hi_st[:, tt * D:(tt + 1) * D], in_=dstb[:],
                                          func=AF.Identity))
                cx.inc('m_lnfree', s2.nop())
            block.scalar(_)
            tok_hi = cx.tok('s')

            def _(v, tt=tt, dstb=dstb):
                cx.wait(v, 'v', tok_hi)
                cx.inc('m_lofree', v.tensor_tensor(lo_st[:, tt * D:(tt + 1) * D], dstb[:],
                                                   hi_st[:, tt * D:(tt + 1) * D], ALU.subtract))
            block.vector(_)
        tok_stage = cx.tok('m_lofree')
        tok_stage_s = cx.tok('s')

        def _(y, b=b):
            cx.wait(y, 'y', tok_stage, tok_stage_s)
            cx.inc('dn1', y.dma_start(
                out=dram['n2l'].rearrange("(a p) d -> p a d", p=128)[:, b * BT:(b + 1) * BT, :],
                in_=sb['vtm'][:, :, :]), 16)
            cx.inc('dn1', y.dma_start(
                out=dram['n2lo'].rearrange("(a p) d -> p a d", p=128)[:, b * BT:(b + 1) * BT, :],
                in_=sb['n1fm'][:, :, :].rearrange("p a b -> p (a b)").rearrange(
                    "p (a d) -> p a d", d=D)), 16)
        block.sync(_)

        def _(g, b=b):
            cx.wait(g, 'g', tok_stage, tok_stage_s)
            with g.If_eq(sbn['g_rreg'], 0):
                cx.inc('dsh', g.dma_start(
                    out=shr['n2sh'].rearrange("(a p) d -> p a d", p=128)[:, b * BT:(b + 1) * BT, :],
                    in_=sb['vtm'][:, :, :]), 16)
            with g.Else():
                g.dma_start(
                    out=shr['n2sh'].rearrange("(a p) d -> p a d", p=128)[:, (NBC + b) * BT:(NBC + b + 1) * BT, :],
                    in_=sb['vtm'][:, :, :]).then_inc(cx.sem('dsh'), 16)
            cx.inc('m_stfree', g.nop())
            g.wait_ge(cx.sem('dsh'), cx.cnt['dsh'] - 16)
        block.gpsimd(_)
        st_used = cx.tok('m_stfree')

    def _(g):
        cx.wait(g, 'g', cx.tok('dn1'))
        g.wait_ge(cx.sem('dsh'), cx.cnt['dsh'])
    block.gpsimd(_)
    emit_flag_write(cx, block, F_N2, shr, sbn)

    def seg_bc2(ap16):
        return ap16.unsqueeze(2).broadcast_to([128, ST, 64])

    prev_gmm = None
    for b in range(NBC):
        base = b * S

        def _(y, b=b):
            cx.wait(y, 'y', cx.tok('dn1'), cx.tok('m_gfree'), prev_gmm)
            for dc in range(DT):
                cx.inc('dtp', y.dma_start(out=sb['qfm'][:, dc, 0:S],
                                          in_=dram['n2l'][base:base + S, dc * 128:(dc + 1) * 128],
                                          transpose=True), 16)
                cx.inc('dtp', y.dma_start(out=sb['n1fm'][:, dc, 0:S],
                                          in_=dram['n2lo'][base:base + S, dc * 128:(dc + 1) * 128],
                                          transpose=True), 16)
        block.sync(_)
        tok_fm = cx.tok('dtp')

        for tt in range(ST):
            def _(t, tt=tt):
                cx.wait(t, 't', tok_fm, cx.tok('m_psB_free'))
                last = None
                for di in range(DT):
                    t.ldweights(sb['qfm'][:, di, tt * 128:(tt + 1) * 128])
                    last = t.matmul(ps['B'][:, 0:128], sb['qfm'][:, di, tt * 128:(tt + 1) * 128],
                                    sb['gwcat'][:, di, :], start=(di == 0), stop=False)
                for di in range(DT):
                    t.ldweights(sb['n1fm'][:, di, tt * 128:(tt + 1) * 128])
                    last = t.matmul(ps['B'][:, 0:128], sb['n1fm'][:, di, tt * 128:(tt + 1) * 128],
                                    sb['gwcat2'][:, di, :], start=False, stop=(di == DT - 1))
                cx.inc('pe', last)
            block.tensor(_)
            tok_gmm = cx.tok('pe')

            def _(s2, tt=tt):
                cx.wait(s2, 's', tok_gmm)
                cx.inc('s', s2.activation(out=sb['gsc'][:, :], in_=ps['B'][:, 64:128],
                                          func=AF.Identity))
            block.scalar(_)
            tok_evg = cx.tok('s')

            def _(v, tt=tt):
                cx.wait(v, 'v', tok_evg)
                cx.inc('m_psB_free', v.tensor_tensor(sb['lgb'][:, tt, :], ps['B'][:, 0:64],
                                                     sb['gsc'][:, :], ALU.add))
            block.vector(_)
        prev_gmm = cx.tok('m_psB_free')

        def _(v, b=b):
            cx.wait(v, 'v', prev_gmm)
            cx.inc('v', v.tensor_reduce(sb['m16'][:, :], sb['lgb'][:, :, :], axis=AX.X, op=ALU.max))
            cx.wait(v, 'v', cx.tok('v'))
            cx.inc('v', v.tensor_tensor(sb['prb'][:, :, :], sb['lgb'][:, :, :],
                                        seg_bc2(sb['m16'][:, :]), ALU.subtract))
        block.vector(_)
        tok_sm1 = cx.tok('v')

        def _(s2):
            cx.wait(s2, 's', tok_sm1)
            cx.inc('s', s2.activation(out=sb['prb'][:, :, :], in_=sb['prb'][:, :, :], func=AF.Exp))
        block.scalar(_)
        tok_expb = cx.tok('s')

        def _(v, b=b):
            cx.wait(v, 'v', tok_expb)
            cx.inc('v', v.tensor_reduce(sb['m16'][:, :], sb['prb'][:, :, :], axis=AX.X, op=ALU.add))
            cx.wait(v, 'v', cx.tok('v'))
            cx.inc('v', v.reciprocal(sb['r16'][:, :], sb['m16'][:, :]))
            cx.wait(v, 'v', cx.tok('v'))
            cx.inc('v', v.tensor_tensor(sb['prb'][:, :, :], sb['prb'][:, :, :],
                                        seg_bc2(sb['r16'][:, :]), ALU.mult))
            cx.wait(v, 'v', cx.tok('v'))
            cx.inc('v', v.tensor_copy(sb['prc'][:, :, :], sb['prb'][:, :, :]))
            for k in range(TOPK):
                cx.wait(v, 'v', cx.tok('v'))
                cx.inc('v', v.tensor_reduce(sb['m16'][:, :], sb['prc'][:, :, :], axis=AX.X,
                                            op=ALU.max))
                cx.wait(v, 'v', cx.tok('v'))
                cx.inc('v', v.tensor_tensor(sb['msk'][:, :, :], sb['prc'][:, :, :],
                                            seg_bc2(sb['m16'][:, :]), ALU.not_equal))
                cx.wait(v, 'v', cx.tok('v'))
                cx.inc('v', v.tensor_tensor(sb['prc'][:, :, :], sb['prc'][:, :, :],
                                            sb['msk'][:, :, :], ALU.mult))
            cx.wait(v, 'v', cx.tok('v'))
            cx.inc('v', v.tensor_tensor(sb['Hb'][:, :, :], sb['prb'][:, :, :], sb['prc'][:, :, :],
                                        ALU.not_equal))
            cx.wait(v, 'v', cx.tok('v'))
            cx.inc('v', v.tensor_tensor(sb['whb'][:, :, :], sb['prb'][:, :, :], sb['Hb'][:, :, :],
                                        ALU.mult))
            cx.wait(v, 'v', cx.tok('v'))
            cx.inc('v', v.tensor_reduce(sb['m16'][:, :], sb['whb'][:, :, :], axis=AX.X, op=ALU.add))
            cx.wait(v, 'v', cx.tok('v'))
            cx.inc('v', v.reciprocal(sb['r16'][:, :], sb['m16'][:, :]))
            cx.wait(v, 'v', cx.tok('v'))
            cx.inc('v', v.tensor_tensor(sb['whb'][:, :, :], sb['whb'][:, :, :],
                                        seg_bc2(sb['r16'][:, :]), ALU.mult))
            cx.wait(v, 'v', cx.tok('v'))
            cx.inc('v', v.tensor_copy(sb['Hbb'][:, :, :], sb['Hb'][:, :, :]))
        block.vector(_)
        tok_top = cx.tok('v')

        def _(y, b=b):
            cx.wait(y, 'y', tok_top)
            cx.inc('dHW', y.dma_start(
                out=dram['H'].rearrange("(a p) e -> p a e", p=128)[:, b * BT:(b + 1) * BT, :],
                in_=sb['Hbb'][:, :, :]), 16)
            cx.inc('dHW', y.dma_start(
                out=dram['Wh'].rearrange("(a p) e -> p a e", p=128)[:, b * BT:(b + 1) * BT, :],
                in_=sb['whb'][:, :, :]), 16)
            cx.inc('m_gfree', y.nop())
        block.sync(_)

    def _(v):
        cx.wait(v, 'v', cx.tok('dHW'))
        cx.inc('v', v.memset(sb['Osb'][:, :], 0.0))
        cx.inc('v', v.memset(sb['Osb2'][:, :], 0.0))
        cx.wait(v, 'v', cx.tok('v'))
        cx.inc('v', v.memset(sb['carry'][:, :], 0.0))
    block.vector(_)
    tok_oz = cx.tok('v')

    emit_flag_poll(cx, block, F_TOT, shr, sbn, only_role=1, from_role=0)

    def _(g):
        cx.wait(g, 'g', tok_oz)
        with g.If_ne(sbn['g_rreg'], 0):
            cx.inc('dto', g.dma_start(out=sb['Osb2'][0:1, :], in_=shr['tot'][cid, 0:1, :]), 16)
        with g.Else():
            g.dma_start(out=cx.dummy_dram[0:1, 0:1],
                        in_=cx.dummy_sb[0:1, 0:1]).then_inc(cx.sem('dto'), 16)
    block.gpsimd(_)
    tok_oload = cx.tok('dto')

    def _(v):
        cx.wait(v, 'v', tok_oload)
        cx.inc('v', v.scalar_tensor_tensor(sb['Osb'][0:1, :], sb['Osb2'][0:1, :],
                                           sb['rolef1'][0:1, 0:1],
                                           sb['Osb'][0:1, :], ALU.mult, ALU.add))
    block.vector(_)
    tok_oinit = cx.tok('v')

    for b in range(NBC):
        def _(y, b=b):
            cx.wait(y, 'y', cx.tok('m_cfree'), cx.tok('m_gfree'))
            cx.inc('dHl', y.dma_start(
                out=sb['Hbb'][:, :, :],
                in_=dram['H'].rearrange("(a p) e -> p a e", p=128)[:, b * BT:(b + 1) * BT, :]), 16)
        block.sync(_)
        tok_hl = cx.tok('dHl')

        def _(v, b=b):
            cx.wait(v, 'v', tok_oinit, cx.tok('m_carry'))
            cx.inc('v', v.tensor_reduce(sb['fmin'][0:1, 0:1], sb['Osb'][0:1, :], axis=AX.X,
                                        op=ALU.min))
            cx.wait(v, 'v', cx.tok('v'))
            cx.inc('v', v.tensor_copy(sb['flagbuf_i'][0:1, b + 16 * cid:b + 16 * cid + 1],
                                      sb['fmin'][0:1, 0:1]))
            cx.wait(v, 'v', cx.tok('v'))
            cx.inc('v', v.memset(sb['carry'][0:1, :], 0.0))
        block.vector(_)
        tok_fl = cx.tok('v')

        for tt in range(ST):
            def _(t, tt=tt, b=b):
                cx.wait(t, 't', tok_fl, tok_hl, cx.tok('m_cevac'), cx.tok('m_carry'))
                t.matmul(ps['B'][:, 0:64], sb['tri'][:, :], sb['Hbb'][:, tt, :],
                         start=True, stop=False)
                if tt > 0:
                    t.matmul(ps['B'][:, 0:64], sb['ones128'][0:1, :], sb['carry'][0:1, :],
                             start=False, stop=False)
                last = t.matmul(ps['B'][:, 0:64], sb['ones128'][0:1, :], sb['Osb'][0:1, :],
                                start=False, stop=True)
                cx.inc('pe', last)
                last = t.matmul(ps['PV'][0:1, 0:64], sb['onescol'][:, 0:1], sb['Hbb'][:, tt, :],
                                start=True, stop=True)
                cx.inc('pe', last)
            block.tensor(_)
            tok_cs = cx.tok('pe')

            def _(v, tt=tt, b=b):
                cx.wait(v, 'v', tok_cs)
                cx.inc('m_cevac', v.tensor_copy(sb['Cb'][:, tt, :], ps['B'][:, 0:64]))
                cx.wait(v, 'v', cx.tok('m_cevac'))
                cx.inc('m_carry', v.tensor_tensor(sb['carry'][0:1, :], sb['carry'][0:1, :],
                                                  ps['PV'][0:1, 0:64], ALU.add))
            block.vector(_)

        def _(v, b=b):
            cx.wait(v, 'v', cx.tok('m_carry'))
            cx.inc('v', v.tensor_tensor(sb['Osb'][0:1, :], sb['Osb'][0:1, :], sb['carry'][0:1, :],
                                        ALU.add))
        block.vector(_)

        def _(y, b=b):
            cx.wait(y, 'y', cx.tok('m_cevac'))
            cx.inc('dC', y.dma_start(
                out=dram['C'].rearrange("(a p) e -> p a e", p=128)[:, b * BT:(b + 1) * BT, :],
                in_=sb['Cb'][:, :, :]), 16)
            cx.inc('m_cfree', y.nop())
        block.sync(_)

    def _(v):
        cx.wait(v, 'v', cx.tok('v'))
        cx.inc('v', v.tensor_copy(sb['Osb2'][0:1, :], sb['Osb'][0:1, :]))
    block.vector(_)
    tok_totc = cx.tok('v')

    def _(g):
        cx.wait(g, 'g', tok_totc)
        with g.If_eq(sbn['g_rreg'], 0):
            cx.inc('dsh', g.dma_start(out=shr['tot'][cid, 0:1, :], in_=sb['Osb2'][0:1, :]), 16)
        with g.Else():
            g.dma_start(out=cx.dummy_dram[0:1, 0:1],
                        in_=cx.dummy_sb[0:1, 0:1]).then_inc(cx.sem('dsh'), 16)
        g.wait_ge(cx.sem('dsh'), cx.cnt['dsh'])
    block.gpsimd(_)
    emit_flag_write(cx, block, F_TOT, shr, sbn, only_role=0)
    emit_flag_poll(cx, block, F_TZ, shr, sbn)

    tok_flags = cx.tok('v')
    for b in range(NBC):
        fslot = b + 16 * cid
        for en, ek in (('vector', 'v'), ('gpsimd', 'g'), ('sync', 'y')):
            def _(eng, ek=ek, fslot=fslot):
                cx.wait(eng, ek, tok_flags)
                eng.reg_load(cx.sbn[ek + '_lreg'], sb['flagbuf_i'][0:1, fslot:fslot + 1])
            getattr(block, en)(_)

        def ybody(y, b=b):
            cx.wait(y, 'y', cx.tok('m_xfree'), cx.tok('m_cfree'))
            cx.inc('dCl', y.dma_start(
                out=sb['Cb'][:, :, :],
                in_=dram['C'].rearrange("(a p) e -> p a e", p=128)[:, b * BT:(b + 1) * BT, :]), 16)
            cx.inc('dCl', y.dma_start(
                out=sb['whb'][:, :, :],
                in_=dram['Wh'].rearrange("(a p) e -> p a e", p=128)[:, b * BT:(b + 1) * BT, :]), 16)
        block.sync(balanced_if(cx, 'y', ybody))
        tok_cl = cx.tok('dCl')

        def gbody(g, b=b):
            cx.wait(g, 'g', cx.tok('m_ilfree'))
            cx.inc('dHx', g.dma_start(
                out=sb['Hb'][:, :, :],
                in_=dram['H'].rearrange("(a p) e -> p a e", p=128)[:, b * BT:(b + 1) * BT, :]), 16)
            cx.inc('g', g.iota(sb['ilocb'][:, :], [[128, ST]], base=b * S + 1,
                               channel_multiplier=1, allow_small_or_imprecise_dtypes=True))
        block.gpsimd(balanced_if(cx, 'g', gbody))
        tok_il = cx.tok('g')
        tok_hx = cx.tok('dHx')

        def vbody(v, b=b):
            cx.wait(v, 'v', tok_cl, tok_il, tok_hx, cx.tok('m_scat'))
            cx.inc('v', v.tensor_copy(sb['prc'][:, :, :], sb['Hb'][:, :, :]))
            cx.wait(v, 'v', cx.tok('v'))
            cx.inc('v', v.tensor_scalar(sb['ilocb'][:, :], sb['ilocb'][:, :],
                                        sb['rolecol'][:, 0:1], None, ALU.add))
            for k in range(TOPK):
                cx.wait(v, 'v', cx.tok('v'))
                cx.inc('v', v.tensor_tensor(sb['msk'][:, :, :], sb['prc'][:, :, :],
                                            sb['iotaMB'][:, :].unsqueeze(1).broadcast_to([128, ST, 64]),
                                            ALU.mult))
                cx.wait(v, 'v', cx.tok('v'))
                cx.inc('v', v.tensor_reduce(sb['m16'][:, :], sb['msk'][:, :, :], axis=AX.X,
                                            op=ALU.min))
                cx.wait(v, 'v', cx.tok('v'))
                cx.inc('v', v.tensor_tensor(sb['eqm'][:, :, :], sb['msk'][:, :, :],
                                            seg_bc2(sb['m16'][:, :]), ALU.is_equal))
                cx.wait(v, 'v', cx.tok('v'))
                cx.inc('v', v.tensor_tensor(sb['prc'][:, :, :], sb['prc'][:, :, :],
                                            sb['eqm'][:, :, :], ALU.subtract))
                cx.wait(v, 'v', cx.tok('v'))
                cx.inc('v', v.tensor_scalar(sb['ekb'][:, :, k], sb['m16'][:, :], BIG, None,
                                            ALU.add))
                cx.wait(v, 'v', cx.tok('v'))
                cx.inc('v', v.tensor_tensor(sb['msk'][:, :, :], sb['eqm'][:, :, :],
                                            sb['Cb'][:, :, :], ALU.mult))
                cx.wait(v, 'v', cx.tok('v'))
                cx.inc('v', v.tensor_reduce(sb['posb'][:, :, k], sb['msk'][:, :, :], axis=AX.X,
                                            op=ALU.add))
                cx.wait(v, 'v', cx.tok('v'))
                cx.inc('v', v.tensor_tensor(sb['msk'][:, :, :], sb['eqm'][:, :, :],
                                            sb['whb'][:, :, :], ALU.mult))
                cx.wait(v, 'v', cx.tok('v'))
                cx.inc('v', v.tensor_reduce(sb['pay'][:, :, k, 1], sb['msk'][:, :, :], axis=AX.X,
                                            op=ALU.add))
            cx.wait(v, 'v', cx.tok('v'))
            cx.inc('v', v.scalar_tensor_tensor(sb['jfb'][:, :, :], sb['ekb'][:, :, :], float(CAP),
                                               sb['posb'][:, :, :], ALU.mult, ALU.add))
            cx.wait(v, 'v', cx.tok('v'))
            cx.inc('v', v.tensor_scalar(sb['eqm'][:, :, 0:TOPK], sb['posb'][:, :, :], float(CAP),
                                        None, ALU.is_ge))
            cx.wait(v, 'v', cx.tok('v'))
            cx.inc('v', v.scalar_tensor_tensor(sb['jfb'][:, :, :], sb['eqm'][:, :, 0:TOPK],
                                               OOBJ, sb['jfb'][:, :, :], ALU.mult, ALU.add))
            cx.wait(v, 'v', cx.tok('v'))
            cx.inc('v', v.tensor_copy(sb['jib'][:, :, :], sb['jfb'][:, :, :]))
            for k in range(TOPK):
                cx.wait(v, 'v', cx.tok('v'))
                cx.inc('v', v.tensor_copy(sb['pay'][:, :, k, 0], sb['ilocb'][:, :]))
        block.vector(balanced_if(cx, 'v', vbody))
        tok_x = cx.tok('v')

        def ybody2(y, b=b):
            cx.wait(y, 'y', tok_x)
            cx.inc('dJ', y.dma_start(
                out=dram['jd'].rearrange("(a p) k -> p a k", p=128)[:, b * BT:(b + 1) * BT, :],
                in_=sb['jib'][:, :, :]), 16)
            cx.inc('m_xfree', y.nop())
        block.sync(balanced_if(cx, 'y', ybody2))

        def gbody2(g, b=b):
            cx.wait(g, 'g', tok_x)
            for tt in range(ST):
                for k in range(TOPK):
                    cx.inc('dTs', g.indirect_dma_start(
                        shr['Tsh'][:, :],
                        IndirectOffsetOnAxis(sb['jib'][:, tt, k].unsqueeze(1), 0),
                        sb['pay'][:, tt, k, :], None,
                        bounds_check=sbn['bnd_slot'], oob_is_err=False), 16)
            cx.inc('m_ilfree', g.nop())
            cx.inc('m_scat', g.nop())
        block.gpsimd(balanced_if(cx, 'g', gbody2))

        def _(g):
            g.wait_ge(cx.sem('dTs'), cx.cnt['dTs'])
            g.wait_ge(cx.sem('dHx'), cx.cnt['dHx'])
        block.gpsimd(_)

    def _(g):
        g.wait_ge(cx.sem('dTs'), cx.cnt['dTs'])
    block.gpsimd(_)
    emit_flag_write(cx, block, F_T, shr, sbn)
    emit_flag_poll(cx, block, F_T, shr, sbn)
    tok_tbar = cx.tok('g')

    def _(v):
        cx.wait(v, 'v', tok_tbar)
        cx.inc('v', v.memset(sb['dtm'][:, :, :], 0.0))
    block.vector(_)
    tok_dz = cx.tok('v')

    def _(y):
        cx.wait(y, 'y', cx.tok('dC'))
        cx.inc('dwg', y.dma_start(out=sb['wgux'][:, :, 0:INTER],
                                  in_=dram['wg'].rearrange("e (c p) i -> e p c i", p=128)[0]), 16)
        cx.inc('dwg', y.dma_start(out=sb['wgux'][:, :, INTER:2 * INTER],
                                  in_=dram['wu'].rearrange("e (c p) i -> e p c i", p=128)[0]), 16)
        cx.inc('dwd', y.dma_start(out=sb['wdx'][:, :, :],
                                  in_=dram['wd'].rearrange("e (c p) h -> e p c h", p=128)[0]), 16)
    block.sync(_)

    for e in range(NEC):
        def _(g, e=e):
            cx.wait(g, 'g', tok_tbar, cx.tok('m_tsl'))
            with g.If_eq(sbn['g_rreg'], 0):
                cx.inc('dtl', g.dma_start(
                    out=sb['tsl'][:, :, :],
                    in_=shr['Tsh'].rearrange("(s p) w -> p s w", p=128)[:, e * 2:(e + 1) * 2, :]), 16)
            with g.Else():
                g.dma_start(
                    out=sb['tsl'][:, :, :],
                    in_=shr['Tsh'].rearrange("(s p) w -> p s w", p=128)[:, (NEC + e) * 2:(NEC + e + 1) * 2, :]
                ).then_inc(cx.sem('dtl'), 16)
        block.gpsimd(_)
        tok_tl = cx.tok('dtl')

        def _(v, e=e):
            cx.wait(v, 'v', tok_tl, tok_dz, cx.tok('m_dtm_free'))
            cx.inc('v', v.tensor_scalar(sb['eqz'][:, :], sb['tsl'][:, :, 0], 0.0, None,
                                        ALU.is_equal))
            cx.wait(v, 'v', cx.tok('v'))
            cx.inc('v', v.tensor_scalar(sb['idxf'][:, :], sb['tsl'][:, :, 0], 1.0, None,
                                        ALU.subtract))
            cx.wait(v, 'v', cx.tok('v'))
            cx.inc('v', v.scalar_tensor_tensor(sb['idxf'][:, :], sb['eqz'][:, :], BIG,
                                               sb['idxf'][:, :], ALU.mult, ALU.add))
            cx.wait(v, 'v', cx.tok('v'))
            cx.inc('v', v.tensor_copy(sb['idxi'][:, :], sb['idxf'][:, :]))
        block.vector(_)
        tok_idx = cx.tok('v')

        def _(g, e=e):
            cx.wait(g, 'g', tok_idx)
            for c in range(2):
                cx.inc('dgx', g.indirect_dma_start(
                    sb['dtm'][:, c, :], None,
                    shr['n2sh'][:, :],
                    IndirectOffsetOnAxis(sb['idxi'][:, c].unsqueeze(1), 0),
                    bounds_check=sbn['bnd_tok'], oob_is_err=False), 16)
            cx.inc('m_tsl', g.nop())
        block.gpsimd(_)
        tok_gx = cx.tok('dgx')

        for c in range(2):
            for dc in range(DT):
                def _(t, c=c, dc=dc):
                    cx.wait(t, 't', tok_gx, cx.tok('m_dfm_ev'))
                    cx.inc('pe', t.transpose(ps['T'][:, :],
                                             sb['dtm'][:, c, dc * 128:(dc + 1) * 128],
                                             sb['ident'][:, :]))
                block.tensor(_)
                tok_tp = cx.tok('pe')

                def _(v, c=c, dc=dc):
                    cx.wait(v, 'v', tok_tp)
                    cx.inc('m_dfm_ev', v.tensor_copy(sb['dfm'][:, dc, c * 128:(c + 1) * 128],
                                                     ps['T'][:, :]))
                block.vector(_)
        tok_dfm = cx.tok('m_dfm_ev')

        def _(v, e=e):
            cx.wait(v, 'v', tok_dfm)
            cx.inc('m_dtm_free', v.nop())
        block.vector(_)

        for sc in range(2):
            def _(t, sc=sc, e=e):
                cx.wait(t, 't', tok_dfm, cx.tok('dwg'), cx.tok('m_gu_ev'))
                last = None
                for di in range(DT):
                    t.ldweights(sb['dfm'][:, di, sc * 128:(sc + 1) * 128])
                    for (w0, w1, p0) in ((0, 512, 0), (512, 768, 512),
                                         (768, 1280, 1024), (1280, 1536, 1536)):
                        last = t.matmul(ps['A'][:, p0:p0 + (w1 - w0)],
                                        sb['dfm'][:, di, sc * 128:(sc + 1) * 128],
                                        sb['wgux'][:, di, w0:w1],
                                        start=(di == 0), stop=(di == DT - 1))
                cx.inc('pe', last)
            block.tensor(_)
            tok_gu = cx.tok('pe')

            def _(s2, sc=sc):
                cx.wait(s2, 's', tok_gu)
                cx.inc('s', s2.activation(out=sb['gsl'][:, :], in_=ps['A'][:, 0:INTER],
                                          func=AF.Silu))
            block.scalar(_)
            tok_sil = cx.tok('s')

            def _(v, sc=sc):
                cx.wait(v, 'v', tok_sil)
                cx.inc('m_gu_ev', v.tensor_tensor(
                    sb['gusl'][:, sc, :],
                    ps['A'][:, 1024:1024 + INTER].rearrange("p (a b) -> p a b", a=2).rearrange(
                        "p a b -> p (a b)") if False else ps['A'][:, 1024:1024 + INTER],
                    sb['gsl'][:, :], ALU.mult))
            block.vector(_)
        tok_gud = cx.tok('m_gu_ev')

        if e + 1 < NEC:
            def _(y, e=e):
                cx.wait(y, 'y', tok_gud)
                cx.inc('dwg', y.dma_start(
                    out=sb['wgux'][:, :, 0:INTER],
                    in_=dram['wg'].rearrange("e (c p) i -> e p c i", p=128)[e + 1]), 16)
                cx.inc('dwg', y.dma_start(
                    out=sb['wgux'][:, :, INTER:2 * INTER],
                    in_=dram['wu'].rearrange("e (c p) i -> e p c i", p=128)[e + 1]), 16)
            block.sync(_)

        for sc in range(2):
            for ic in range(INTER // 128):
                def _(t, sc=sc, ic=ic):
                    cx.wait(t, 't', tok_gud, cx.tok('m_gufm_ev'))
                    cx.inc('pe', t.transpose(ps['T'][:, :],
                                             sb['gusl'][:, sc, ic * 128:(ic + 1) * 128],
                                             sb['ident'][:, :]))
                block.tensor(_)
                tok_tp2 = cx.tok('pe')

                def _(v, sc=sc, ic=ic):
                    cx.wait(v, 'v', tok_tp2)
                    cx.inc('m_gufm_ev', v.tensor_copy(sb['gufm'][:, ic, sc * 128:(sc + 1) * 128],
                                                      ps['T'][:, :]))
                block.vector(_)
        tok_gufm = cx.tok('m_gufm_ev')

        for sc in range(2):
            def _(t, sc=sc, e=e):
                cx.wait(t, 't', tok_gufm, cx.tok('dwd'), cx.tok('m_y_ev'))
                last = None
                for ic in range(INTER // 128):
                    t.ldweights(sb['gufm'][:, ic, sc * 128:(sc + 1) * 128])
                    last = t.matmul(ps['PV'][:, 0:512], sb['gufm'][:, ic, sc * 128:(sc + 1) * 128],
                                    sb['wdx'][:, ic, 0:512],
                                    start=(ic == 0), stop=(ic == INTER // 128 - 1))
                    last = t.matmul(ps['PV'][:, 512:1024], sb['gufm'][:, ic, sc * 128:(sc + 1) * 128],
                                    sb['wdx'][:, ic, 512:1024],
                                    start=(ic == 0), stop=(ic == INTER // 128 - 1))
                cx.inc('pe', last)
            block.tensor(_)
            tok_dn = cx.tok('pe')

            def _(v, sc=sc):
                cx.wait(v, 'v', tok_dn, tok_idx)
                cx.inc('m_y_ev', v.tensor_scalar(sb['ytm'][:, sc, :], ps['PV'][:, 0:1024],
                                                 sb['tsl'][:, sc, 1:2], None, ALU.mult))
            block.vector(_)
        tok_y = cx.tok('m_y_ev')

        if e + 1 < NEC:
            def _(y, e=e):
                cx.wait(y, 'y', tok_y)
                cx.inc('dwd', y.dma_start(
                    out=sb['wdx'][:, :, :],
                    in_=dram['wd'].rearrange("e (c p) h -> e p c h", p=128)[e + 1]), 16)
            block.sync(_)

        def _(g, e=e):
            cx.wait(g, 'g', tok_y)
            with g.If_eq(sbn['g_rreg'], 0):
                cx.inc('dyw', g.dma_start(
                    out=shr['Ywsh'].rearrange("(s p) d -> p s d", p=128)[:, e * 2:(e + 1) * 2, :],
                    in_=sb['ytm'][:, :, :]), 16)
            with g.Else():
                g.dma_start(
                    out=shr['Ywsh'].rearrange("(s p) d -> p s d", p=128)[:, (NEC + e) * 2:(NEC + e + 1) * 2, :],
                    in_=sb['ytm'][:, :, :]).then_inc(cx.sem('dyw'), 16)
            cx.inc('m_yfree', g.nop())
            g.wait_ge(cx.sem('dyw'), cx.cnt['dyw'] - 16)
            g.wait_ge(cx.sem('dgx'), cx.cnt['dgx'])
            g.wait_ge(cx.sem('dtl'), cx.cnt['dtl'])
        block.gpsimd(_)

        def _(v, e=e):
            cx.wait(v, 'v', cx.tok('m_yfree'))
        block.vector(_)

    def _(g):
        g.wait_ge(cx.sem('dyw'), cx.cnt['dyw'])
    block.gpsimd(_)
    emit_flag_write(cx, block, F_YW, shr, sbn)
    emit_flag_poll(cx, block, F_YW, shr, sbn)
    tok_ywbar = cx.tok('g')

    for b in range(NBC):
        fslot = b + 16 * cid
        for en, ek in (('vector', 'v'), ('gpsimd', 'g'), ('sync', 'y')):
            def _(eng, ek=ek, fslot=fslot):
                cx.wait(eng, ek, tok_ywbar)
                eng.reg_load(cx.sbn[ek + '_lreg'], sb['flagbuf_i'][0:1, fslot:fslot + 1])
            getattr(block, en)(_)

        for tt in range(ST):
            trow = b * S + tt * 128

            def _(y, trow=trow):
                cx.wait(y, 'y', cx.tok('m_h1used'))
                cx.inc('dJl', y.dma_start(out=sb['h0t'][:, :],
                                          in_=dram['hin'][trow:trow + 128, :]), 16)
            block.sync(_)

            def ybody3(y, trow=trow):
                cx.inc('dJl', y.dma_start(out=sb['jib'][:, 0, :],
                                          in_=dram['jd'][trow:trow + 128, :]), 16)
            block.sync(balanced_if(cx, 'y', ybody3))
            tok_jl = cx.tok('dJl')

            def vbody3(v):
                cx.wait(v, 'v', cx.tok('m_kb_used'))
                cx.inc('v', v.memset(sb['kbuf'][:, :, :], 0.0))
            block.vector(balanced_if(cx, 'v', vbody3))
            tok_kz = cx.tok('v')

            def gbody3(g, trow=trow):
                cx.wait(g, 'g', tok_jl, tok_kz)
                for k in range(TOPK):
                    cx.inc('dKg', g.indirect_dma_start(
                        sb['kbuf'][:, k, :], None,
                        shr['Ywsh'][:, :],
                        IndirectOffsetOnAxis(sb['jib'][:, 0, k].unsqueeze(1), 0),
                        bounds_check=sbn['bnd_slot'], oob_is_err=False), 16)
                cx.inc('m_kgat', g.nop())
            block.gpsimd(balanced_if(cx, 'g', gbody3))

            def _(g):
                g.wait_ge(cx.sem('dKg'), cx.cnt['dKg'])
            block.gpsimd(_)
            tok_kg = cx.tok('dKg')
            tok_kgat = cx.tok('m_kgat')

            def _(v, trow=trow):
                cx.wait(v, 'v', tok_kg, tok_kgat, cx.tok('dJl'))
                snap = dict(cx.cnt)
                wsnap = dict(cx.wc)
                with v.If_lt(cx.sbn['v_lreg'], 256):
                    cx.inc('v', v.tensor_tensor(sb['kb4'][:, 0, :], sb['kbuf'][:, 0, :],
                                                sb['kbuf'][:, 1, :], ALU.add))
                    cx.inc('v', v.tensor_tensor(sb['kb4'][:, 1, :], sb['kbuf'][:, 2, :],
                                                sb['kbuf'][:, 3, :], ALU.add))
                    cx.inc('v', v.tensor_tensor(sb['kb4'][:, 2, :], sb['kbuf'][:, 4, :],
                                                sb['kbuf'][:, 5, :], ALU.add))
                    cx.inc('v', v.tensor_tensor(sb['kb4'][:, 3, :], sb['kbuf'][:, 6, :],
                                                sb['kbuf'][:, 7, :], ALU.add))
                    v.wait_ge(cx.sem('v'), cx.cnt['v'])
                    cx.inc('v', v.tensor_tensor(sb['kb4'][:, 0, :], sb['kb4'][:, 0, :],
                                                sb['kb4'][:, 1, :], ALU.add))
                    cx.inc('v', v.tensor_tensor(sb['kb4'][:, 2, :], sb['kb4'][:, 2, :],
                                                sb['kb4'][:, 3, :], ALU.add))
                    v.wait_ge(cx.sem('v'), cx.cnt['v'])
                    cx.inc('v', v.tensor_tensor(sb['kb4'][:, 1, :], sb['kb4'][:, 0, :],
                                                sb['kb4'][:, 2, :], ALU.add))
                    v.wait_ge(cx.sem('v'), cx.cnt['v'])
                    cx.inc('m_kb_used', v.tensor_tensor(sb['h1t'][:, :], sb['h0t'][:, :],
                                                        sb['kb4'][:, 1, :], ALU.add))
                    cx.inc('m_h1used', v.nop())
                cx.wc = wsnap
                with v.Else():
                    dv = cx.cnt['v'] - snap.get('v', 0)
                    v.tensor_copy(sb['h1t'][:, :], sb['h0t'][:, :]).then_inc(
                        cx.sem('m_kb_used'), 1)
                    v.sem_inc(cx.sem('v'), dv)
                    v.sem_inc(cx.sem('m_h1used'), 1)
            block.vector(_)
            tok_h2 = cx.tok('m_kb_used')

            def _(y, trow=trow):
                cx.wait(y, 'y', tok_h2)
                cx.inc('dh2', y.dma_start(out=dram['hout'][trow:trow + 128, :],
                                          in_=sb['h1t'][:, :]), 16)
            block.sync(_)

    def _(g):
        cx.wait(g, 'g', cx.tok('dh2'))
        cx.inc('moe_prev_done', g.nop())
    block.gpsimd(_)
    return cx.tok('dh2')





def alloc_moe(nc, E, cx, sb, S, NBC, do_moe=True):
    ST = S // 128
    TOKC = NBC * S
    full = (S == 2048)
    if not do_moe:
        pass
    elif full:
        sb['lgb'] = sb['srow'][:, 0:ST * 64].rearrange("p (a e) -> p a e", e=64)
        sb['prb'] = sb['srow'][:, ST * 64:2 * ST * 64].rearrange("p (a e) -> p a e", e=64)
        sb['prc'] = sb['dwide'][:, 0:ST * 64].rearrange("p (a e) -> p a e", e=64)
        sb['msk'] = sb['dwide'][:, ST * 64:2 * ST * 64].rearrange("p (a e) -> p a e", e=64)
        sb['eqm'] = sb['ln_sqs'][:, 0:ST * 64].rearrange("p (a e) -> p a e", e=64)
        sb['Hbb'] = sb['wqk'][:, 0, 0:ST * 64].rearrange("p (a e) -> p a e", e=64)
        sb['dtm'] = sb['pT'][:, :, :].rearrange("p a b -> p (a b)").rearrange(
            "p (c d) -> p c d", d=D)
        sb['dfm'] = sb['prow'][:, 0:2048].rearrange("p (c d) -> p c d", d=256)
        sb['ytm'] = sb['wvt'][:, 0:2, :]
        sb['gsl'] = sb['wvt'][:, 2, 0:INTER]
        sb['gusl'] = sb['wvt'][:, 3:5, 0:INTER]
        sb['gufm'] = sb['wvt'][:, 5:7, :].rearrange("p a b -> p (a b)")[:, 0:1536].rearrange("p (c d) -> p c d", d=256)
        sb['wgux'] = sb['n1fm'][:, :, 0:2 * INTER]
        sb['wdx'] = sb['qfm'][:, 0:6, 0:1024]
        sb['kbuf'] = sb['vtm'][:, 0:8, :]
        sb['kb4'] = sb['vtm'][:, 8:16, :].rearrange("p a b -> p (a b)").bitcast(
            F32).rearrange("p (a b) -> p a b", b=1024)
    else:
        sb['lgb'] = E(nc.sbuf_tensor("x_lgb", [128, ST, 64], F32))
        sb['prb'] = E(nc.sbuf_tensor("x_prb", [128, ST, 64], F32))
        sb['prc'] = E(nc.sbuf_tensor("x_prc", [128, ST, 64], F32))
        sb['msk'] = E(nc.sbuf_tensor("x_msk", [128, ST, 64], F32))
        sb['eqm'] = E(nc.sbuf_tensor("x_eqm", [128, ST, 64], F32))
        sb['Hbb'] = E(nc.sbuf_tensor("x_Hbb", [128, ST, 64], BF16))
        sb['dtm'] = E(nc.sbuf_tensor("x_dtm", [128, 2, D], BF16))
        sb['dfm'] = E(nc.sbuf_tensor("x_dfm", [128, 8, 256], BF16))
        sb['ytm'] = E(nc.sbuf_tensor("x_ytm", [128, 2, D], BF16))
        sb['gsl'] = E(nc.sbuf_tensor("x_gsl", [128, INTER], BF16))
        sb['gusl'] = E(nc.sbuf_tensor("x_gusl", [128, 2, INTER], BF16))
        sb['gufm'] = E(nc.sbuf_tensor("x_gufm", [128, 6, 256], BF16))
        sb['wgux'] = E(nc.sbuf_tensor("x_wgux", [128, DT, 2 * INTER], BF16))
        sb['wdx'] = E(nc.sbuf_tensor("x_wdx", [128, 6, 1024], BF16))
        sb['kbuf'] = E(nc.sbuf_tensor("x_kbuf", [128, 8, D], BF16))
        sb['kb4'] = E(nc.sbuf_tensor("x_kb4", [128, 4, D], F32))
    sb['n2f32'] = E(nc.sbuf_tensor("n2f32", [128, D], F32))
    sb['n2f32b'] = sb['qfm'][:, :, :].rearrange("p a b -> p (a b)").bitcast(
        F32)[:, 0:D]
    sb['moe1'] = E(nc.sbuf_tensor("moe1", [128, 3, ST * 64], F32))
    sb['Hb'] = sb['moe1'][:, 0, :].rearrange("p (a e) -> p a e", e=64)
    sb['whb'] = sb['moe1'][:, 1, :].rearrange("p (a e) -> p a e", e=64)
    sb['Cb'] = sb['moe1'][:, 2, :].rearrange("p (a e) -> p a e", e=64)
    sb['gwcat'] = E(nc.sbuf_tensor("gwcat", [128, DT, 128], BF16))
    sb['gwcat2'] = E(nc.sbuf_tensor("gwcat2", [128, DT, 128], BF16))
    sb['gwf'] = E(nc.sbuf_tensor("gwf", [128, DT, 64], F32))
    sb['gsc'] = E(nc.sbuf_tensor("gsc", [128, 64], F32))
    sb['m16'] = E(nc.sbuf_tensor("m16", [128, ST], F32))
    sb['r16'] = E(nc.sbuf_tensor("r16", [128, ST], F32))
    sb['fmin'] = E(nc.sbuf_tensor("fmin", [1, 1], F32))
    sb['Osb'] = E(nc.sbuf_tensor("Osb", [1, 64], F32))
    sb['Osb2'] = E(nc.sbuf_tensor("Osb2", [1, 64], F32))
    sb['carry'] = E(nc.sbuf_tensor("carry", [1, 64], F32))
    sb['tri'] = E(nc.sbuf_tensor("tri", [128, 128], BF16))
    sb['onescol'] = E(nc.sbuf_tensor("onescol", [128, 1], BF16))
    sb['iotaMB'] = E(nc.sbuf_tensor("iotaMB", [128, 64], F32))
    sb['rolecol'] = E(nc.sbuf_tensor("rolecol", [128, 1], F32))
    sb['rolef1'] = E(nc.sbuf_tensor("rolef1", [1, 1], F32))
    sb['ilocb'] = E(nc.sbuf_tensor("ilocb", [128, ST], F32))
    sb['ekb'] = E(nc.sbuf_tensor("ekb", [128, ST, 8], F32))
    sb['posb'] = E(nc.sbuf_tensor("posb", [128, ST, 8], F32))
    sb['jfb'] = E(nc.sbuf_tensor("jfb", [128, ST, 8], F32))
    sb['jib'] = E(nc.sbuf_tensor("jib", [128, ST, 8], I32))
    sb['pay'] = E(nc.sbuf_tensor("pay", [128, ST, 8, 2], F32))
    sb['flagbuf_i'] = E(nc.sbuf_tensor("flagbuf_i", [1, 32], I32))
    sb['zeros2'] = E(nc.sbuf_tensor("zeros2", [1, 2], F32))
    sb['tsl'] = E(nc.sbuf_tensor("tsl", [128, 2, 2], F32))
    sb['eqz'] = E(nc.sbuf_tensor("eqz", [128, 2], F32))
    sb['idxf'] = E(nc.sbuf_tensor("idxf", [128, 2], F32))
    sb['idxi'] = E(nc.sbuf_tensor("idxi", [128, 2], I32))
    sb['noncebuf'] = E(nc.sbuf_tensor("noncebuf", [1, 4], I32))
    sb['rsb_i'] = E(nc.sbuf_tensor("rsb_i", [1, 1], I32))
    sb['nsb_i'] = E(nc.sbuf_tensor("nsb_i", [1, 1], I32))
    sb['flagrd'] = E(nc.sbuf_tensor("flagrd", [1, 4], I32))
    sb['dummy_sb'] = E(nc.sbuf_tensor("dummy_sb", [1, 4], F32))


def emit_preamble_moe(cx, block, sb, ps, shr, sbn, ins, S, NBC, do_moe=True):
    TOKC = NBC * S
    nc = cx.nc

    def _(y):
        if do_moe:
            cx.inc('dw2', y.dma_start(out=sb['gwf'][:, :, :],
                                      in_=ins['gate_wt'].rearrange("(c p) e -> p c e", p=128)), 16)
        cx.inc('dw2', y.dma_start(out=sb['rolef1'][:, :], in_=ins['rolef'][:, :]), 16)
        cx.inc('dw2', y.dma_start(out=sb['rsb_i'][:, :], in_=ins['role'][:, :]), 16)
        cx.inc('dw2', y.dma_start(out=sb['nsb_i'][:, :], in_=ins['nonce'][:, :]), 16)
    block.sync(_)
    tok_l = cx.tok('dw2')

    def _(g):
        cx.wait(g, 'g', tok_l)
        g.reg_load(sbn['g_rreg'], sb['rsb_i'][0:1, 0:1])
        g.reg_load(sbn['nreg'], sb['nsb_i'][0:1, 0:1])
        g.reg_mov(sbn['pc'], 0)
        g.reg_mov(sbn['bnd_slot'], NE * CAP - 1)
        g.reg_mov(sbn['bnd_tok'], 2 * TOKC - 1)
        cx.inc('g', g.iota(sb['iotaMB'][:, :], [[1, 64]], base=-1000000, channel_multiplier=0,
                           allow_small_or_imprecise_dtypes=True))
        if do_moe:
            for e in range(NEC):
                cx.inc('dwc', g.dma_start(out=cx.dram_wg[e], in_=ins['wg'][e]), 16)
                cx.inc('dwc', g.dma_start(out=cx.dram_wu[e], in_=ins['wu'][e]), 16)
                cx.inc('dwc', g.dma_start(out=cx.dram_wd[e], in_=ins['wd'][e]), 16)
    block.gpsimd(_)
    tok_g = cx.tok('g')

    def _(v):
        cx.wait(v, 'v', tok_l, tok_g)
        cx.inc('v', v.memset(sb['noncebuf'][:, :], 0))
        cx.wait(v, 'v', cx.tok('v'))
        cx.inc('v', v.tensor_copy(sb['noncebuf'][0:1, 0:1], sb['nsb_i'][0:1, 0:1]))
        cx.inc('v', v.memset(sb['zeros2'][:, :], 0.0))
        cx.inc('v', v.memset(sb['onescol'][:, :], 1.0))
        cx.inc('v', v.memset(sb['dummy_sb'][:, :], 0.0))
        cx.inc('v', v.tensor_scalar(sb['tri'][:, :], sb['kkrow'][:, :], sb['qqcol'][:, 0:1],
                                    None, ALU.is_gt))
        if do_moe:
            cx.inc('v', v.tensor_copy(sb['gwcat'][:, :, 0:64], sb['gwf'][:, :, :]))
            cx.wait(v, 'v', cx.tok('v'))
            cx.inc('v', v.tensor_tensor(sb['gwcat'][:, :, 64:128], sb['gwf'][:, :, :],
                                        sb['gwcat'][:, :, 0:64], ALU.subtract))
            cx.wait(v, 'v', cx.tok('v'))
            cx.inc('v', v.tensor_copy(sb['gwcat2'][:, :, 0:64], sb['gwcat'][:, :, 0:64]))
            cx.inc('v', v.memset(sb['gwcat2'][:, :, 64:128], 0.0))
    block.vector(_)
    tok_v = cx.tok('v')

    def _(t):
        cx.wait(t, 't', tok_v)
        cx.inc('pe', t.matmul(ps['B'][:, 0:1], sb['ones128'][:, :], sb['rolef1'][:, :],
                              start=True, stop=True))
    block.tensor(_)
    tok_rb = cx.tok('pe')

    def _(v):
        cx.wait(v, 'v', tok_rb)
        cx.inc('v', v.tensor_scalar(sb['rolecol'][:, :], ps['B'][:, 0:1], float(TOKC), None,
                                    ALU.mult))
    block.vector(_)





def emit_attn_b(cx, block, sb, ps, shr, sbn, dram, ins, S, NBC, E2):
    nc = cx.nc
    ST = S // 128
    BT = ST
    TOKC = NBC * S
    NH2 = NHEAD // 2
    F_X0, F_SC, F_XG, F_AR = 10, 11, 12, 13
    LMT = LM // 128

    def _(y):
        cx.wait(y, 'y', cx.tok('dh2'), cx.tok('dh1'))
        cx.inc('b_lnfree', y.nop())
        cx.inc('dw2', y.dma_start(out=sb['wvb1'][:, :], in_=ins['wv_b'][:, :]), 16)
    block.sync(_)

    def _(g):
        cx.wait(g, 'g', cx.tok('dh2'), cx.tok('dw'), cx.tok('dwb'))
        g.reg_mov(sbn['bnd_S2'], S * S - 1)
        cx.inc('dwb', g.dma_start(out=sb['wvw'][:, :, :],
                                  in_=ins['wv_w'].rearrange("(c p) o -> p c o", p=128)), 16)
        cx.inc('dwb', g.dma_start(out=sb['outwh'][:, :, :],
                                  in_=ins['outw'].rearrange("(h p) o -> p h o", p=64)), 16)
        cx.inc('g', g.iota(sb['idt'][:, :], [[1, 256]], base=0, channel_multiplier=-1,
                           allow_small_or_imprecise_dtypes=True))
        cx.inc('g', g.iota(sb['rtrow'][:, :], [[1, ST]], base=0, channel_multiplier=0,
                           allow_small_or_imprecise_dtypes=True))
    block.gpsimd(_)
    tok_wv = cx.tok('dwb')
    tok_idt = cx.tok('g')

    def _(v):
        cx.wait(v, 'v', tok_idt)
        cx.inc('v', v.tensor_scalar(sb['idI'][:, 0, :], sb['idt'][:, :], 0.0, None,
                                    ALU.is_equal))
        cx.wait(v, 'v', cx.tok('v'))
        cx.inc('v', v.tensor_scalar(sb['idI'][:, 1, :], sb['idt'][:, :], 128.0, None,
                                    ALU.is_equal))
        cx.wait(v, 'v', cx.tok('b_lnfree'))
        cx.inc('v', v.memset(sb['h0t'][:, :], 0.0))
        cx.inc('v', v.memset(sb['onescolF'][:, :], 1.0))
    block.vector(_)
    tok_zb = cx.tok('v')

    def _(y):
        cx.wait(y, 'y', tok_zb)
        for zt in range(S // 128):
            cx.inc('dxz', y.dma_start(out=dram['xgp'][zt * 128:(zt + 1) * 128, :],
                                      in_=sb['h0t'][:, :]), 16)
    block.sync(_)

    def _(t):
        cx.wait(t, 't', cx.tok('dw2'), cx.tok('v'), cx.tok('s'), cx.tok('pe'))
        t.matmul(ps['PV'][:, 0:512], sb['ones128'][0:1, :], sb['wvb1'][0:1, 0:512],
                 start=True, stop=True)
        last = t.matmul(ps['PV'][:, 512:1024], sb['ones128'][0:1, :],
                        sb['wvb1'][0:1, 512:1024], start=True, stop=True)
        cx.inc('pe', last)
    block.tensor(_)
    tok_bb = cx.tok('pe')

    def _(v):
        cx.wait(v, 'v', tok_bb)
        cx.inc('b_pv', v.tensor_copy(sb['wvbb'][:, :], ps['PV'][:, 0:1024]))
    block.vector(_)

    st_used = None
    for b in range(NBC):
        base = b * S
        for tt in range(ST):
            pb = (b * ST + tt) % 2
            srcb = sb['lnin'] if pb == 0 else sb['lnin_b']
            dstb = sb['n2f32'] if pb == 0 else sb['n2f32b']
            kfree = ('b_lnfree', max(0, cx.cnt.get('b_lnfree', 0) - 1))
            dfree = ('dn3f', max(0, cx.cnt.get('dn3f', 0) - 16))

            def _(y, tt=tt, b=b, srcb=srcb, kfree=kfree):
                cx.wait(y, 'y', kfree)
                cx.inc('dln', y.dma_start(out=srcb[:],
                                          in_=dram['hin'][base + tt * 128:base + (tt + 1) * 128, :]), 16)
            block.sync(_)
            tok_ld = cx.tok('dln')

            def _(v, kfree=kfree, dfree=dfree):
                cx.wait(v, 'v', tok_ld, st_used, kfree, dfree)
            block.vector(_)

            def _(s2, kfree=kfree, dfree=dfree):
                cx.wait(s2, 's', tok_ld, st_used, dfree, tok_bb, kfree)
            block.scalar(_)
            tok_ln = emit_ln_tile(cx, block, sb, srcb, dstb, dfree,
                                  scr=('' if pb == 0 else 'b'))

            def _(s2, tt=tt, dstb=dstb):
                cx.wait(s2, 's', tok_ln)
                cx.inc('s', s2.activation(out=sb['vtm'][:, tt, :], in_=dstb[:],
                                          func=AF.Identity))
                cx.inc('b_lnfree', s2.nop())
            block.scalar(_)
            tok_hi = cx.tok('s')

            def _(y, tt=tt, b=b, dstb=dstb):
                cx.wait(y, 'y', tok_hi)
                cx.inc('dn3f', y.dma_start(out=dram['n3f'][base + tt * 128:base + (tt + 1) * 128, :],
                                           in_=dstb[:]), 16)
            block.sync(_)
        tok_stage_s = cx.tok('s')

        def _(y, b=b):
            cx.wait(y, 'y', tok_stage_s)
            cx.inc('dn1', y.dma_start(
                out=dram['n3l'].rearrange("(a p) d -> p a d", p=128)[:, b * BT:(b + 1) * BT, :],
                in_=sb['vtm'][:, :, :]), 16)
        block.sync(_)
        st_used = cx.tok('dn1')

    def _(g):
        cx.wait(g, 'g', cx.tok('dn1'))
        cx.inc('dwb', g.dma_start(out=sb['qkvw2'][:, :, :],
                                  in_=ins['qkv_w'].rearrange("(c p) o -> p c o", p=128)), 16)
        cx.inc('dwb', g.dma_start(out=sb['qkvh'][:, :, :, :],
                                  in_=ins['qkvh'].rearrange("(c p) h w -> p h c w", p=128)), 16)
    block.gpsimd(_)
    tok_qw2 = cx.tok('dwb')

    def _(v):
        cx.wait(v, 'v', st_used)
        for nm in ['px', 'pxT', 'pz', 'pzT', 'pxz', 'pxzT', 'ptA', 'ptB',
                   'a2sb', 'a2bf', 'a3b', 'a1b', 'a1T', 'a3T', 'nlmkb', 'vT', 'a3v']:
            cx.inc('v', v.memset(sb[nm][:, :, :], 0.0))
    block.vector(_)

    def _(g):
        cx.wait(g, 'g', cx.tok('dn1'), cx.tok('dn3f'))
        with g.If_eq(sbn['g_rreg'], 0):
            cx.inc('dsh', g.dma_start(out=shr['x0sh'][:, :], in_=dram['n3l'][0:S, :]), 16)
        with g.Else():
            g.dma_start(out=cx.dummy_dram[0:1, 0:1],
                        in_=cx.dummy_sb[0:1, 0:1]).then_inc(cx.sem('dsh'), 16)
        g.wait_ge(cx.sem('dsh'), cx.cnt['dsh'])
    block.gpsimd(_)
    emit_flag_write(cx, block, F_X0, shr, sbn, only_role=0)
    emit_flag_poll(cx, block, F_X0, shr, sbn, only_role=1, from_role=0)
    if B_STOP == 'b0':
        return cx.tok('dh3')

    def _(v):
        cx.wait(v, 'v', cx.tok('dn1'))
        cx.inc('v', v.memset(sb['scmax'][:, :], 0.0))
    block.vector(_)

    for b in range(NBC):
        base = b * S

        def _(y, b=b):
            cx.wait(y, 'y', cx.tok('dn1'), cx.tok('pe'))
            for dc in range(DT):
                cx.inc('dtp', y.dma_start(out=sb['n1fm'][:, dc, 0:S],
                                          in_=dram['n3l'][base:base + S, dc * 128:(dc + 1) * 128],
                                          transpose=True), 16)
        block.sync(_)
        tok_fm = cx.tok('dtp')

        for tc in range(ST):
            def _(t, tc=tc):
                cx.wait(t, 't', tok_fm, cx.tok('b_pv'), tok_wv)
                last = None
                for di in range(DT):
                    t.ldweights(sb['n1fm'][:, di, tc * 128:(tc + 1) * 128])
                    last = t.matmul(ps['PV'][:, 0:512], sb['n1fm'][:, di, tc * 128:(tc + 1) * 128],
                                    sb['wvw'][:, di, 0:512], start=(di == 0), stop=(di == DT - 1))
                    last = t.matmul(ps['PV'][:, 512:1024], sb['n1fm'][:, di, tc * 128:(tc + 1) * 128],
                                    sb['wvw'][:, di, 512:1024], start=(di == 0), stop=(di == DT - 1))
                cx.inc('pe', last)
            block.tensor(_)
            tok_mm = cx.tok('pe')

            def _(v, tc=tc):
                cx.wait(v, 'v', tok_mm, cx.tok('dval'))
                cx.inc('b_pv', v.tensor_tensor(sb['h1t'][:, :], ps['PV'][:, 0:1024],
                                               sb['wvbb'][:, :], ALU.add))
            block.vector(_)
            tok_ev = cx.tok('b_pv')

            def _(y, tc=tc, b=b):
                cx.wait(y, 'y', tok_ev)
                cx.inc('dval', y.dma_start(
                    out=dram['val'][base + tc * 128:base + (tc + 1) * 128, :],
                    in_=sb['h1t'][:, :]), 16)
            block.sync(_)

        nlm = S // LGRP

        def _(v, b=b):
            cx.wait(v, 'v', tok_fm, cx.tok('b_a2done'), cx.tok('dn3f'))
            for dc in range(DT):
                cx.wait(v, 'v', cx.tok('v'))
                cx.inc('v', v.tensor_reduce(
                    sb['n2f32'][:, 0:nlm],
                    sb['n1fm'][:, dc, 0:S].rearrange("p (l g) -> p l g", g=LGRP),
                    axis=AX.X, op=ALU.add))
                cx.wait(v, 'v', cx.tok('v'))
                cx.inc('v', v.tensor_scalar(sb['xlf'][:, dc, 0:nlm], sb['n2f32'][:, 0:nlm],
                                            1.0 / LGRP, None, ALU.mult))
        block.vector(_)
        tok_xl = cx.tok('v')
        for qc in range(8):
            def _(t, qc=qc):
                cx.wait(t, 't', tok_xl, cx.tok('b_qle'), tok_qw2)
                last = None
                for di in range(DT):
                    t.ldweights(sb['qkvw2'][:, di, qc * 128:(qc + 1) * 128])
                    last = t.matmul(ps['B'][:, 0:nlm], sb['qkvw2'][:, di, qc * 128:(qc + 1) * 128],
                                    sb['xlf'][:, di, 0:nlm], start=(di == 0), stop=(di == DT - 1))
                cx.inc('pe', last)
            block.tensor(_)
            tok_qmm = cx.tok('pe')

            def _(v, qc=qc):
                cx.wait(v, 'v', tok_qmm)
                sc = (DH ** -0.5) if qc < 4 else 1.0
                cx.inc('b_qle', v.tensor_scalar(sb['qlkl'][:, qc, 0:nlm], ps['B'][:, 0:nlm], sc,
                                                None, ALU.mult))
            block.vector(_)
        tok_qlkl = cx.tok('b_qle')

        for h in range(NHEAD):
            hc, ho = divmod(h * DH, 128)
            nl2 = (nlm + 127) // 128
            for lc in range(nl2):
                lw = min(128, nlm - lc * 128)

                def _(t, hc=hc, ho=ho, lc=lc, lw=lw):
                    cx.wait(t, 't', tok_qlkl, cx.tok('b_a2e'))
                    cx.inc('pe', t.matmul(ps['B'][0:lw, 0:nlm],
                                          sb['qlkl'][ho:ho + DH, hc, lc * 128:lc * 128 + lw],
                                          sb['qlkl'][ho:ho + DH, 4 + hc, 0:nlm],
                                          start=True, stop=True))
                block.tensor(_)
                tok_a2 = cx.tok('pe')

                def _(v, lc=lc, lw=lw):
                    cx.wait(v, 'v', tok_a2)
                    cx.inc('b_a2e', v.tensor_copy(sb['a2sb'][0:lw, lc, 0:nlm],
                                                  ps['B'][0:lw, 0:nlm]))
                block.vector(_)
            tok_a2ev = cx.tok('b_a2e')

            def _(v, h=h):
                cx.wait(v, 'v', tok_a2ev)
                nl2 = (nlm + 127) // 128
                cx.inc('v', v.tensor_reduce(sb['m16'][:, 0:nl2], sb['a2sb'][:, 0:nl2, 0:nlm],
                                            axis=AX.X, op=ALU.max))
                cx.wait(v, 'v', cx.tok('v'))
                cx.inc('v', v.tensor_tensor(sb['a2sb'][:, 0:nl2, 0:nlm],
                                            sb['a2sb'][:, 0:nl2, 0:nlm],
                                            sb['m16'][:, 0:nl2].unsqueeze(2).broadcast_to(
                                                [128, nl2, nlm]), ALU.subtract))
            block.vector(_)
            tok_sm = cx.tok('v')

            def _(s2, h=h):
                cx.wait(s2, 's', tok_sm)
                cx.inc('s', s2.activation(out=sb['a2sb'][:, 0:(nlm + 127) // 128, 0:nlm],
                                          in_=sb['a2sb'][:, 0:(nlm + 127) // 128, 0:nlm],
                                          func=AF.Exp))
            block.scalar(_)
            tok_ex = cx.tok('s')

            def _(v, h=h):
                cx.wait(v, 'v', tok_ex)
                nl2 = (nlm + 127) // 128
                cx.inc('v', v.tensor_reduce(sb['m16'][:, 0:nl2], sb['a2sb'][:, 0:nl2, 0:nlm],
                                            axis=AX.X, op=ALU.add))
                cx.wait(v, 'v', cx.tok('v'))
                cx.inc('v', v.reciprocal(sb['r16'][:, 0:nl2], sb['m16'][:, 0:nl2]))
                cx.wait(v, 'v', cx.tok('v'))
                cx.inc('v', v.tensor_tensor(sb['a2bf'][:, 0:nl2, 0:nlm],
                                            sb['a2sb'][:, 0:nl2, 0:nlm],
                                            sb['r16'][:, 0:nl2].unsqueeze(2).broadcast_to(
                                                [128, nl2, nlm]), ALU.mult))
            block.vector(_)
            tok_pr = cx.tok('v')

            def _(t, h=h):
                cx.wait(t, 't', tok_pr, cx.tok('b_csum'))
                nl2 = (nlm + 127) // 128
                last = None
                for lc in range(nl2):
                    lw = min(128, nlm - lc * 128)
                    last = t.matmul(ps['A'][0:1, 0:nlm], sb['onescol'][0:lw, 0:1],
                                    sb['a2bf'][0:lw, lc, 0:nlm], start=(lc == 0),
                                    stop=(lc == nl2 - 1))
                cx.inc('pe', last)
            block.tensor(_)
            tok_cs2 = cx.tok('pe')

            def _(v, h=h):
                cx.wait(v, 'v', tok_cs2)
                cx.inc('v', v.tensor_reduce(sb['sc1'][0:1, 0:1], ps['A'][0:1, 0:nlm],
                                            axis=AX.X, op=ALU.max))
                cx.wait(v, 'v', cx.tok('v'))
                cx.inc('b_csum', v.tensor_max(sb['scmax'][0:1, 0:1], sb['scmax'][0:1, 0:1],
                                              sb['sc1'][0:1, 0:1]))
                cx.inc('b_a2done', v.nop())
            block.vector(_)

    def _(g):
        cx.wait(g, 'g', cx.tok('v'))
        with g.If_eq(sbn['g_rreg'], 0):
            cx.inc('dsh', g.dma_start(out=shr['scsh'][0, :, :], in_=sb['scmax'][0:1, 0:4]), 16)
        with g.Else():
            g.dma_start(out=shr['scsh'][1, :, :], in_=sb['scmax'][0:1, 0:4]).then_inc(
                cx.sem('dsh'), 16)
        g.wait_ge(cx.sem('dsh'), cx.cnt['dsh'])
    block.gpsimd(_)
    emit_flag_write(cx, block, F_SC, shr, sbn)
    emit_flag_poll(cx, block, F_SC, shr, sbn)
    tok_scb = cx.tok('g')

    def _(g):
        cx.wait(g, 'g', tok_scb)
        with g.If_eq(sbn['g_rreg'], 0):
            cx.inc('dto', g.dma_start(out=sb['sc2'][0:1, :], in_=shr['scsh'][1, :, :]), 16)
        with g.Else():
            g.dma_start(out=sb['sc2'][0:1, :], in_=shr['scsh'][0, :, :]).then_inc(
                cx.sem('dto'), 16)
    block.gpsimd(_)
    tok_sc2 = cx.tok('dto')

    def _(v):
        cx.wait(v, 'v', tok_sc2)
        cx.inc('v', v.tensor_max(sb['scmax'][0:1, 0:4], sb['scmax'][0:1, 0:4], sb['sc2'][0:1, 0:4]))
        cx.wait(v, 'v', cx.tok('v'))
        cx.inc('v', v.reciprocal(sb['rscale'][0:1, 0:1], sb['scmax'][0:1, 0:1]))
    block.vector(_)
    tok_rsc0 = cx.tok('v')

    def _(t):
        cx.wait(t, 't', tok_rsc0)
        cx.inc('pe', t.matmul(ps['B'][:, 0:1], sb['ones128'][:, :], sb['rscale'][0:1, 0:1],
                              start=True, stop=True))
    block.tensor(_)
    tok_rsb = cx.tok('pe')

    def _(v):
        cx.wait(v, 'v', tok_rsb)
        cx.inc('v', v.tensor_copy(sb['rscale128'][:, 0:1], ps['B'][:, 0:1]))
    block.vector(_)
    tok_rsc = cx.tok('v')
    if B_STOP == 'b1':
        return cx.tok('dh3')

    def _(y):
        cx.wait(y, 'y', cx.tok('pe'), cx.tok('v'))
        for dc in range(DT):
            cx.inc('dtp', y.dma_start(out=sb['n1fm'][:, dc, 0:S],
                                      in_=shr['x0sh'][0:S, dc * 128:(dc + 1) * 128],
                                      transpose=True), 16)
    block.sync(_)
    tok_x0 = cx.tok('dtp')

    def _(t):
        cx.wait(t, 't', tok_x0, cx.tok('dw2'))
        cx.inc('pe', t.matmul(ps['B'][0:64, 0:NH2 * 33],
                              sb['ones128'][0:1, 0:64], sb['convw1'][0:1, :],
                              start=True, stop=True))
    block.tensor(_)
    tok_cb = cx.tok('pe')

    def _(v):
        cx.wait(v, 'v', tok_cb)
        cx.inc('v', v.tensor_copy(sb['convb'][:, :], ps['B'][0:64, 0:NH2 * 33]))
    block.vector(_)
    tok_cbv = cx.tok('v')

    def _(g):
        cx.wait(g, 'g', cx.tok('dxz'))
    block.gpsimd(_)

    for hh in range(NH2):
        emit_nystrom_head(cx, block, sb, ps, shr, sbn, dram, S, hh, tok_x0, tok_rsc)

    def _(g):
        cx.wait(g, 'g', cx.tok('dxg'))
        with g.If_eq(sbn['g_rreg'], 0):
            cx.inc('dsh', g.dma_start(out=shr['xgsh'][0, :, :], in_=dram['xgp'][:, :]), 16)
        with g.Else():
            g.dma_start(out=shr['xgsh'][1, :, :], in_=dram['xgp'][:, :]).then_inc(
                cx.sem('dsh'), 16)
        g.wait_ge(cx.sem('dsh'), cx.cnt['dsh'])
    block.gpsimd(_)
    emit_flag_write(cx, block, F_XG, shr, sbn)
    emit_flag_poll(cx, block, F_XG, shr, sbn)
    tok_xgb = cx.tok('g')
    if B_STOP == 'b2':
        return cx.tok('dh3')

    def _(y):
        cx.wait(y, 'y', cx.tok('g'))
        cx.inc('dw2', y.dma_start(out=sb['outb1'][:, :], in_=ins['out_b'][:, :]), 16)
    block.sync(_)

    def _(t):
        cx.wait(t, 't', tok_xgb, cx.tok('dw2'), cx.tok('pe'), cx.tok('v'), cx.tok('s'))
        t.matmul(ps['PV'][:, 0:512], sb['ones128'][0:1, :], sb['outb1'][0:1, 0:512],
                 start=True, stop=True)
        last = t.matmul(ps['PV'][:, 512:1024], sb['ones128'][0:1, :],
                        sb['outb1'][0:1, 512:1024], start=True, stop=True)
        cx.inc('pe', last)
    block.tensor(_)
    tok_obm = cx.tok('pe')

    def _(v):
        cx.wait(v, 'v', tok_obm)
        cx.inc('v', v.tensor_copy(sb['outbb'][:, :], ps['PV'][:, 0:1024]))
    block.vector(_)
    tok_ob = cx.tok('v')

    for tt in range(ST):
        def _(y, tt=tt):
            cx.wait(y, 'y', tok_xgb, cx.tok('b_e0'))
            cx.inc('dxa', y.dma_start(out=sb['lnin'][:, :], in_=shr['xgsh'][0, tt * 128:(tt + 1) * 128, :]), 16)
            cx.inc('dxa', y.dma_start(out=sb['h0t'][:, :], in_=shr['xgsh'][1, tt * 128:(tt + 1) * 128, :]), 16)
        block.sync(_)
        tok_xg2 = cx.tok('dxa')

        def _(g, tt=tt):
            cx.wait(g, 'g', tok_xgb, cx.tok('b_e0'), tok_obm)
            with g.If_eq(sbn['g_rreg'], 0):
                cx.inc('dxb', g.dma_start(out=sb['n2f32'][:, :],
                                          in_=dram['n3f'][tt * 128:(tt + 1) * 128, :]), 16)
            with g.Else():
                g.dma_start(out=sb['n2f32'][:, :],
                            in_=shr['x0sh'][tt * 128:(tt + 1) * 128, :]).then_inc(
                    cx.sem('dxb'), 16)
        block.gpsimd(_)
        tok_x0t = cx.tok('dxb')

        def _(v, tt=tt):
            cx.wait(v, 'v', tok_xg2, tok_x0t, tok_ob, cx.tok('dxgs'))
            cx.inc('v', v.tensor_tensor(sb['h1t'][:, :], sb['lnin'][:, :], sb['h0t'][:, :],
                                        ALU.add))
            cx.wait(v, 'v', cx.tok('v'))
            cx.inc('v', v.tensor_tensor(sb['h1t'][:, :], sb['h1t'][:, :], sb['outbb'][:, :],
                                        ALU.add))
            cx.wait(v, 'v', cx.tok('v'))
            cx.inc('v', v.tensor_tensor(sb['h0t'][:, :], sb['h1t'][:, :], sb['n2f32'][:, :],
                                        ALU.add))
            cx.wait(v, 'v', cx.tok('v'))
            cx.inc('v', v.tensor_copy(sb['lnout'][:, :], sb['h0t'][:, :]))
            cx.wait(v, 'v', cx.tok('v'))
            cx.inc('b_e0', v.tensor_tensor(sb['enclo'][:, :], sb['h0t'][:, :],
                                           sb['lnout'][:, :], ALU.subtract))
        block.vector(_)
        tok_e0 = cx.tok('b_e0')

        def _(y, tt=tt):
            cx.wait(y, 'y', tok_e0)
            cx.inc('dxgs', y.dma_start(out=dram['xg0'][tt * 128:(tt + 1) * 128, :],
                                       in_=sb['h1t'][:, :]), 16)
            cx.inc('dxgs', y.dma_start(out=dram['enc0'][tt * 128:(tt + 1) * 128, :],
                                       in_=sb['lnout'][:, :]), 16)
            cx.inc('dxgs', y.dma_start(out=dram['enc0l'][tt * 128:(tt + 1) * 128, :],
                                       in_=sb['enclo'][:, :]), 16)
        block.sync(_)
    tok_enc = cx.tok('dxgs')

    def _(y):
        cx.wait(y, 'y', cx.tok('g'))
        cx.inc('dw2', y.dma_start(out=sb['wqf'][:, :, :],
                                  in_=ins['wq'].rearrange("(c p) o -> p c o", p=128)), 16)
    block.sync(_)
    tok_wqf = cx.tok('dw2')

    def _(v):
        cx.wait(v, 'v', tok_wqf)
        cx.inc('v', v.tensor_copy(sb['wqh'][:, :, :], sb['wqf'][:, :, :]))
        cx.wait(v, 'v', cx.tok('v'))
        cx.inc('v', v.tensor_tensor(sb['wql'][:, :, :], sb['wqf'][:, :, :],
                                    sb['wqh'][:, :, :], ALU.subtract))
    block.vector(_)
    tok_wqs = cx.tok('v')

    def _(y):
        cx.wait(y, 'y', tok_wqs)
        cx.inc('dw2', y.dma_start(out=sb['wqf'][:, :, :],
                                  in_=ins['wk'].rearrange("(c p) o -> p c o", p=128)), 16)
    block.sync(_)
    tok_wkf = cx.tok('dw2')

    def _(v):
        cx.wait(v, 'v', tok_wkf)
        cx.inc('v', v.tensor_copy(sb['wkh'][:, :, :], sb['wqf'][:, :, :]))
        cx.wait(v, 'v', cx.tok('v'))
        cx.inc('v', v.tensor_tensor(sb['wkl'][:, :, :], sb['wqf'][:, :, :],
                                    sb['wkh'][:, :, :], ALU.subtract))
    block.vector(_)
    tok_wsp = cx.tok('v')

    def _(y):
        cx.wait(y, 'y', tok_enc, cx.tok('pe'), cx.tok('v'))
        for dc in range(DT):
            cx.inc('dtp', y.dma_start(out=sb['n1fm'][:, dc, 0:S],
                                      in_=dram['enc0'][0:S, dc * 128:(dc + 1) * 128],
                                      transpose=True), 16)
            cx.inc('dtp', y.dma_start(out=sb['elfm'][:, dc, 0:S],
                                      in_=dram['enc0l'][0:S, dc * 128:(dc + 1) * 128],
                                      transpose=True), 16)
    block.sync(_)
    tok_e0fm = cx.tok('dtp')

    for which in ('q', 'k'):
        wh, wl = ('wqh', 'wql') if which == 'q' else ('wkh', 'wkl')
        dh_, dl_ = ('q0h', 'q0l') if which == 'q' else ('k0h', 'k0l')
        for oc in range(2):
            def _(t, wh=wh, wl=wl, oc=oc):
                cx.wait(t, 't', tok_e0fm, tok_wsp, cx.tok('b_qk'))
                last = None
                chains = [(wh, 'n1fm'), (wl, 'n1fm'), (wh, 'elfm'), (wl, 'elfm')]
                for ci, (wn, en) in enumerate(chains):
                    for di in range(DT):
                        t.ldweights(sb[wn][:, di, oc * 128:(oc + 1) * 128])
                        for mv in range(S // 512):
                            last = t.matmul(ps['A'][:, mv * 512:(mv + 1) * 512],
                                            sb[wn][:, di, oc * 128:(oc + 1) * 128],
                                            sb[en][:, di, mv * 512:(mv + 1) * 512],
                                            start=(ci == 0 and di == 0),
                                            stop=(ci == 3 and di == DT - 1))
                cx.inc('pe', last)
            block.tensor(_)
            tok_qk = cx.tok('pe')

            def _(s2, oc=oc):
                cx.wait(s2, 's', tok_qk)
                cx.inc('s', s2.activation(out=sb['q0f'][:, 0:S], in_=ps['A'][:, 0:S],
                                          func=AF.Identity))
            block.scalar(_)
            tok_qe = cx.tok('s')

            def _(v, dh_=dh_, dl_=dl_, oc=oc):
                cx.wait(v, 'v', tok_qe)
                cx.inc('v', v.tensor_copy(sb[dh_][:, oc, 0:S], sb['q0f'][:, 0:S]))
                cx.wait(v, 'v', cx.tok('v'))
                cx.inc('b_qk', v.tensor_tensor(sb[dl_][:, oc, 0:S], sb['q0f'][:, 0:S],
                                               sb[dh_][:, oc, 0:S], ALU.subtract))
            block.vector(_)
    tok_q0 = cx.tok('b_qk')

    for it in range(ST):
        def _(t, it=it):
            cx.wait(t, 't', tok_q0, cx.tok('b_sev'))
            last = None
            chains = [('q0h', 'k0h'), ('q0h', 'k0l'), ('q0l', 'k0h'), ('q0l', 'k0l')]
            for ci, (qn, kn) in enumerate(chains):
                for oc in range(2):
                    t.ldweights(sb[qn][:, oc, it * 128:(it + 1) * 128])
                    for mv in range(S // 512):
                        last = t.matmul(ps['A'][:, mv * 512:(mv + 1) * 512],
                                        sb[qn][:, oc, it * 128:(it + 1) * 128],
                                        sb[kn][:, oc, mv * 512:(mv + 1) * 512],
                                        start=(ci == 0 and oc == 0),
                                        stop=(ci == 3 and oc == 1))
                cx.inc('pe', last)
        block.tensor(_)
        tok_sm2 = cx.tok('pe')

        def _(s2, it=it):
            cx.wait(s2, 's', tok_sm2)
            cx.inc('b_sev', s2.activation(out=sb['srow2'][:, 0:S], in_=ps['A'][:, 0:S],
                                          func=AF.Identity, scale=1.0 / 16.0))
        block.scalar(_)
        tok_sev = cx.tok('b_sev')

        def _(y, it=it):
            cx.wait(y, 'y', tok_sev)
            cx.inc('dS', y.dma_start(out=dram['Sd'][it * 128:(it + 1) * 128, :],
                                     in_=sb['srow2'][:, 0:S]), 16)
        block.sync(_)
    tok_Sd = cx.tok('dS')
    if B_STOP == 'b3':
        return cx.tok('dh3')

    NCH = E2 // 128
    def _(g):
        cx.wait(g, 'g', tok_Sd)
        cx.inc('dxb', g.dma_start(out=sb['erow'][:, :], in_=dram['er'][:, :]), 16)
        cx.inc('dxb', g.dma_start(out=sb['ecol'][:, :], in_=dram['ec'][:, :]), 16)
        cx.inc('dxb', g.dma_start(out=sb['eval'][:, :], in_=dram['ev'][:, :]), 16)
    block.gpsimd(_)
    tok_eg = cx.tok('dxb')

    def _(v):
        cx.wait(v, 'v', tok_eg)
        cx.inc('v', v.tensor_scalar(sb['elo_i'][:, :], sb['erow'][:, :], 127, None,
                                    ALU.bitwise_and))
        cx.inc('v', v.tensor_scalar(sb['ehi_i'][:, :], sb['erow'][:, :], 7, None,
                                    ALU.arith_shift_right))
        cx.inc('v', v.tensor_scalar(sb['eidx'][:, :], sb['erow'][:, :], S - 1, None,
                                    ALU.min))
        cx.inc('v', v.tensor_scalar(sb['ecol'][:, :], sb['ecol'][:, :], S - 1, None,
                                    ALU.min))
        cx.wait(v, 'v', cx.tok('v'))
        cx.inc('v', v.tensor_scalar(sb['eidx'][:, :], sb['eidx'][:, :], S, None, ALU.mult))
        cx.inc('v', v.tensor_copy(sb['elo'][:, :], sb['elo_i'][:, :]))
        cx.inc('v', v.tensor_copy(sb['ehi'][:, :], sb['ehi_i'][:, :]))
        cx.wait(v, 'v', cx.tok('v'))
        cx.inc('v', v.tensor_tensor(sb['eidx'][:, :], sb['eidx'][:, :], sb['ecol'][:, :],
                                    ALU.add))
    block.vector(_)
    tok_ei = cx.tok('v')

    for ch in range(NCH):
        def _(g, ch=ch):
            cx.wait(g, 'g', tok_ei, cx.tok('b_ctr'))
            cx.inc('dgx', g.indirect_dma_start(
                sb['ctrb'][:, ch % 64:ch % 64 + 1], None,
                dram['Sd'].rearrange("a b -> (a b)").rearrange("(a b) -> a b", b=1),
                IndirectOffsetOnAxis(sb['eidx'][:, ch:ch + 1], 0),
                bounds_check=sbn['bnd_S2'], oob_is_err=False), 16)
            if ch % 16 == 15:
                g.wait_ge(cx.sem('dgx'), cx.cnt['dgx'])
        block.gpsimd(_)
        tok_ct = cx.tok('dgx')

        def _(v, ch=ch):
            cx.wait(v, 'v', tok_ct, cx.tok('b_oh'))
            c6 = ch % 64
            c2 = ch % 2
            cx.inc('v', v.tensor_tensor(sb['cvfb'][:, c6:c6 + 1], sb['ctrb'][:, c6:c6 + 1],
                                        sb['eval'][:, ch:ch + 1], ALU.mult))
            cx.wait(v, 'v', cx.tok('v'))
            cx.inc('v', v.tensor_scalar(sb['ehh'][:, c2 * ST:c2 * ST + ST],
                                        sb['rtrow'][:, 0:ST], sb['ehi'][:, ch:ch + 1],
                                        None, ALU.is_equal))
            cx.inc('b_ctr', v.tensor_scalar(sb['eLb'][:, c2 * 128:c2 * 128 + 128],
                                            sb['kkrow'][:, :], sb['elo'][:, ch:ch + 1],
                                            sb['cvfb'][:, c6:c6 + 1], ALU.is_equal, ALU.mult))
        block.vector(_)
        tok_oh = cx.tok('b_ctr')

        def _(t, ch=ch):
            cx.wait(t, 't', tok_oh)
            c2 = ch % 2
            last = t.matmul(ps['B'][:, 0:ST], sb['eLb'][:, c2 * 128:c2 * 128 + 128],
                            sb['ehh'][:, c2 * ST:c2 * ST + ST],
                            start=(ch == 0), stop=(ch == NCH - 1))
            cx.inc('b_oh', last)
        block.tensor(_)
    tok_ar = cx.tok('b_oh')

    def _(v):
        cx.wait(v, 'v', tok_ar)
        cx.inc('v', v.tensor_copy(sb['alphat'][:, :], ps['B'][:, 0:ST]))
    block.vector(_)
    tok_arp = cx.tok('v')

    def _(g):
        cx.wait(g, 'g', tok_arp)
        with g.If_eq(sbn['g_rreg'], 0):
            cx.inc('dsh', g.dma_start(out=shr['arsh'][0, :, :], in_=sb['alphat'][:, :]), 16)
        with g.Else():
            g.dma_start(out=shr['arsh'][1, :, :], in_=sb['alphat'][:, :]).then_inc(
                cx.sem('dsh'), 16)
        g.wait_ge(cx.sem('dsh'), cx.cnt['dsh'])
    block.gpsimd(_)
    emit_flag_write(cx, block, F_AR, shr, sbn)
    emit_flag_poll(cx, block, F_AR, shr, sbn)
    tok_arb = cx.tok('g')

    def _(g):
        cx.wait(g, 'g', tok_arb)
        with g.If_eq(sbn['g_rreg'], 0):
            cx.inc('dxb', g.dma_start(out=sb['arp2b'][:, :], in_=shr['arsh'][1, :, :]), 16)
        with g.Else():
            g.dma_start(out=sb['arp2b'][:, :], in_=shr['arsh'][0, :, :]).then_inc(
                cx.sem('dxb'), 16)
    block.gpsimd(_)
    tok_ar2 = cx.tok('dxb')
    if B_STOP == 'b3b':
        return cx.tok('dh3')

    def _(v):
        cx.wait(v, 'v', tok_ar2)
        cx.inc('v', v.tensor_tensor(sb['alphat'][:, :], sb['alphat'][:, :],
                                    sb['arp2b'][:, :], ALU.add))
        cx.wait(v, 'v', cx.tok('v'))
        cx.inc('v', v.tensor_reduce(sb['sc128'][:, 0:1], sb['alphat'][:, :], axis=AX.X,
                                    op=ALU.max, negate=True))
        cx.wait(v, 'v', cx.tok('v'))
        cx.inc('v', v.tensor_copy(sb['mbneg'][:, 0:1], sb['sc128'][:, 0:1]))
    block.vector(_)
    tok_am = cx.tok('v')

    def _(t):
        cx.wait(t, 't', tok_am, cx.tok('pe'), cx.tok('b_psT'))
        cx.inc('pe', t.transpose(ps['T'][0:1, :], sb['mbneg'][:, 0:1], sb['ident'][:, :]))
    block.tensor(_)
    tok_amt = cx.tok('pe')

    def _(v):
        cx.wait(v, 'v', tok_amt)
        cx.inc('v', v.tensor_copy(sb['mrow'][0:1, :], ps['T'][0:1, :]))
        cx.wait(v, 'v', cx.tok('v'))
        cx.inc('v', v.tensor_reduce(sb['sc1'][0:1, 0:1], sb['mrow'][0:1, :], axis=AX.X,
                                    op=ALU.min))
    block.vector(_)
    tok_c0 = cx.tok('v')

    def _(t):
        cx.wait(t, 't', tok_c0)
        cx.inc('pe', t.matmul(ps['B'][:, 256:257], sb['ones128'][0:1, :], sb['sc1'][0:1, 0:1],
                              start=True, stop=True))
    block.tensor(_)
    tok_cb2 = cx.tok('pe')

    def _(v):
        cx.wait(v, 'v', tok_cb2)
        cx.inc('v', v.tensor_copy(sb['sc128'][:, 3:4], ps['B'][:, 256:257]))
    block.vector(_)
    tok_cbb = cx.tok('v')

    def _(s2):
        cx.wait(s2, 's', tok_cbb)
        cx.inc('s', s2.activation(out=sb['alphat'][:, :], in_=sb['alphat'][:, :], func=AF.Exp,
                                  bias=sb['sc128'][:, 3:4], accum_out=sb['asum'][:, 0:1]))
    block.scalar(_)
    tok_ae = cx.tok('s')

    def _(t):
        cx.wait(t, 't', tok_ae)
        cx.inc('pe', t.matmul(ps['B'][0:1, 257:258], sb['asum'][:, 0:1],
                              sb['onescolF'][:, 0:1], start=True, stop=True))
    block.tensor(_)
    tok_ts = cx.tok('pe')

    def _(v):
        cx.wait(v, 'v', tok_ts)
        cx.inc('v', v.reciprocal(sb['sc1'][0:1, 1:2], ps['B'][0:1, 257:258]))
    block.vector(_)
    tok_rc = cx.tok('v')

    def _(t):
        cx.wait(t, 't', tok_rc)
        cx.inc('pe', t.matmul(ps['B'][:, 258:259], sb['ones128'][0:1, :], sb['sc1'][0:1, 1:2],
                              start=True, stop=True))
    block.tensor(_)
    tok_rb2 = cx.tok('pe')

    def _(v):
        cx.wait(v, 'v', tok_rb2)
        cx.inc('v', v.tensor_copy(sb['sc128'][:, 2:3], ps['B'][:, 258:259]))
        cx.wait(v, 'v', cx.tok('v'))
        cx.inc('v', v.tensor_scalar(sb['alphat'][:, :], sb['alphat'][:, :],
                                    sb['sc128'][:, 2:3], None, ALU.mult))
    block.vector(_)
    tok_alt = cx.tok('v')
    if B_STOP == 'b4':
        return cx.tok('dh3')

    nbc5 = {'b5a': 1, 'b5b': 4, 'b5c': 8}.get(B_STOP, NBC)
    bufs = [dict(lnin=sb['lnin'], h0t=sb['h0t'], h1t=sb['h1t'], n2=sb['n2f32'],
                 xow=sb['xow']),
            dict(lnin=sb['lnin2'], h0t=sb['h0t2'], h1t=sb['h1t2'], n2=sb['n2f32c'],
                 xow=sb['xow2'])]
    if S == 2048:
        bufs.append(dict(lnin=sb['lnin3'], h0t=sb['h0t3'], h1t=sb['h1t3'],
                         n2=sb['n2f32d'], xow=sb['xow3']))
    NS5 = len(bufs)
    for b in range(nbc5):
        base = b * S
        for tt in range(ST):
            trow = base + tt * 128
            bf = bufs[(b * ST + tt) % NS5]
            kfree = ('dh3', max(0, cx.cnt.get('dh3', 0) - 16 * (NS5 - 1)))
            vfree = ('b_xo', max(0, cx.cnt.get('b_xo', 0) - (NS5 - 1)))

            def _(y, tt=tt, trow=trow, bf=bf, kfree=kfree, vfree=vfree):
                cx.wait(y, 'y', tok_alt, vfree, kfree)
                cx.inc('dxo', y.dma_start(out=bf['lnin'][:, :],
                                          in_=dram['val'][trow:trow + 128, :]), 16)
                cx.inc('dxo', y.dma_start(out=bf['h0t'][:, :],
                                          in_=dram['n3f'][trow:trow + 128, :]), 16)
                cx.inc('dxo', y.dma_start(out=bf['h1t'][:, :],
                                          in_=dram['xg0'][tt * 128:(tt + 1) * 128, :]), 16)
                cx.inc('dxo', y.dma_start(out=bf['n2'][:, :],
                                          in_=dram['hin'][trow:trow + 128, :]), 16)
            block.sync(_)
            tok_lx = cx.tok('dxo')

            def _(v, tt=tt, bf=bf, vfree=vfree):
                cx.wait(v, 'v', tok_lx, vfree)
                cx.inc('v', v.tensor_tensor(bf['h1t'][:, :], bf['h1t'][:, :], bf['h0t'][:, :],
                                            ALU.add))
                cx.inc('v', v.tensor_scalar(bf['lnin'][:, :], bf['lnin'][:, :],
                                            sb['alphat'][:, tt:tt + 1], None, ALU.mult))
            block.vector(_)
            tok_xl2 = cx.tok('v')

            def _(s2, tt=tt, bf=bf):
                cx.wait(s2, 's', tok_xl2)
                cx.inc('s', s2.activation(out=bf['xow'][:, :], in_=bf['lnin'][:, :],
                                          func=AF.Sigmoid, scale=-1.0))
            block.scalar(_)
            tok_sw = cx.tok('s')

            def _(v, tt=tt, trow=trow, bf=bf):
                cx.wait(v, 'v', tok_sw)
                cx.inc('v', v.tensor_tensor(bf['xow'][:, :], bf['xow'][:, :], bf['xow'][:, :],
                                            ALU.mult))
                cx.wait(v, 'v', cx.tok('v'))
                cx.inc('v', v.tensor_tensor(bf['lnin'][:, :], bf['lnin'][:, :], bf['h1t'][:, :],
                                            ALU.subtract))
                cx.wait(v, 'v', cx.tok('v'))
                cx.inc('v', v.tensor_tensor(bf['lnin'][:, :], bf['lnin'][:, :], bf['xow'][:, :],
                                            ALU.mult))
                cx.wait(v, 'v', cx.tok('v'))
                cx.inc('v', v.scalar_tensor_tensor(bf['h1t'][:, :], bf['h1t'][:, :], 2.0,
                                                   bf['n2'][:, :], ALU.mult, ALU.add))
                cx.wait(v, 'v', cx.tok('v'))
                cx.inc('b_xo', v.scalar_tensor_tensor(bf['h1t'][:, :], bf['lnin'][:, :], 2.0,
                                                      bf['h1t'][:, :], ALU.mult, ALU.add))
            block.vector(_)
            tok_h3 = cx.tok('b_xo')

            def _(y, trow=trow, bf=bf):
                cx.wait(y, 'y', tok_h3)
                cx.inc('dh3', y.dma_start(out=dram['hout'][trow:trow + 128, :],
                                          in_=bf['h1t'][:, :]), 16)
            block.sync(_)
    return cx.tok('dh3')


def emit_nystrom_head(cx, block, sb, ps, shr, sbn, dram, S, hh, tok_x0, tok_rsc):
    nc = cx.nc
    ST = S // 128
    nlm = S // LGRP
    nl2 = (nlm + 127) // 128
    NH2 = NHEAD // 2
    for which in range(3):
        def _(t, hh=hh, which=which):
            cx.wait(t, 't', tok_x0, cx.tok('b_nyev'))
            last = None
            for di in range(DT):
                t.ldweights(sb['qkvh'][:, hh, di, which * 64:(which + 1) * 64])
                for mv in range(S // 512):
                    last = t.matmul(ps['A'][0:64, mv * 512:(mv + 1) * 512],
                                    sb['qkvh'][:, hh, di, which * 64:(which + 1) * 64],
                                    sb['n1fm'][:, di, mv * 512:(mv + 1) * 512],
                                    start=(di == 0), stop=(di == DT - 1))
            cx.inc('pe', last)
        block.tensor(_)
        tok_nmm = cx.tok('pe')

        def _(v, which=which):
            cx.wait(v, 'v', tok_nmm)
            sc = (DH ** -0.5) if which == 0 else 1.0
            cx.inc('b_nyev', v.tensor_scalar(sb['nqkvb'][0:64, which, 0:S],
                                             ps['A'][0:64, 0:S], sc, None, ALU.mult))
        block.vector(_)
    tok_nqkv = cx.tok('b_nyev')

    def _(v, hh=hh):
        cx.wait(v, 'v', tok_nqkv)
        for which in range(2):
            cx.inc('v', v.tensor_reduce(
                sb['n2f32'][0:64, which * nlm:(which + 1) * nlm],
                sb['nqkvb'][0:64, which, 0:S].rearrange("p (l g) -> p l g", g=LGRP),
                axis=AX.X, op=ALU.add))
        cx.wait(v, 'v', cx.tok('v'))
        for which in range(2):
            cx.inc('v', v.tensor_scalar(sb['nlmkb'][0:64, which, 0:nlm],
                                        sb['n2f32'][0:64, which * nlm:(which + 1) * nlm],
                                        1.0 / LGRP, None, ALU.mult))
    block.vector(_)
    tok_nlm = cx.tok('v')

    for tc in range(ST):
        def _(t, tc=tc):
            cx.wait(t, 't', tok_nlm, cx.tok('b_a1'))
            cx.inc('pe', t.matmul(ps['B'][:, 0:nlm], sb['nqkvb'][0:64, 0, tc * 128:(tc + 1) * 128],
                                  sb['nlmkb'][0:64, 1, 0:nlm], start=True, stop=True))
        block.tensor(_)
        tok_a1m = cx.tok('pe')
        tok_a1sm = emit_softmax_rows(cx, block, sb, ps['B'], nlm, 'b_a1', tc)
    tok_a1 = cx.tok('b_a1')

    for lc in range(nl2):
        lw = min(128, nlm - lc * 128)
        def _(t, lc=lc, lw=lw):
            cx.wait(t, 't', tok_nlm, cx.tok('b_a3'))
            last = None
            for mv in range(S // 512):
                last = t.matmul(ps['A'][0:lw, mv * 512:(mv + 1) * 512],
                                sb['nlmkb'][0:64, 0, lc * 128:lc * 128 + lw],
                                sb['nqkvb'][0:64, 1, mv * 512:(mv + 1) * 512],
                                start=True, stop=True)
            cx.inc('pe', last)
        block.tensor(_)
        tok_a3m = cx.tok('pe')
        emit_softmax_rows_wide(cx, block, sb, ps['A'], S, 'b_a3', lc, lw)
    tok_a3 = cx.tok('b_a3')

    for lc in range(nl2):
        lw = min(128, nlm - lc * 128)
        def _(t, lc=lc, lw=lw):
            cx.wait(t, 't', tok_nlm, cx.tok('b_a2x'), cx.tok('b_a1'))
            cx.inc('pe', t.matmul(ps['B'][0:lw, 0:nlm],
                                  sb['nlmkb'][0:64, 0, lc * 128:lc * 128 + lw],
                                  sb['nlmkb'][0:64, 1, 0:nlm], start=True, stop=True))
        block.tensor(_)
        emit_softmax_rows2(cx, block, sb, ps['B'], nlm, 'b_a2x', lc, lw)
    tok_a2x = cx.tok('b_a2x')

    emit_pinv(cx, block, sb, ps, S, tok_a2x, tok_rsc)
    tok_z = cx.tok('b_pz')

    for lc in range(nl2):
        for tc2 in range(ST):
            def _(t, lc=lc, tc2=tc2):
                cx.wait(t, 't', tok_a3, cx.tok('b_psT'))
                cx.inc('pe', t.transpose(ps['T'][:, :],
                                         sb['a3b'][0:128, lc, tc2 * 128:(tc2 + 1) * 128],
                                         sb['ident'][:, :]))
            block.tensor(_)
            tok_t1 = cx.tok('pe')

            def _(v, lc=lc, tc2=tc2):
                cx.wait(v, 'v', tok_t1)
                cx.inc('b_psT', v.tensor_copy(sb['a3T'][:, tc2, lc * 128:(lc + 1) * 128],
                                              ps['T'][:, :]))
            block.vector(_)
    for tc2 in range(ST):
        def _(t, tc2=tc2):
            cx.wait(t, 't', tok_nqkv, cx.tok('b_psT'))
            cx.inc('pe', t.transpose(ps['T'][:, 0:64], sb['nqkvb'][0:64, 2, tc2 * 128:(tc2 + 1) * 128],
                                     sb['ident'][0:64, 0:64]))
        block.tensor(_)
        tok_t2 = cx.tok('pe')

        def _(v, tc2=tc2):
            cx.wait(v, 'v', tok_t2)
            cx.inc('b_psT', v.tensor_copy(sb['vT'][:, tc2, 0:64], ps['T'][:, 0:64]))
        block.vector(_)
    tok_vT = cx.tok('b_psT')
    tok_a3T = cx.tok('b_psT')

    for lc in range(nl2):
        lw = min(128, nlm - lc * 128)
        def _(t, lc=lc, lw=lw):
            cx.wait(t, 't', tok_a3T, tok_vT, cx.tok('b_a3v'))
            last = None
            for tc2 in range(ST):
                t.ldweights(sb['a3T'][:, tc2, lc * 128:lc * 128 + lw])
                last = t.matmul(ps['B'][0:lw, 0:64], sb['a3T'][:, tc2, lc * 128:lc * 128 + lw],
                                sb['vT'][:, tc2, 0:64], start=(tc2 == 0), stop=(tc2 == ST - 1))
            cx.inc('pe', last)
        block.tensor(_)
        tok_a3vm = cx.tok('pe')

        def _(v, lc=lc, lw=lw):
            cx.wait(v, 'v', tok_a3vm)
            cx.inc('b_a3v', v.tensor_copy(sb['a3v'][0:lw, lc, 0:64], ps['B'][0:lw, 0:64]))
        block.vector(_)
    tok_a3v = cx.tok('b_a3v')

    for lc in range(nl2):
        lw = min(128, nlm - lc * 128)
        def _(t, lc=lc, lw=lw):
            cx.wait(t, 't', tok_z, tok_a3v, cx.tok('b_t6'))
            last = None
            for kc in range(nl2):
                kw = min(128, nlm - kc * 128)
                t.ldweights(sb['pzT'][0:kw, kc, lc * 128:lc * 128 + lw])
                last = t.matmul(ps['B'][0:lw, 0:64], sb['pzT'][0:kw, kc, lc * 128:lc * 128 + lw],
                                sb['a3v'][0:kw, kc, 0:64], start=(kc == 0), stop=(kc == nl2 - 1))
            cx.inc('pe', last)
        block.tensor(_)
        tok_t6m = cx.tok('pe')

        def _(v, lc=lc, lw=lw):
            cx.wait(v, 'v', tok_t6m)
            cx.inc('b_t6', v.tensor_copy(sb['a3v'][0:lw, nl2 + lc, 0:64], ps['B'][0:lw, 0:64]))
        block.vector(_)
    tok_t6 = cx.tok('b_t6')

    def _(v, hh=hh):
        cx.wait(v, 'v', tok_vT, cx.tok('b_t6'))
        cx.inc('v', v.memset(sb['cacc'][0:64, 0:S], 0.0))
        for tp in range(CONVK):
            off = tp - CONVK // 2
            s0 = max(0, -off)
            s1 = min(S, S - off)
            cx.wait(v, 'v', cx.tok('v'))
            cx.inc('v', v.scalar_tensor_tensor(
                sb['cacc'][0:64, s0:s1],
                sb['nqkvb'][0:64, 2, s0 + off:s1 + off],
                sb['convb'][0:64, hh * CONVK + tp:hh * CONVK + tp + 1],
                sb['cacc'][0:64, s0:s1], ALU.mult, ALU.add))
    block.vector(_)
    tok_cv = cx.tok('v')

    for lc in range(nl2):
        for tc in range(ST):
            def _(t, lc=lc, tc=tc):
                cx.wait(t, 't', tok_a1, cx.tok('b_psT'))
                lw = min(128, nlm - lc * 128)
                cx.inc('pe', t.transpose(ps['T'][0:lw, :], sb['a1b'][:, tc, lc * 128:lc * 128 + lw],
                                         sb['ident'][:, :]))
            block.tensor(_)
            tok_t3 = cx.tok('pe')

            def _(v, lc=lc, tc=tc):
                cx.wait(v, 'v', tok_t3)
                lw = min(128, nlm - lc * 128)
                cx.inc('b_psT', v.tensor_copy(sb['a1T'][0:lw, lc, tc * 128:(tc + 1) * 128],
                                              ps['T'][0:lw, :]))
            block.vector(_)
    tok_a1T = cx.tok('b_psT')

    def _(t):
        cx.wait(t, 't', tok_a1T, tok_t6, cx.tok('b_of'))
        last = None
        for lc in range(nl2):
            lw = min(128, nlm - lc * 128)
            t.ldweights(sb['a3v'][0:lw, nl2 + lc, 0:64])
            for mv in range(S // 512):
                last = t.matmul(ps['A'][0:64, mv * 512:(mv + 1) * 512],
                                sb['a3v'][0:lw, nl2 + lc, 0:64],
                                sb['a1T'][0:lw, lc, mv * 512:(mv + 1) * 512],
                                start=(lc == 0), stop=(lc == nl2 - 1))
        cx.inc('pe', last)
    block.tensor(_)
    tok_of = cx.tok('pe')

    def _(v):
        cx.wait(v, 'v', tok_of, tok_cv)
        cx.inc('b_of', v.tensor_tensor(sb['outh'][0:64, 0:S], ps['A'][0:64, 0:S],
                                       sb['cacc'][0:64, 0:S], ALU.add))
    block.vector(_)
    tok_oh2 = cx.tok('b_of')

    for tc in range(ST):
        def _(t, tc=tc, hh=hh):
            cx.wait(t, 't', tok_oh2, cx.tok('b_xgp'))
            last = None
            t.ldweights(sb['outh'][0:64, tc * 128:(tc + 1) * 128])
            for mv in range(2):
                last = t.matmul(ps['PV'][:, mv * 512:(mv + 1) * 512],
                                sb['outh'][0:64, tc * 128:(tc + 1) * 128],
                                sb['outwh'][0:64, hh, mv * 512:(mv + 1) * 512],
                                start=True, stop=True)
            cx.inc('pe', last)
        block.tensor(_)
        tok_xgm = cx.tok('pe')

        def _(v, tc=tc):
            cx.wait(v, 'v', tok_xgm)
            cx.inc('b_xgp', v.tensor_copy(sb['h1t'][:, :], ps['PV'][:, 0:1024]))
        block.vector(_)
        tok_xge = cx.tok('b_xgp')

        def _(g, tc=tc):
            cx.wait(g, 'g', tok_xge)
            cx.inc('dxg', g.dma_start(out=dram['xgp'][tc * 128:(tc + 1) * 128, :],
                                      in_=sb['h1t'][:, :], accum_op=ALU.add), 16)
            g.wait_ge(cx.sem('dxg'), cx.cnt['dxg'])
            cx.inc('b_xgp2', g.nop())
        block.gpsimd(_)

        def _(v, tc=tc):
            cx.wait(v, 'v', cx.tok('b_xgp2'))
        block.vector(_)


def emit_softmax_rows(cx, block, sb, psap, width, donesem, tc):
    def _(v, tc=tc):
        cx.wait(v, 'v', cx.tok('pe'))
        cx.inc('v', v.reduce_max(sb['sc128'][:, 0:1], psap[:, 0:width], axis=AX.X, negate=True))
    block.vector(_)
    tok_m = cx.tok('v')

    def _(s2, tc=tc):
        cx.wait(s2, 's', tok_m)
        cx.inc('s', s2.activation(out=sb['e128'][:, 0:width], in_=psap[:, 0:width], func=AF.Exp,
                                  bias=sb['sc128'][:, 0:1], accum_out=sb['sc128'][:, 1:2]))
    block.scalar(_)
    tok_e = cx.tok('s')

    def _(v, tc=tc):
        cx.wait(v, 'v', tok_e)
        cx.inc('v', v.reciprocal(sb['sc128'][:, 2:3], sb['sc128'][:, 1:2]))
        cx.wait(v, 'v', cx.tok('v'))
        cx.inc(donesem, v.tensor_scalar(sb['a1b'][:, tc, 0:width], sb['e128'][:, 0:width],
                                        sb['sc128'][:, 2:3], None, ALU.mult))
    block.vector(_)
    return cx.tok(donesem)


def emit_softmax_rows_wide(cx, block, sb, psap, width, donesem, lc, lw):
    def _(v):
        cx.wait(v, 'v', cx.tok('pe'))
        cx.inc('v', v.reduce_max(sb['sc128'][0:lw, 0:1], psap[0:lw, 0:width], axis=AX.X,
                                 negate=True))
    block.vector(_)
    tok_m = cx.tok('v')

    def _(s2):
        cx.wait(s2, 's', tok_m)
        cx.inc('s', s2.activation(out=sb['e128'][0:lw, 0:width], in_=psap[0:lw, 0:width],
                                  func=AF.Exp, bias=sb['sc128'][0:lw, 0:1],
                                  accum_out=sb['sc128'][0:lw, 1:2]))
    block.scalar(_)
    tok_e = cx.tok('s')

    def _(v, lc=lc, lw=lw):
        cx.wait(v, 'v', tok_e)
        cx.inc('v', v.reciprocal(sb['sc128'][0:lw, 2:3], sb['sc128'][0:lw, 1:2]))
        cx.wait(v, 'v', cx.tok('v'))
        cx.inc(donesem, v.tensor_scalar(sb['a3b'][0:lw, lc, 0:width], sb['e128'][0:lw, 0:width],
                                        sb['sc128'][0:lw, 2:3], None, ALU.mult))
    block.vector(_)
    return cx.tok(donesem)


def emit_softmax_rows2(cx, block, sb, psap, width, donesem, lc, lw):
    def _(v):
        cx.wait(v, 'v', cx.tok('pe'))
        cx.inc('v', v.reduce_max(sb['sc128'][0:lw, 0:1], psap[0:lw, 0:width], axis=AX.X,
                                 negate=True))
    block.vector(_)
    tok_m = cx.tok('v')

    def _(s2):
        cx.wait(s2, 's', tok_m)
        cx.inc('s', s2.activation(out=sb['e128'][0:lw, 0:width], in_=psap[0:lw, 0:width],
                                  func=AF.Exp, bias=sb['sc128'][0:lw, 0:1],
                                  accum_out=sb['sc128'][0:lw, 1:2]))
    block.scalar(_)
    tok_e = cx.tok('s')

    def _(v, lc=lc, lw=lw):
        cx.wait(v, 'v', tok_e)
        cx.inc('v', v.reciprocal(sb['sc128'][0:lw, 2:3], sb['sc128'][0:lw, 1:2]))
        cx.wait(v, 'v', cx.tok('v'))
        cx.inc(donesem, v.tensor_scalar(sb['px'][0:lw, lc, 0:width], sb['e128'][0:lw, 0:width],
                                        sb['sc128'][0:lw, 2:3], None, ALU.mult))
    block.vector(_)
    return cx.tok(donesem)


def emit_pinv(cx, block, sb, ps, S, tok_x, tok_rsc):
    nc = cx.nc
    nlm = S // LGRP
    nl2 = (nlm + 127) // 128

    def tposes(src3, dst3, tokdep, warsem):
        for a in range(nl2):
            aw = min(128, nlm - a * 128)
            for bb in range(nl2):
                bw = min(128, nlm - bb * 128)
                def _(t, a=a, bb=bb, aw=aw, bw=bw):
                    cx.wait(t, 't', tokdep, cx.tok('b_psT'))
                    cx.inc('pe', t.transpose(ps['T'][0:bw, 0:aw], sb[src3][0:aw, a, bb * 128:bb * 128 + bw],
                                             sb['ident'][0:aw, 0:aw]))
                block.tensor(_)
                tok_tp = cx.tok('pe')

                def _(v, a=a, bb=bb, aw=aw, bw=bw):
                    cx.wait(v, 'v', tok_tp)
                    cx.inc('b_psT', v.tensor_copy(sb[dst3][0:bw, bb, a * 128:a * 128 + aw],
                                                 ps['T'][0:bw, 0:aw]))
                block.vector(_)
        return cx.tok('b_psT')

    def matmul_sq(lhsT3, rhs3, outps, tokdep, extra=None):
        toks = []
        for ic in range(nl2):
            iw = min(128, nlm - ic * 128)
            def _(t, ic=ic, iw=iw):
                cx.wait(t, 't', tokdep, cx.tok('b_pmm'))
                if extra is not None:
                    cx.wait(t, 't', extra)
                last = None
                for kc in range(nl2):
                    kw = min(128, nlm - kc * 128)
                    t.ldweights(sb[lhsT3][0:kw, kc, ic * 128:ic * 128 + iw])
                    last = t.matmul(outps[0:iw, ic * nlm:ic * nlm + nlm] if False else
                                    ps['A'][0:iw, ic * 512:ic * 512 + nlm],
                                    sb[lhsT3][0:kw, kc, ic * 128:ic * 128 + iw],
                                    sb[rhs3][0:kw, kc, 0:nlm],
                                    start=(kc == 0), stop=(kc == nl2 - 1))
                cx.inc('pe', last)
            block.tensor(_)
            toks.append(cx.tok('pe'))
        return toks

    tok_xT = tposes('px', 'pxT', tok_x, 'b_pxT')

    def _(v):
        cx.wait(v, 'v', tok_xT, tok_rsc)
        for a in range(nl2):
            cx.inc('v', v.tensor_scalar(sb['pz'][:, a, 0:nlm], sb['pxT'][:, a, 0:nlm],
                                        sb['rscale'][0:1, 0:1].partition_broadcast(128)
                                        if False else sb['rscale128'][:, 0:1], None, ALU.mult))
    block.vector(_)
    tok_z0 = cx.tok('v')

    prev_z = tok_z0
    for it in range(6):
        tok_zT = tposes('pz', 'pzT', prev_z, 'b_pzT')
        mm1 = matmul_sq('pxT', 'pz', None, prev_z)
        def _(v, mm1=mm1):
            cx.wait(v, 'v', *mm1)
            for ic in range(nl2):
                iw = min(128, nlm - ic * 128)
                cx.inc('b_pmm', v.tensor_copy(sb['pxz'][0:iw, ic, 0:nlm],
                                              ps['A'][0:iw, ic * 512:ic * 512 + nlm]))
        block.vector(_)
        tok_xz = cx.tok('b_pmm')
        tok_xzT = tposes('pxz', 'pxzT', tok_xz, 'b_pxzT')

        def _(v):
            cx.wait(v, 'v', tok_xz)
            for ic in range(nl2):
                cx.inc('v', v.scalar_tensor_tensor(sb['ptA'][:, ic, 0:nlm], sb['idI'][:, ic, 0:nlm],
                                                   7.0, sb['pxz'][:, ic, 0:nlm],
                                                   ALU.mult, ALU.subtract))
        block.vector(_)
        tok_tA = cx.tok('v')

        mm2 = matmul_sq('pxzT', 'ptA', None, tok_tA, extra=tok_xzT)
        def _(v, mm2=mm2):
            cx.wait(v, 'v', *mm2)
            for ic in range(nl2):
                iw = min(128, nlm - ic * 128)
                cx.inc('b_pmm', v.tensor_copy(sb['ptB'][0:iw, ic, 0:nlm],
                                              ps['A'][0:iw, ic * 512:ic * 512 + nlm]))
        block.vector(_)
        tok_tB = cx.tok('b_pmm')

        def _(v):
            cx.wait(v, 'v', tok_tB)
            for ic in range(nl2):
                cx.inc('v', v.scalar_tensor_tensor(sb['ptA'][:, ic, 0:nlm], sb['idI'][:, ic, 0:nlm],
                                                   15.0, sb['ptB'][:, ic, 0:nlm],
                                                   ALU.mult, ALU.subtract))
        block.vector(_)
        tok_tA2 = cx.tok('v')

        mm3 = matmul_sq('pxzT', 'ptA', None, tok_tA2)
        def _(v, mm3=mm3):
            cx.wait(v, 'v', *mm3)
            for ic in range(nl2):
                iw = min(128, nlm - ic * 128)
                cx.inc('b_pmm', v.tensor_copy(sb['ptB'][0:iw, ic, 0:nlm],
                                              ps['A'][0:iw, ic * 512:ic * 512 + nlm]))
        block.vector(_)
        tok_tB2 = cx.tok('b_pmm')

        def _(v):
            cx.wait(v, 'v', tok_tB2)
            for ic in range(nl2):
                cx.inc('v', v.scalar_tensor_tensor(sb['ptA'][:, ic, 0:nlm], sb['idI'][:, ic, 0:nlm],
                                                   13.0, sb['ptB'][:, ic, 0:nlm],
                                                   ALU.mult, ALU.subtract))
        block.vector(_)
        tok_tA3 = cx.tok('v')

        mm4 = matmul_sq('pzT', 'ptA', None, tok_tA3, extra=tok_zT)
        def _(v, mm4=mm4):
            cx.wait(v, 'v', *mm4)
            for ic in range(nl2):
                iw = min(128, nlm - ic * 128)
                cx.inc('b_pz', v.tensor_scalar(sb['pz'][0:iw, ic, 0:nlm],
                                               ps['A'][0:iw, ic * 512:ic * 512 + nlm], 0.25,
                                               None, ALU.mult))
        block.vector(_)
        prev_z = cx.tok('b_pz')
        def _(v):
            cx.wait(v, 'v', prev_z)
            cx.inc('b_pmm', v.nop())
        block.vector(_)
    tposes('pz', 'pzT', prev_z, 'b_pzT')
    def _(v):
        cx.wait(v, 'v', cx.tok('b_psT'))
        cx.inc('b_pz', v.nop())
    block.vector(_)




def alloc_attn_b(nc, E, cx, sb, S, NBC, E2):
    ST = S // 128
    nlm = S // LGRP
    nl2 = (nlm + 127) // 128
    NH2 = NHEAD // 2
    NCH = E2 // 128
    full = (S == 2048)
    sb['wvw'] = sb['wqk']
    vtm_bf = sb['vtm'][:, :, :].rearrange("p a b -> p (a b)")
    if full:
        sb['qkvw2'] = vtm_bf[:, 0:DT * 1536].rearrange("p (a b) -> p a b", b=1536)
        for i, nm in enumerate(['px', 'pxT', 'pz', 'pzT', 'pxz', 'pxzT', 'ptA', 'ptB']):
            sb[nm] = vtm_bf[:, DT * 1536 + i * 512:DT * 1536 + (i + 1) * 512].rearrange(
                "p (a b) -> p a b", b=256)
        sb['qlkl'] = sb['prow'][:, 0:2048].rearrange("p (a b) -> p a b", b=256)
        sb['xlf'] = sb['pT'][:, :, :].rearrange("p a b -> p (a b)")[:, 0:DT * 256].rearrange(
            "p (a b) -> p a b", b=256)
        sb['wvbb'] = sb['srow'][:, 0:1024]
        sb['outbb'] = sb['srow'][:, 0:1024]
        qkvh_bf = sb['moe1'][:, :, :].rearrange("p a b -> p (a b)").bitcast(BF16)
        sb['qkvh'] = qkvh_bf[:, 0:NH2 * DT * 192].rearrange(
            "p (h c w) -> p h c w", h=NH2, c=DT)
    else:
        sb['qkvw2'] = E(nc.sbuf_tensor("b_qkvw2", [128, DT, 1536], BF16))
        for nm in ['px', 'pxT', 'pz', 'pzT', 'pxz', 'pxzT', 'ptA', 'ptB']:
            sb[nm] = E(nc.sbuf_tensor("b_" + nm, [128, 2, 256], BF16))
        sb['qlkl'] = E(nc.sbuf_tensor("b_qlkl", [128, 8, 256], BF16))
        sb['xlf'] = E(nc.sbuf_tensor("b_xlf", [128, 8, 256], BF16))
        sb['wvbb'] = E(nc.sbuf_tensor("b_wvbb", [128, 1024], F32))
        sb['outbb'] = E(nc.sbuf_tensor("b_outbb", [128, 1024], F32))
        sb['qkvh'] = E(nc.sbuf_tensor("b_qkvh", [128, NH2, DT, 192], BF16))
    sb['outh'] = sb['prow'][0:64, 0:S]
    pT_flat = sb['pT'][:, :, :].rearrange("p a b -> p (a b)")
    sb['vT'] = pT_flat[:, 0:ST * 64].rearrange("p (a b) -> p a b", b=64)
    sb['a3v'] = pT_flat[:, ST * 64:ST * 64 + 4 * 64].rearrange("p (a b) -> p a b", b=64)
    sb['outwh'] = sb['wvt'][0:64, 0:NH2, :]
    wvt_bf = sb['wvt'][:, :, :].rearrange("p a b -> p (a b)")
    sb['wqh'] = wvt_bf[:, 0:2048].rearrange("p (a b) -> p a b", b=256)
    sb['wql'] = wvt_bf[:, 2048:4096].rearrange("p (a b) -> p a b", b=256)
    sb['wkh'] = wvt_bf[:, 4096:6144].rearrange("p (a b) -> p a b", b=256)
    sb['wkl'] = wvt_bf[:, 6144:8192].rearrange("p (a b) -> p a b", b=256)
    sb['wqf'] = sb['dwide'][:, 0:2048].rearrange("p (a b) -> p a b", b=256)
    qfm_flat = sb['qfm'][:, :, :].rearrange("p a b -> p (a b)")
    o1 = ST * 256
    o2 = nl2 * S
    sb['a1b'] = qfm_flat[:, 0:o1].rearrange("p (a b) -> p a b", b=256)
    sb['a1T'] = qfm_flat[:, o1:o1 + o2].rearrange("p (a b) -> p a b", b=S)
    sb['a3b'] = qfm_flat[:, o1 + o2:o1 + 2 * o2].rearrange("p (a b) -> p a b", b=S)
    sb['a3T'] = qfm_flat[:, o1 + 2 * o2:2 * o1 + 2 * o2].rearrange("p (a b) -> p a b", b=256)
    sb['q0h'] = qfm_flat[:, 0:2 * S].rearrange("p (a b) -> p a b", b=S)
    sb['q0l'] = qfm_flat[:, 2 * S:4 * S].rearrange("p (a b) -> p a b", b=S)
    sb['k0h'] = qfm_flat[:, 4 * S:6 * S].rearrange("p (a b) -> p a b", b=S)
    sb['k0l'] = qfm_flat[:, 6 * S:8 * S].rearrange("p (a b) -> p a b", b=S)
    sb['elfm'] = vtm_bf[:, 0:DT * S].rearrange("p (a b) -> p a b", b=S)
    sb['q0f'] = sb['srow'][:, 0:S]
    if full:
        sb['enclo'] = sb['prow'][:, 0:1024]
    else:
        sb['enclo'] = sb['qlkl'][:, :, :].rearrange("p a b -> p (a b)")[:, 0:1024]
    sb['nqkvb'] = vtm_bf[0:64, 0:3 * S].rearrange("p (a b) -> p a b", b=S)
    sb['a2sb'] = sb['ln_sqs'][:, 0:nl2 * nlm].rearrange("p (a b) -> p a b", b=nlm)
    sb['xow'] = sb['ln_sqs'][:, 0:1024]
    sb['a2bf'] = sb['lnout'][:, 0:nl2 * nlm].rearrange("p (a b) -> p a b", b=nlm)
    sb['cacc'] = sb['dwide'][0:64, 0:S]
    sb['e128'] = sb['srow'][:, 0:S]
    sb['srow2'] = sb['srow'][:, 0:S]
    vtm_i32 = vtm_bf.bitcast(I32)
    vtm_f32 = vtm_bf.bitcast(F32)
    sb['lnin2'] = sb['dwide'][:, 0:1024]
    sb['h0t2'] = sb['dwide'][:, 1024:2048]
    sb['xow2'] = vtm_f32[:, 0:1024]
    if full:
        sb['h1t2'] = sb['srow'][:, 0:1024]
        sb['n2f32c'] = sb['srow'][:, 1024:2048]
        qfm_f32 = sb['qfm'][:, :, :].rearrange("p a b -> p (a b)").bitcast(F32)
        sb['lnin3'] = qfm_f32[:, 0:1024]
        sb['h0t3'] = qfm_f32[:, 1024:2048]
        sb['h1t3'] = qfm_f32[:, 2048:3072]
        sb['n2f32d'] = qfm_f32[:, 3072:4096]
        sb['xow3'] = vtm_f32[:, 1024:2048]
    else:
        sb['h1t2'] = E(nc.sbuf_tensor("b_h1t2", [128, D], F32))
        sb['n2f32c'] = E(nc.sbuf_tensor("b_n2f32c", [128, D], F32))
    sb['erow'] = vtm_i32[:, 0:NCH]
    sb['ecol'] = vtm_i32[:, NCH:2 * NCH]
    sb['eidx'] = vtm_i32[:, 2 * NCH:3 * NCH]
    sb['elo_i'] = vtm_i32[:, 3 * NCH:4 * NCH]
    sb['ehi_i'] = vtm_i32[:, 4 * NCH:5 * NCH]
    sb['eval'] = vtm_f32[:, 5 * NCH:6 * NCH]
    sb['elo'] = vtm_f32[:, 6 * NCH:7 * NCH]
    sb['ehi'] = vtm_f32[:, 7 * NCH:8 * NCH]
    sb['nlmkb'] = E(nc.sbuf_tensor("b_nlmkb", [64, 2, 256], BF16))
    sb['idI'] = E(nc.sbuf_tensor("b_idI", [128, 2, 256], BF16))
    sb['idt'] = E(nc.sbuf_tensor("b_idt", [128, 256], F32))
    sb['convb'] = E(nc.sbuf_tensor("b_convb", [64, NH2 * CONVK], F32))
    sb['convw1'] = E(nc.sbuf_tensor("b_convw1", [1, NH2 * CONVK], F32))
    sb['wvb1'] = sb['n2f32'][0:1, :]
    sb['outb1'] = sb['n2f32'][0:1, :]
    sb['ctrb'] = E(nc.sbuf_tensor("b_ctrb", [128, 64], F32))
    sb['cvfb'] = E(nc.sbuf_tensor("b_cvfb", [128, 64], F32))
    sb['eLb'] = E(nc.sbuf_tensor("b_eLb", [128, 256], F32))
    sb['ehh'] = E(nc.sbuf_tensor("b_ehh", [128, 2 * ST], F32))
    sb['rtrow'] = E(nc.sbuf_tensor("b_rtrow", [128, ST], F32))
    sb['alphat'] = E(nc.sbuf_tensor("b_alphat", [128, ST], F32))
    sb['arp2b'] = E(nc.sbuf_tensor("b_arp2b", [128, ST], F32))
    sb['asum'] = E(nc.sbuf_tensor("b_asum", [128, 1], F32))
    sb['mbneg'] = E(nc.sbuf_tensor("b_mbneg", [128, 1], BF16))
    sb['mrow'] = E(nc.sbuf_tensor("b_mrow", [1, 128], F32))
    sb['onescolF'] = E(nc.sbuf_tensor("b_onescolF", [128, 1], F32))
    sb['scmax'] = E(nc.sbuf_tensor("b_scmax", [1, 4], F32))
    sb['sc1'] = E(nc.sbuf_tensor("b_sc1", [1, 4], F32))
    sb['sc2'] = E(nc.sbuf_tensor("b_sc2", [1, 4], F32))
    sb['rscale'] = E(nc.sbuf_tensor("b_rscale", [1, 1], F32))
    sb['rscale128'] = E(nc.sbuf_tensor("b_rscale128", [128, 1], F32))
    sb['sc128'] = E(nc.sbuf_tensor("b_sc128", [128, 4], F32))


def emit_preamble_b(cx, block, sb, sbn, ins, S, NBC):
    def _(y):
        cx.inc('dw2', y.dma_start(out=sb['convw1'][:, :], in_=ins['conv'][:, :]), 16)
    block.sync(_)


def build_program(S, NBC, phases, E2=65536, dbg=False):
    nc = bass.Bass("TRN2", target_bir_lowering=False, detect_race_conditions=False)
    TOKC = NBC * S
    TOKI = 2 * TOKC
    NSLOT = NE * CAP
    do_m1 = "m1" in phases
    do_m2 = "m2" in phases
    do_b = "b" in phases
    do_moe = do_m1 or do_m2
    do_role = do_moe or do_b
    NH2 = NHEAD // 2

    x_in = nc.dram_tensor("x16", [TOKC, D], F32, kind="ExternalInput").ap()
    wqk_in = nc.dram_tensor("w_qk", [D, D], F32, kind="ExternalInput").ap()
    wvt_in = nc.dram_tensor("w_v_t", [D, D], F32, kind="ExternalInput").ap()
    alibi_in = nc.dram_tensor("alibi", [1, 1], F32, kind="ExternalInput").ap()
    ins = dict(w_qk=wqk_in, w_v_t=wvt_in, alibi=alibi_in)
    if do_role:
        ins['role'] = nc.dram_tensor("role", [1, 1], I32, kind="ExternalInput").ap()
        ins['rolef'] = nc.dram_tensor("rolef", [1, 1], F32, kind="ExternalInput").ap()
        ins['nonce'] = nc.dram_tensor("nonce", [1, 1], I32, kind="ExternalInput").ap()
    if do_moe:
        ins['gate_wt'] = nc.dram_tensor("gate_wt", [D, NE], F32, kind="ExternalInput").ap()
        ins['wg'] = nc.dram_tensor("wg_in", [NEC, D, INTER], F32, kind="ExternalInput").ap()
        ins['wu'] = nc.dram_tensor("wu_in", [NEC, D, INTER], F32, kind="ExternalInput").ap()
        ins['wd'] = nc.dram_tensor("wd_in", [NEC, INTER, D], F32, kind="ExternalInput").ap()
    if do_b:
        ins['qkv_w'] = nc.dram_tensor("qkv_w", [D, 3 * NHEAD * DH], F32,
                                      kind="ExternalInput").ap()
        ins['qkvh'] = nc.dram_tensor("qkvh_in", [D, NH2, 192], F32, kind="ExternalInput").ap()
        ins['wq'] = nc.dram_tensor("wq_in", [D, WQK], F32, kind="ExternalInput").ap()
        ins['wk'] = nc.dram_tensor("wk_in", [D, WQK], F32, kind="ExternalInput").ap()
        ins['wv_w'] = nc.dram_tensor("wvw_in", [D, D], F32, kind="ExternalInput").ap()
        ins['outw'] = nc.dram_tensor("outw_in", [NH2 * DH, D], F32, kind="ExternalInput").ap()
        ins['out_b'] = nc.dram_tensor("outb_in", [1, D], F32, kind="ExternalInput").ap()
        ins['wv_b'] = nc.dram_tensor("wvb_in", [1, D], F32, kind="ExternalInput").ap()
        ins['conv'] = nc.dram_tensor("conv_in", [1, NH2 * CONVK], F32,
                                     kind="ExternalInput").ap()

    outname = {"a": "out_h1", "m1": "out_h2", "b": "out_h3", "m2": "out_h4"}[phases[-1]]
    out_t = nc.dram_tensor(outname, [TOKC, D], F32, kind="ExternalOutput").ap()

    h1_dram = nc.dram_tensor("h1d", [TOKC, D], F32, kind="Internal").ap()
    n1_dram = nc.dram_tensor("n1d", [S, D], BF16, kind="Internal").ap()
    dram = dict(h0=x_in, h1=h1_dram, n1scr=n1_dram)
    if do_moe:
        dram['h2'] = nc.dram_tensor("h2d", [TOKC, D], F32, kind="Internal").ap()
        dram['n2l'] = nc.dram_tensor("n2ld", [TOKC, D], BF16, kind="Internal").ap()
        dram['n2lo'] = nc.dram_tensor("n2lod", [TOKC, D], BF16, kind="Internal").ap()
        dram['H'] = nc.dram_tensor("Hd", [TOKC, NE], BF16, kind="Internal").ap()
        dram['Wh'] = nc.dram_tensor("Whd", [TOKC, NE], F32, kind="Internal").ap()
        dram['C'] = nc.dram_tensor("Cd", [TOKC, NE], F32, kind="Internal").ap()
        dram['jd'] = nc.dram_tensor("jdd", [TOKC, TOPK], I32, kind="Internal").ap()
        dram['wg'] = nc.dram_tensor("wg_bf", [NEC, D, INTER], BF16, kind="Internal").ap()
        dram['wu'] = nc.dram_tensor("wu_bf", [NEC, D, INTER], BF16, kind="Internal").ap()
        dram['wd'] = nc.dram_tensor("wd_bf", [NEC, INTER, D], BF16, kind="Internal").ap()
    if do_b:
        dram['h3'] = nc.dram_tensor("h3d", [TOKC, D], F32, kind="Internal").ap()
        dram['n3l'] = nc.dram_tensor("n3ld", [TOKC, D], BF16, kind="Internal").ap()
        dram['n3f'] = nc.dram_tensor("n3fd", [TOKC, D], F32, kind="Internal").ap()
        dram['val'] = nc.dram_tensor("vald", [TOKC, D], F32, kind="Internal").ap()
        dram['Sd'] = nc.dram_tensor("Sdd", [S, S], F32, kind="Internal").ap()
        dram['xg0'] = nc.dram_tensor("xg0d", [S, D], F32, kind="Internal").ap()
        dram['enc0'] = nc.dram_tensor("enc0d", [S, D], BF16, kind="Internal").ap()
        dram['enc0l'] = nc.dram_tensor("enc0ld", [S, D], BF16, kind="Internal").ap()
        dram['xgp'] = nc.dram_tensor("xgpd", [S, D], F32, kind="Internal").ap()
        dram['er'] = nc.dram_tensor("er", [128, E2 // 128], I32, kind="ExternalInput").ap()
        dram['ec'] = nc.dram_tensor("ec", [128, E2 // 128], I32, kind="ExternalInput").ap()
        dram['ev'] = nc.dram_tensor("ev", [128, E2 // 128], F32, kind="ExternalInput").ap()

    shr = {}
    if do_role:
        shr['flags'] = nc.dram_tensor("flagsh", [24, 2, 4], I32, kind="Internal",
                                      addr_space="Shared").ap()
    if do_moe:
        shr['n2sh'] = nc.dram_tensor("n2sh", [TOKI, D], BF16, kind="Internal",
                                     addr_space="Shared").ap()
        shr['Ywsh'] = nc.dram_tensor("Ywsh", [NSLOT, D], BF16, kind="Internal",
                                     addr_space="Shared").ap()
        shr['Tsh'] = nc.dram_tensor("Tsh", [NSLOT, 2], F32, kind="Internal",
                                    addr_space="Shared").ap()
        shr['tot'] = nc.dram_tensor("totsh", [2, 1, NE], F32, kind="Internal",
                                    addr_space="Shared").ap()
    if do_b:
        shr['x0sh'] = nc.dram_tensor("x0sh", [S, D], BF16, kind="Internal",
                                     addr_space="Shared").ap()
        shr['xgsh'] = nc.dram_tensor("xgsh", [2, S, D], F32, kind="Internal",
                                     addr_space="Shared").ap()
        shr['scsh'] = nc.dram_tensor("scsh", [2, 1, 4], F32, kind="Internal",
                                     addr_space="Shared").ap()
        shr['arsh'] = nc.dram_tensor("arsh", [2, 128, S // 128], F32, kind="Internal",
                                     addr_space="Shared").ap()

    with ExitStack() as stack:
        E = stack.enter_context
        cx = CX(nc, stack)
        cx.swdge_sems = {'dsh', 'dfl', 'dto', 'dTs', 'dgx', 'dtl', 'dyw', 'dKg',
                         'dpoll', 'dw', 'dwc', 'dHx', 'dwb'}
        cx.dummy_dram = nc.dram_tensor("dmyd", [1, 4], F32, kind="Internal").ap()
        sb, ps = alloc_common(nc, E, cx, S)
        sbn = {}
        if do_role:
            alloc_moe(nc, E, cx, sb, S, NBC, do_moe=do_moe)
            sbn['g_rreg'] = nc.gpsimd.alloc_register("g_rreg")
            sbn['g_lreg'] = nc.gpsimd.alloc_register("g_lreg")
            sbn['nreg'] = nc.gpsimd.alloc_register("nreg")
            sbn['pr'] = nc.gpsimd.alloc_register("prr")
            sbn['fv'] = nc.gpsimd.alloc_register("fvr")
            sbn['pc'] = nc.gpsimd.alloc_register("pcr")
            sbn['v_lreg'] = nc.vector.alloc_register("v_lreg")
            sbn['bnd_slot'] = nc.gpsimd.alloc_register("bnd_slot")
            sbn['bnd_tok'] = nc.gpsimd.alloc_register("bnd_tok")
            sbn['y_lreg'] = nc.sync.alloc_register("y_lreg")
        if do_moe:
            cx.dram_wg = [dram['wg'][e] for e in range(NEC)]
            cx.dram_wu = [dram['wu'][e] for e in range(NEC)]
            cx.dram_wd = [dram['wd'][e] for e in range(NEC)]
        if do_b:
            alloc_attn_b(nc, E, cx, sb, S, NBC, E2)
            sbn['bnd_S2'] = nc.gpsimd.alloc_register("bnd_S2")
        if do_role and S == 2048:
            moe1f = sb['moe1'][:, :, :].rearrange("p a b -> p (a b)")
            sb['lnin_b'] = moe1f[:, 0:1024]
            sb['ln_sqsb'] = moe1f[:, 1024:2048]
            sb['lnoutb'] = moe1f[:, 2048:2560].bitcast(BF16)
        else:
            sb['lnin_b'] = E(nc.sbuf_tensor("lnin_b", [128, D], F32))
            sb['ln_sqsb'] = E(nc.sbuf_tensor("ln_sqsb", [128, D], F32))
            sb['lnoutb'] = E(nc.sbuf_tensor("lnoutb", [128, D], BF16))
        cx.sbn = sbn
        if do_role:
            cx.dummy_sb = sb['dummy_sb']
        block = E(nc.Block())

        emit_preamble(cx, block, sb, ps, ins)
        if do_role:
            sbn['noncebuf'] = sb['noncebuf']
            sbn['flagrd'] = sb['flagrd']
            emit_preamble_moe(cx, block, sb, ps, shr, sbn, ins, S, NBC, do_moe=do_moe)
        if do_b:
            emit_preamble_b(cx, block, sb, sbn, ins, S, NBC)

        if "a" in phases:
            emit_attn_a(cx, block, sb, ps, dram, S, NBC)
        else:
            def _(y):
                cx.inc('dh1', y.dma_start(out=h1_dram[:, :], in_=x_in[:, :]), 16)
            block.sync(_)
        cur = h1_dram

        if do_m1:
            def _(y):
                cx.wait(y, 'y', cx.tok('dh1'))
            block.sync(_)

            def _(g):
                cx.wait(g, 'g', cx.tok('dh1'), cx.tok('dwc'))
            block.gpsimd(_)
            dram_m1 = dict(dram)
            dram_m1['hin'] = cur
            dram_m1['hout'] = out_t if phases[-1] == "m1" else dram['h2']
            emit_moe(cx, block, sb, ps, shr, sbn, dram_m1, S, NBC, 0)
            cur = dram_m1['hout']

        if do_b:
            dram_b = dict(dram)
            dram_b['hin'] = cur
            dram_b['hout'] = out_t if phases[-1] == "b" else dram['h3']
            emit_attn_b(cx, block, sb, ps, shr, sbn, dram_b, ins, S, NBC, E2)
            cur = dram_b['hout']

        if do_m2:
            def _(y):
                cx.wait(y, 'y', cx.tok('dh3'))
            block.sync(_)

            def _(v):
                cx.wait(v, 'v', cx.tok('dh3'))
                cx.inc('moe_prev_done', v.nop())
            block.vector(_)

            def _(g):
                cx.wait(g, 'g', cx.tok('dh3'))
            block.gpsimd(_)
            dram_m2 = dict(dram)
            dram_m2['hin'] = cur
            dram_m2['hout'] = out_t
            emit_moe(cx, block, sb, ps, shr, sbn, dram_m2, S, NBC, 1)
            cur = out_t

        if dbg and do_b:
            dbgo = {}
            for nm, src in [("dbg_n3f", dram['n3f']), ("dbg_val", dram['val']),
                            ("dbg_Sd", dram['Sd']), ("dbg_xg0", dram['xg0']),
                            ("dbg_xgp", dram['xgp'])]:
                dbgo[nm] = nc.dram_tensor(nm, list(src.shape), F32,
                                          kind="ExternalOutput").ap()
            alph_o = nc.dram_tensor("dbg_alph", [128, S // 128], F32,
                                    kind="ExternalOutput").ap()
            sb_dumps = [("dbg_a1b", sb['a1b'][:, :, :], BF16),
                        ("dbg_a3b", sb['a3b'][:, :, :], BF16),
                        ("dbg_a3T", sb['a3T'][:, :, :], BF16),
                        ("dbg_vT", sb['vT'][:, :, :], BF16),
                        ("dbg_nlmk", sb['nlmkb'][:, :, :], BF16),
                        ("dbg_a1", sb['a1b'][:, :, :], BF16),
                        ("dbg_a3", sb['a3b'][:, :, :], BF16),
                        ("dbg_pz", sb['pz'][:, :, :], BF16),
                        ("dbg_px", sb['px'][:, :, :], BF16),
                        ("dbg_a3v", sb['a3v'][:, :, :], BF16),
                        ("dbg_outh", sb['outh'][:, :], BF16),
                        ("dbg_cacc", sb['cacc'][:, :], F32),
                        ("dbg_rsc", sb['rscale128'][:, :], F32)]
            sb_outs = {}
            for nm, ap, dt in sb_dumps:
                sb_outs[nm] = nc.dram_tensor(nm, list(ap.shape), dt,
                                             kind="ExternalOutput").ap()

            def _(y):
                cx.wait(y, 'y', cx.tok('dh3'), cx.tok('v'), cx.tok('s'))
                for nm, ap, dt in sb_dumps:
                    cx.inc('dout', y.dma_start(out=sb_outs[nm], in_=ap), 16)
                for nm, src in [("dbg_n3f", dram['n3f']), ("dbg_val", dram['val']),
                                ("dbg_Sd", dram['Sd']), ("dbg_xg0", dram['xg0']),
                                ("dbg_xgp", dram['xgp'])]:
                    cx.inc('dout', y.dma_start(out=dbgo[nm], in_=src), 16)
                cx.inc('dout', y.dma_start(out=alph_o, in_=sb['alphat'][:, :]), 16)
                y.wait_ge(cx.sem('dout'), cx.cnt['dout'])
            block.sync(_)

        def _(y):
            cx.wait(y, 'y', cx.tok('dh1'), cx.tok('dh2'), cx.tok('dh3'),
                    cx.tok('v'), cx.tok('s'), cx.tok('pe'), cx.tok('g'))
            if cur is h1_dram:
                nchunk = max(1, TOKC // 4096)
                step = TOKC // nchunk
                for i in range(nchunk):
                    cx.inc('dout', y.dma_start(out=out_t[i * step:(i + 1) * step, :],
                                               in_=cur[i * step:(i + 1) * step, :]), 16)
                y.wait_ge(cx.sem('dout'), cx.cnt['dout'])
        block.sync(_)

    return nc



def _host_shard(inputs):
    x = np.asarray(inputs["L0_attn_a_x"])
    B, S, Dm = x.shape
    NBC = B // 2
    w_qk = np.asarray(inputs["L0_attn_a_W_qk"])
    w_v = np.asarray(inputs["L0_attn_a_W_v"])
    alibi = np.asarray(inputs["L0_attn_a_alibi_slope"]).reshape(1, 1)
    gate_w = np.asarray(inputs["L0_ffn_gate_w"])
    w_gate = np.asarray(inputs["L0_ffn_w_gate"])
    w_up = np.asarray(inputs["L0_ffn_w_up"])
    w_down = np.asarray(inputs["L0_ffn_w_down"])
    qkv_w = np.asarray(inputs["L0_attn_b_qkv_w"])
    wq = np.asarray(inputs["L0_attn_b_wq"])
    wk = np.asarray(inputs["L0_attn_b_wk"])
    wv_w = np.asarray(inputs["L0_attn_b_wv_w"])
    wv_b = np.asarray(inputs["L0_attn_b_wv_b"]).reshape(1, Dm)
    out_w = np.asarray(inputs["L0_attn_b_out_w"])
    out_b = np.asarray(inputs["L0_attn_b_out_b"]).reshape(1, Dm)
    conv_w = np.asarray(inputs["L0_attn_b_conv_w"])
    er = np.ascontiguousarray(np.asarray(inputs["L0_attn_b_adj_rows"]).astype(np.int32))
    ec = np.ascontiguousarray(np.asarray(inputs["L0_attn_b_adj_cols"]).astype(np.int32))
    ev = np.ascontiguousarray(np.asarray(inputs["L0_attn_b_adj_vals"]).astype(np.float32))
    Efull = er.shape[0]
    E2 = Efull // 2

    w_v_t = np.ascontiguousarray(w_v.T)
    gate_wt = np.ascontiguousarray(gate_w.T)
    nonce = np.array([[np.random.randint(1, 2 ** 30)]], np.int32)
    NH2 = NHEAD // 2
    ims = []
    for role in range(2):
        x16 = np.ascontiguousarray(x[role * NBC:(role + 1) * NBC].reshape(-1, Dm))
        elo = role * NEC
        heads = [role * NH2 + hh for hh in range(NH2)]
        qkvh = np.stack([np.concatenate(
            [qkv_w[:, g * DH:(g + 1) * DH],
             qkv_w[:, 512 + g * DH:512 + (g + 1) * DH],
             qkv_w[:, 1024 + g * DH:1024 + (g + 1) * DH]], axis=1)
            for g in heads], axis=1)
        conv = np.ascontiguousarray(
            conv_w[role * NH2:(role + 1) * NH2, 0, :, 0].reshape(1, NH2 * CONVK))
        im = {
            "x16": x16, "w_qk": w_qk, "w_v_t": w_v_t, "alibi": alibi,
            "role": np.array([[role]], np.int32),
            "rolef": np.array([[float(role)]], np.float32),
            "nonce": nonce,
            "gate_wt": gate_wt,
            "wg_in": np.ascontiguousarray(np.swapaxes(w_gate[elo:elo + NEC], 1, 2)),
            "wu_in": np.ascontiguousarray(np.swapaxes(w_up[elo:elo + NEC], 1, 2)),
            "wd_in": np.ascontiguousarray(np.swapaxes(w_down[elo:elo + NEC], 1, 2)),
            "qkv_w": qkv_w,
            "qkvh_in": np.ascontiguousarray(qkvh),
            "wq_in": wq, "wk_in": wk, "wvw_in": wv_w,
            "outw_in": np.ascontiguousarray(out_w[role * NH2 * DH:(role + 1) * NH2 * DH]),
            "outb_in": out_b, "wvb_in": wv_b, "conv_in": conv,
            "er": np.ascontiguousarray(er[role * E2:(role + 1) * E2].reshape(-1, 128).T),
            "ec": np.ascontiguousarray(ec[role * E2:(role + 1) * E2].reshape(-1, 128).T),
            "ev": np.ascontiguousarray(ev[role * E2:(role + 1) * E2].reshape(-1, 128).T),
        }
        ims.append(im)
    return [dict(ims[c % 2]) for c in range(8)], E2


def kernel(**inputs):
    x = np.asarray(inputs["L0_attn_a_x"])
    B, S, Dm = x.shape
    NBC = B // 2
    ims, E2 = _host_shard(inputs)
    nc = build_program(S, NBC, ("a", "m1", "b", "m2"), E2=E2)
    r = run_bass_kernel_spmd(nc, ims, core_ids=list(range(8)))
    out = np.empty((B, S, Dm), np.float32)
    for c in range(8):
        k = c // 2
        role = c % 2
        h4 = r.results[c]["out_h4"].reshape(NBC, S, Dm)
        out[role * NBC + 4 * k: role * NBC + 4 * k + 4] = h4[4 * k:4 * k + 4]
    return out
